# version 16
# baseline (speedup 1.0000x reference)
import numpy as np
import contextlib
import concourse.bass as bass
import concourse.mybir as mybir

F32 = mybir.dt.float32
F16 = mybir.dt.float16
AF = mybir.ActivationFunctionType
ALU = mybir.AluOpType
AX = mybir.AxisListType

MAGIC = 1.5 * 2.0**23
THETA = 10000.0
TWO_PI = 2.0 * np.pi
C1 = 6.28125
C2 = float(np.float32(np.frombuffer(np.uint32(
    np.frombuffer(np.float32(TWO_PI - C1).tobytes(), dtype=np.uint32)[0]
    & 0xFFFFF000).tobytes(), dtype=np.float32)[0]))
C3 = float(TWO_PI - C1 - C2)
INV2PI = float(np.float32(1.0 / TWO_PI))
LNT32 = float(np.log(THETA) / 32.0)
PI = float(np.pi)

D = 1024
H = 16
HD = 64
DC = 8
II = 2816
IT = 22
EPS = 1e-5

c_jcol, c_hi, c_jm, c_pair, c_sgn, c_f = 0, 1, 2, 3, 4, 5
c_bq, c_bk = 6, 14
c_negmu, c_rstd = 22, 24
c_xmax = 26
c_invsx, c_sx = 28, 29
c_hmax = 30
c_invs2, c_s2 = 32, 33
c_cg, c_cu = 34, 38
c_rsd = 42
c_rmax8 = 56
c_bias = 48
c_sc = 64
c_eps = 79
c_st6 = 80
c_mv = 92
c_tmp = 96
c_mq2 = 104

SEMS = ["dS", "dSa", "dSb", "dG", "dGa", "dGb", "dOa", "dOb",
        "cPE", "cDVE", "cACT", "cPL", "cc1", "cc2",
        "sST", "sEXP", "sAV", "sBC", "sDR", "sDN", "sNM",
        "sG", "sU", "sSIL", "sHP", "sWQ", "sWT", "sQE", "sT1", "sPERM",
        "sROPE", "sSQ", "sRD", "dWD", "sSP", "sAOE", "sSC"]


class Sems:
    def __init__(self, nc, st):
        self.h, self.n = {}, {}
        for name in SEMS:
            self.h[name] = st.enter_context(nc.semaphore(name))
            self.n[name] = 0

    def inc(self, instr, s, k=1):
        instr.then_inc(self.h[s], k)
        self.n[s] += k
        return self.n[s]

    def wait(self, eng, s, n=None):
        v = self.n[s] if n is None else n
        if v > 0:
            eng.wait_ge(self.h[s], v)


def build(S=2048, n_cores=8, taps=()):
    NT = S // 128
    SLAB = min(512, S)
    NSLAB = S // SLAB
    QG = min(1024, S)
    NQG = S // QG
    FQ = min(512, S)
    NFQ = S // FQ
    KTOT = DC * NSLAB

    nc = bass.Bass("TRN2", target_bir_lowering=False, debug=False)

    x_h = nc.dram_tensor("x", [S, D], F32, kind="ExternalInput").ap()
    wq_h = nc.dram_tensor("Wq", [D, D], F32, kind="ExternalInput").ap()
    bq_h = nc.dram_tensor("bq", [D], F32, kind="ExternalInput").ap()
    wk_h = nc.dram_tensor("Wk", [D, D], F32, kind="ExternalInput").ap()
    bk_h = nc.dram_tensor("bk", [D], F32, kind="ExternalInput").ap()
    wv_h = nc.dram_tensor("Wv", [D, D], F32, kind="ExternalInput").ap()
    bv_h = nc.dram_tensor("bv", [D], F32, kind="ExternalInput").ap()
    wo_h = nc.dram_tensor("Wo", [D, D], F32, kind="ExternalInput").ap()
    clipx_h = nc.dram_tensor("clip_x", [1], F32, kind="ExternalInput").ap()
    clipd_h = nc.dram_tensor("clip_down", [1], F32, kind="ExternalInput").ap()
    wg_h = nc.dram_tensor("w_gate", [II, D], F32, kind="ExternalInput").ap()
    wu_h = nc.dram_tensor("w_up", [II, D], F32, kind="ExternalInput").ap()
    wd_h = nc.dram_tensor("w_down", [D, II], F32, kind="ExternalInput").ap()
    out_h = nc.dram_tensor("out", [S, D], F32, kind="ExternalOutput").ap()

    tap_h = {}
    def mktap(name, shape, dt):
        if name in taps:
            tap_h[name] = nc.dram_tensor("tap_" + name, list(shape), dt,
                                         kind="ExternalOutput").ap()

    mktap("xn", [128, DC * S], F16)
    mktap("q", [128, DC * S], F16)
    mktap("k", [128, DC * S], F16)
    mktap("v", [128, NT * 1040], F16)
    mktap("xq", [128, DC * S], F16)
    mktap("attn", [128, DC * S], F16)
    mktap("hp", [128, IT * S], F16)
    mktap("ao", [128, DC * S], F16)
    mktap("gmax1", [1, 1], F32)
    mktap("gmax2", [1, 1], F32)

    scr = nc.dram_tensor("scr", [8, 128], F32)
    cc1_in = nc.dram_tensor("cc1_in", [1, 1], F32)
    cc1_out = nc.dram_tensor("cc1_out", [1, 1], F32)
    cc2_in = nc.dram_tensor("cc2_in", [1, 1], F32)
    cc2_out = nc.dram_tensor("cc2_out", [1, 1], F32)
    ao_d = nc.dram_tensor("ao_d", [S, D], F16)

    A_COLS = max(DC * S, 16 * 1024)

    st = contextlib.ExitStack()
    with st:
        A1 = st.enter_context(nc.sbuf_tensor("A1", [128, A_COLS], F16))
        A4 = st.enter_context(nc.sbuf_tensor("A4", [128, A_COLS], F16))
        A5 = st.enter_context(nc.sbuf_tensor("A5", [128, max(NT * 1040, 6 * S + 4096)], F16))
        WT = st.enter_context(nc.sbuf_tensor("WTa", [128, 8192], F16))
        WS = st.enter_context(nc.sbuf_tensor("WSa", [128, 2048], F16))
        X32 = st.enter_context(nc.sbuf_tensor("X32", [128, 2816], F32))
        EX = st.enter_context(nc.sbuf_tensor("EXa", [128, max(5120, 2 * S)], F16))
        COLS = st.enter_context(nc.sbuf_tensor("COLS", [128, 128], F32))
        CN = st.enter_context(nc.sbuf_tensor("CN", [128, 384], F16))
        ROW = st.enter_context(nc.sbuf_tensor("ROWa", [1, 128], F32))
        SC16 = st.enter_context(nc.sbuf_tensor("SC16", [1, 32], F16))
        BVR = st.enter_context(nc.sbuf_tensor("BVR", [1, D], F16))
        PQ = st.enter_context(nc.sbuf_tensor("PQ", [128, 512], F32))
        PS = st.enter_context(nc.psum_tensor("PS", [128, 4096], F32))
        sm = Sems(nc, st)

        IDE = CN[:, 0:128]
        PERM = CN[:, 128:256]
        ONES = CN[:, 256:384]

        def col(c):
            return COLS[:, c:c + 1]

        def scal(c):
            return COLS[0:1, c_sc + c:c_sc + c + 1]

        def cchunks(W):
            return [(c0, min(512, W - c0)) for c0 in range(0, W, 512)]

        def dve_chain(v, ops, last_sem=None):
            for j, fn in enumerate(ops):
                i = fn(v)
                if last_sem is not None and j == len(ops) - 1:
                    sm.inc(i, last_sem)
                else:
                    sm.inc(i, "cDVE")
                    sm.wait(v, "cDVE")

        def act_chain(sc, ops, last_sem=None):
            for j, fn in enumerate(ops):
                i = fn(sc)
                if last_sem is not None and j == len(ops) - 1:
                    sm.inc(i, last_sem)
                else:
                    sm.inc(i, "cACT")
                    sm.wait(sc, "cACT")

        def pl_chain(g, ops):
            for fn in ops:
                i = fn(g)
                sm.inc(i, "cPL")
                sm.wait(g, "cPL")

        def hp_ap(it, t0, w):
            if it < 8:
                return A2[:, it * S + t0:it * S + t0 + w]
            if it < 16:
                return A3[:, (it - 8) * S + t0:(it - 8) * S + t0 + w]
            return A5[:, (it - 16) * S + t0:(it - 16) * S + t0 + w]

        def emit_taps(blk, items):
            live = [(n, ap) for n, ap in items if n in tap_h]
            if not live:
                return
            @blk.gpsimd
            def _(g):
                sm.wait(g, "cDVE")
                sm.wait(g, "cACT")
                for n, ap in live:
                    sm.wait(g, "dG")
                    i = g.dma_start(out=tap_h[n][:, :] if tap_h[n].shape[0] > 1
                                    else tap_h[n][0:1, :], in_=ap)
                    sm.inc(i, "dG", 16)
                sm.wait(g, "dG")

        p0 = contextlib.ExitStack()
        A2 = A3 = None
        with p0:
            F32T = p0.enter_context(nc.sbuf_tensor("F32T", [128, 2 * S], F32))
            SA = X32[:, 0:S]
            SB = F32T[:, 0:S]
            SCc = F32T[:, S:2 * S]

            with nc.Block() as blk:
                @blk.sync
                def _(sy):
                    i = sy.dma_start(out=COLS[:, c_bq:c_bq + 8],
                                     in_=bq_h.rearrange("(a p) -> p a", p=128),
                                     allow_slow_non_contiguous=True)
                    sm.inc(i, "dS", 16)
                    i = sy.dma_start(out=COLS[:, c_bk:c_bk + 8],
                                     in_=bk_h.rearrange("(a p) -> p a", p=128),
                                     allow_slow_non_contiguous=True)
                    sm.inc(i, "dS", 16)
                    i = sy.dma_start(out=COLS[0:1, c_sc + 0:c_sc + 1],
                                     in_=clipx_h.rearrange("(a b) -> a b", a=1))
                    sm.inc(i, "dS", 16)
                    i = sy.dma_start(out=COLS[0:1, c_sc + 1:c_sc + 2],
                                     in_=clipd_h.rearrange("(a b) -> a b", a=1))
                    sm.inc(i, "dS", 16)

                @blk.gpsimd
                def _(g):
                    g.memset(col(c_eps), EPS)
                    g.iota(col(c_jcol), [[0, 1]], channel_multiplier=1,
                           allow_small_or_imprecise_dtypes=True)
                    i = g.iota(SA, [[1, S]], channel_multiplier=0,
                               allow_small_or_imprecise_dtypes=True)
                    sm.inc(i, "cPL")
                    g.memset(CN[:, 0:256], 0.0)
                    i = g.memset(ONES, 1.0)
                    sm.inc(i, "cPL")
                    sm.wait(g, "cPL")
                    pl_chain(g, [
                        lambda g: g.affine_select(out=IDE, in_=IDE,
                                                  compare_op=ALU.not_equal,
                                                  fill=1.0, base=0,
                                                  pattern=[[-1, 128]],
                                                  channel_multiplier=1),
                        lambda g: g.memset(WS[:, 0:256], 0.0),
                        lambda g: g.tensor_copy(WS[:, 1:128], IDE[:, 0:127]),
                        lambda g: g.tensor_copy(WS[:, 128:255], IDE[:, 1:128]),
                        lambda g: g.memset(
                            A5[:, 0:NT * 1040].rearrange("p (t h c) -> p t h c",
                                                         t=NT, h=H)[:, :, :, 64:65],
                            1.0),
                    ])

                @blk.vector
                def _(v):
                    sm.wait(v, "cPL", 1)
                    dve_chain(v, [
                        lambda v: v.tensor_scalar(col(c_hi), col(c_jcol), 1.0 / 64.0,
                                                  -0.5 + 0.25 / 64.0, ALU.mult, ALU.add),
                        lambda v: v.tensor_scalar(col(c_hi), col(c_hi), MAGIC, MAGIC,
                                                  ALU.add, ALU.subtract),
                        lambda v: v.scalar_tensor_tensor(col(c_jm), col(c_hi), -64.0,
                                                         col(c_jcol), ALU.mult, ALU.add),
                        lambda v: v.tensor_scalar(col(c_pair), col(c_jm), 0.5, -0.25,
                                                  ALU.mult, ALU.add),
                        lambda v: v.tensor_scalar(col(c_pair), col(c_pair), MAGIC, MAGIC,
                                                  ALU.add, ALU.subtract),
                        lambda v: v.scalar_tensor_tensor(col(c_sgn), col(c_pair), 2.0,
                                                         col(c_jm), ALU.mult, ALU.subtract),
                        lambda v: v.tensor_scalar(col(c_sgn), col(c_sgn), 2.0, 1.0,
                                                  ALU.mult, ALU.add),
                    ])

                @blk.scalar
                def _(sc):
                    sm.wait(sc, "cDVE")
                    i = sc.activation(col(c_f), col(c_pair), AF.Exp, scale=-LNT32)
                    sm.inc(i, "cACT")

                @blk.vector
                def _(v):
                    sm.wait(v, "cACT")
                    dve_chain(v, [
                        lambda v: v.tensor_scalar(SB, SA, col(c_f), None, ALU.mult),
                        lambda v: v.tensor_scalar(SA, SB, INV2PI, None, ALU.mult),
                        lambda v: v.tensor_scalar(SCc, SA, MAGIC, MAGIC, ALU.add, ALU.subtract),
                        lambda v: v.scalar_tensor_tensor(SA, SCc, -C1, SB, ALU.mult, ALU.add),
                        lambda v: v.scalar_tensor_tensor(SB, SCc, -C2, SA, ALU.mult, ALU.add),
                        lambda v: v.scalar_tensor_tensor(SCc, SCc, -C3, SB, ALU.mult, ALU.add)
                        if False else
                        v.scalar_tensor_tensor(SB, SCc, -C3, SB, ALU.mult, ALU.add)
                        if False else
                        v.scalar_tensor_tensor(SA, SCc, -C3, SB, ALU.mult, ALU.add),
                        lambda v: v.tensor_scalar(SA, SA, PI, -PI, ALU.min, ALU.max)
                        if False else
                        v.tensor_scalar(SB, SA, PI, -PI, ALU.min, ALU.max),
                        lambda v: v.tensor_scalar(SA, SB, 1.0, None, ALU.mult),
                        lambda v: v.tensor_scalar(SB, SA, col(c_sgn), None, ALU.mult),
                    ])

                @blk.scalar
                def _(sc):
                    sm.wait(sc, "cDVE")
                    i = sc.activation(EX[:, 0:S], SB, AF.Sin)
                    sm.inc(i, "cACT")

                @blk.vector
                def _(v):
                    sm.wait(v, "cACT")
                    dve_chain(v, [
                        lambda v: v.tensor_scalar(SB, SA, PI / 2.0, None, ALU.add),
                        lambda v: v.tensor_scalar(SCc, SB, PI, None, ALU.is_gt),
                        lambda v: v.scalar_tensor_tensor(SA, SCc, -TWO_PI, SB,
                                                         ALU.mult, ALU.add),
                        lambda v: v.tensor_scalar(SCc, SA, PI, -PI, ALU.min, ALU.max),
                        lambda v: v.tensor_scalar(SA, SCc, 1.0, None, ALU.mult),
                    ])

                @blk.scalar
                def _(sc):
                    sm.wait(sc, "cDVE")
                    i = sc.activation(EX[:, S:2 * S], SA, AF.Sin)
                    sm.inc(i, "cACT")

                @blk.vector
                def _(v):
                    sm.wait(v, "cACT")
                    sm.wait(v, "cPL", 6)
                    dve_chain(v, [
                        lambda v: v.tensor_scalar(col(c_tmp), col(c_sgn), 0.5, 0.5,
                                                  ALU.mult, ALU.add),
                        lambda v: v.tensor_tensor(WS[:, 256:384], WS[:, 0:128],
                                                  WS[:, 128:256], ALU.subtract),
                        lambda v: v.tensor_scalar(WS[:, 256:384], WS[:, 256:384],
                                                  col(c_tmp), None, ALU.mult),
                        lambda v: v.tensor_tensor(PERM, WS[:, 256:384], WS[:, 128:256],
                                                  ALU.add),
                        lambda v: v.memset(COLS[:, c_xmax:c_xmax + 2], 0.0),
                    ])

                act_pre = sm.n["cACT"]
                dve_pre = sm.n["cDVE"]
                VPT, APT = 7, 2

                @blk.sync
                def _(sy):
                    for tt in range(NT):
                        dsem = "dSa" if tt % 2 == 0 else "dSb"
                        if tt < 2:
                            sm.wait(sy, "cDVE", dve_pre)
                            sm.wait(sy, "cACT", act_pre)
                        else:
                            sm.wait(sy, "cDVE", dve_pre + (tt - 2) * VPT + 5)
                        sm.wait(sy, dsem)
                        i = sy.dma_start(out=X32[:, (tt % 2) * 1024:(tt % 2) * 1024 + 1024],
                                         in_=x_h[tt * 128:(tt + 1) * 128, :])
                        sm.inc(i, dsem, 16)

                @blk.vector
                def _(v):
                    for tt in range(NT):
                        dsem = "dSa" if tt % 2 == 0 else "dSb"
                        xs = X32[:, (tt % 2) * 1024:(tt % 2) * 1024 + 1024]
                        ws = WS[:, (tt % 2) * 1024:(tt % 2) * 1024 + 1024]
                        sm.wait(v, dsem, (tt // 2 + 1) * 16)
                        dve_chain(v, [
                            lambda v: v.bn_stats(COLS[:, c_st6:c_st6 + 6], xs[:, 0:512]),
                            lambda v: v.bn_stats(COLS[:, c_st6 + 6:c_st6 + 12],
                                                 xs[:, 512:1024]),
                            lambda v: v.bn_aggr(COLS[:, c_mv:c_mv + 2],
                                                COLS[:, c_st6:c_st6 + 12]),
                            lambda v: v.tensor_scalar(col(c_negmu + tt % 2),
                                                      col(c_mv), -1.0, None, ALU.mult),
                        ])
                        sm.wait(v, "cACT", act_pre + tt * APT + 2)
                        if tt >= 2:
                            sm.wait(v, "sG", tt - 1)
                        dve_chain(v, [
                            lambda v: v.tensor_scalar(ws, xs, col(c_negmu + tt % 2),
                                                      col(c_rstd + tt % 2),
                                                      ALU.add, ALU.mult),
                            lambda v: v.tensor_reduce(col(c_tmp), ws, AX.X, ALU.max,
                                                      apply_absolute_value=True),
                            lambda v: v.tensor_tensor(col(c_xmax + (tt + 1) % 2),
                                                      col(c_xmax + tt % 2),
                                                      col(c_tmp), ALU.max),
                        ])

                @blk.scalar
                def _(sc):
                    for tt in range(NT):
                        sm.wait(sc, "cDVE", dve_pre + tt * VPT + 4)
                        act_chain(sc, [
                            lambda sc: sc.activation(col(c_tmp + 1), col(c_mv + 1),
                                                     AF.Ln, bias=col(c_eps)),
                            lambda sc: sc.activation(col(c_rstd + tt % 2),
                                                     col(c_tmp + 1), AF.Exp, scale=-0.5),
                        ])
                        sm.wait(sc, "sG", tt + 1)
                        bank = (tt % 2) * 2048
                        i = sc.copy(
                            A1[:, 0:DC * S].rearrange("p (c s) -> p c s", c=DC)[:, :, tt * 128:(tt + 1) * 128],
                            PS[:, bank:bank + 1024].rearrange("p (c b) -> p c b", c=DC))
                        sm.inc(i, "sWT")

                @blk.tensor
                def _(t):
                    sm.wait(t, "cPL", 3)
                    for tt in range(NT):
                        ws = WS[:, (tt % 2) * 1024:(tt % 2) * 1024 + 1024]
                        sm.wait(t, "cDVE", dve_pre + tt * VPT + 5)
                        if tt >= 2:
                            sm.wait(t, "sWT", tt - 1)
                        bank = (tt % 2) * 2048
                        for cd in range(DC):
                            i = t.matmul(PS[:, bank + cd * 128:bank + (cd + 1) * 128],
                                         ws[:, cd * 128:(cd + 1) * 128], IDE,
                                         start=True, stop=True)
                        sm.inc(i, "sG")

                @blk.gpsimd
                def _(g):
                    sm.wait(g, "cDVE", dve_pre + NT * VPT)
                    i = g.dma_start(out=scr.ap()[0:1, :].rearrange("a n -> n a"),
                                    in_=col(c_xmax + NT % 2))
                    sm.inc(i, "dG", 16)
                    sm.wait(g, "dG")
                    i = g.dma_start(out=ROW[0:1, :], in_=scr.ap()[0:1, :])
                    sm.inc(i, "dG", 16)

                @blk.vector
                def _(v):
                    sm.wait(v, "dG")
                    i = v.tensor_reduce(scal(2), ROW[0:1, :], AX.X, ALU.max)
                    sm.inc(i, "cDVE")

                @blk.gpsimd
                def _(g):
                    sm.wait(g, "cDVE")
                    i = g.dma_start(out=cc1_in.ap()[0:1, :], in_=scal(2))
                    sm.inc(i, "dG", 16)
                    sm.wait(g, "dG")
                    i = g.collective_compute(
                        "AllReduce", ALU.max,
                        replica_groups=[list(range(n_cores))],
                        ins=[cc1_in.ap().opt()], outs=[cc1_out.ap().opt()])
                    sm.inc(i, "cc1")

        A2 = st.enter_context(nc.sbuf_tensor("A2", [128, max(DC * S, 8 * S)], F16))
        A3 = st.enter_context(nc.sbuf_tensor("A3", [128, max(DC * S, 8 * S)], F16))

        def load_weight_T(blk, w_dram, wt_free_sem=None, wt_free_val=0):
            pe0 = sm.n["sU"]
            ev0 = sm.n["sWT"]

            @blk.gpsimd
            def _(g):
                for cj in range(8):
                    wsem = "dGa" if cj % 2 == 0 else "dGb"
                    sm.wait(g, "sU", pe0 + max(0, (cj - 1) * 2))
                    sm.wait(g, wsem)
                    i = g.dma_start(out=WS[:, (cj % 2) * 1024:(cj % 2) * 1024 + 1024],
                                    in_=w_dram[cj * 128:(cj + 1) * 128, :])
                    sm.inc(i, wsem, 16)

            dg_a = sm.n["dGa"] - 4 * 16
            dg_b = sm.n["dGb"] - 4 * 16

            @blk.tensor
            def _(t):
                for cj in range(8):
                    wsem = "dGa" if cj % 2 == 0 else "dGb"
                    base = (dg_a if cj % 2 == 0 else dg_b) + (cj // 2 + 1) * 16
                    sm.wait(t, wsem, base)
                    ws = WS[:, (cj % 2) * 1024:(cj % 2) * 1024 + 1024]
                    for b in range(2):
                        nb = 2 * cj + b
                        if nb >= 2:
                            sm.wait(t, "sWT", ev0 + nb - 1)
                        pbank = 3072 + (nb % 2) * 512
                        for kk in range(4):
                            cd = b * 4 + kk
                            i = t.matmul(PS[:, pbank + kk * 128:pbank + (kk + 1) * 128],
                                         ws[:, cd * 128:(cd + 1) * 128], IDE,
                                         start=True, stop=True)
                        sm.inc(i, "sU")

            @blk.scalar
            def _(sc):
                if wt_free_sem is not None:
                    sm.wait(sc, wt_free_sem, wt_free_val)
                for cj in range(8):
                    for b in range(2):
                        nb = 2 * cj + b
                        sm.wait(sc, "sU", pe0 + nb + 1)
                        pbank = 3072 + (nb % 2) * 512
                        i = sc.copy(
                            WT[:, 0:8192].rearrange("p (c b) -> p c b", c=DC)
                            [:, b * 4:(b + 1) * 4, cj * 128:(cj + 1) * 128],
                            PS[:, pbank:pbank + 512].rearrange("p (c b) -> p c b", c=4))
                        sm.inc(i, "sWT")

        with nc.Block() as blk:
            def qk_proj(dst, bias_c):
                NG = 8 * NQG
                sST0 = sm.n["sST"]
                sQE0 = sm.n["sQE"]
                sT10 = sm.n["sT1"]
                sPERM0 = sm.n["sPERM"]
                sROPE0 = sm.n["sROPE"]
                wt_ready = sm.n["sWT"]

                @blk.tensor
                def _(t):
                    sm.wait(t, "sWT", wt_ready)
                    for g in range(NG + 1):
                        if g < NG:
                            cj, qg = divmod(g, NQG)
                            gb = (g % 2) * 1024
                            if g >= 2:
                                sm.wait(t, "sQE", sQE0 + g - 1)
                            for cd in range(DC):
                                for c0, w in cchunks(QG):
                                    i = t.matmul(
                                        PS[:, gb + c0:gb + c0 + w],
                                        WT[:, cd * 1024 + cj * 128:cd * 1024 + (cj + 1) * 128],
                                        A1[:, cd * S + qg * QG + c0:cd * S + qg * QG + c0 + w],
                                        start=(cd == 0), stop=(cd == DC - 1))
                            sm.inc(i, "sST")
                        if g >= 1:
                            gp = g - 1
                            sm.wait(t, "sT1", sT10 + gp + 1)
                            if gp >= 1:
                                sm.wait(t, "sROPE", sROPE0 + gp)
                            rs = A4[:, (gp % 2) * 1024:(gp % 2) * 1024 + QG]
                            for c0, w in cchunks(QG):
                                i = t.matmul(PS[:, 2048 + c0:2048 + c0 + w],
                                             PERM, rs[:, c0:c0 + w],
                                             start=True, stop=True)
                            sm.inc(i, "sPERM")

                sSQ0 = sm.n["sSQ"]

                @blk.scalar
                def _(sc):
                    sm.wait(sc, "dS", 64)
                    for g in range(NG):
                        cj = g // NQG
                        gb = (g % 2) * 1024
                        sm.wait(sc, "sST", sST0 + g + 1)
                        sm.wait(sc, "sT1", sT10 + max(0, g - 1))
                        sm.wait(sc, "sSQ", sSQ0 + max(0, g - 1))
                        i = sc.activation(X32[:, gb:gb + QG], PS[:, gb:gb + QG],
                                          AF.Identity, bias=col(bias_c + cj))
                        sm.inc(i, "sQE")

                @blk.vector
                def _(v):
                    for g in range(NG):
                        cj, qg = divmod(g, NQG)
                        t0 = qg * QG
                        gb = (g % 2) * 1024
                        xr = X32[:, gb:gb + QG]
                        rs = A4[:, gb:gb + QG]
                        sm.wait(v, "sQE", sQE0 + g + 1)
                        if g >= 2:
                            sm.wait(v, "sPERM", sPERM0 + g - 1)
                        i = v.tensor_tensor(rs, xr, EX[:, t0:t0 + QG], ALU.mult)
                        sm.inc(i, "sT1")
                        sm.wait(v, "sPERM", sPERM0 + g + 1)
                        i = v.tensor_tensor(rs, xr, EX[:, S + t0:S + t0 + QG], ALU.mult)
                        sm.inc(i, "sSQ")
                        sm.wait(v, "sSQ")
                        i = v.tensor_tensor(dst[:, cj * S + t0:cj * S + t0 + QG],
                                            rs, PS[:, 2048:2048 + QG], ALU.add)
                        sm.inc(i, "sROPE")

            load_weight_T(blk, wq_h)
            qk_proj(A2, c_bq)
            load_weight_T(blk, wk_h, wt_free_sem="sST", wt_free_val=sm.n["sST"])
            qk_proj(A3, c_bk)
            load_weight_T(blk, wv_h, wt_free_sem="sST", wt_free_val=sm.n["sST"])

            @blk.gpsimd
            def _(g):
                sm.wait(g, "dGa")
                i = g.dma_start(out=BVR[0:1, :], in_=bv_h.rearrange("(a b) -> a b", a=1))
                sm.inc(i, "dGa", 16)

            dg_bv = sm.n["dGa"]
            rope_done = sm.n["sROPE"]
            wtv_ready = sm.n["sWT"]
            sG0 = sm.n["sG"]
            sU0v = sm.n["sHP"]

            @blk.tensor
            def _(t):
                sm.wait(t, "dGa", dg_bv)
                sm.wait(t, "sROPE", rope_done)
                sm.wait(t, "sWT", wtv_ready)
                for c0, w in cchunks(D):
                    i = t.matmul(PS[:, 2048 + c0:2048 + c0 + w], ONES[0:1, 0:128],
                                 BVR[0:1, c0:c0 + w], start=True, stop=True)
                sm.inc(i, "sG")
                for tt in range(NT):
                    gb = (tt % 2) * 1024
                    if tt >= 2:
                        sm.wait(t, "sHP", sU0v + tt - 1)
                    for cd in range(DC):
                        for c0, w in cchunks(D):
                            i = t.matmul(
                                PS[:, gb + c0:gb + c0 + w],
                                A1[:, cd * S + tt * 128:cd * S + (tt + 1) * 128],
                                WT[:, cd * 1024 + c0:cd * 1024 + c0 + w],
                                start=(cd == 0), stop=(cd == DC - 1))
                    sm.inc(i, "sG")

            @blk.vector
            def _(v):
                sm.wait(v, "sG", sG0 + 1)
                i = v.tensor_copy(WS[:, 0:1024], PS[:, 2048:3072])
                sm.inc(i, "cDVE")
                sm.wait(v, "cDVE")
                for tt in range(NT):
                    gb = (tt % 2) * 1024
                    sm.wait(v, "sG", sG0 + tt + 2)
                    i = v.tensor_tensor(
                        A5[:, tt * 1040:(tt + 1) * 1040].rearrange(
                            "p (h c) -> p h c", h=H)[:, :, 0:64],
                        PS[:, gb:gb + 1024].rearrange("p (h c) -> p h c", h=H),
                        WS[:, 0:1024].rearrange("p (h c) -> p h c", h=H),
                        ALU.add)
                    sm.inc(i, "sHP")

            @blk.gpsimd
            def _(g):
                sm.wait(g, "cc1")
                i = g.dma_start(out=scal(3), in_=cc1_out.ap()[0:1, :])
                sm.inc(i, "dG", 16)

            @blk.scalar
            def _(sc):
                sm.wait(sc, "dS", 64)
                i = sc.activation(scal(4), scal(0), AF.Sigmoid)
                sm.inc(i, "cACT")

            @blk.vector
            def _(v):
                sm.wait(v, "cACT")
                sm.wait(v, "dG")
                dve_chain(v, [
                    lambda v: v.tensor_tensor(scal(5), scal(4), scal(3), ALU.mult),
                    lambda v: v.reciprocal(scal(6), scal(5)),
                    lambda v: v.tensor_scalar(SC16[0:1, 0:1], scal(6), 127.0, None,
                                              ALU.mult),
                    lambda v: v.tensor_scalar(SC16[0:1, 1:2], scal(5), 1.0 / 127.0, None,
                                              ALU.mult),
                ])

            v_done = sm.n["sHP"]
            dve_sc = sm.n["cDVE"]

            @blk.tensor
            def _(t):
                sm.wait(t, "cDVE", dve_sc)
                sm.wait(t, "sHP", v_done)
                i = t.matmul(PS[:, 2048:2049], ONES[0:1, 0:128], SC16[0:1, 0:1],
                             start=True, stop=True)
                sm.inc(i, "cPE")
                i = t.matmul(PS[:, 2049:2050], ONES[0:1, 0:128], SC16[0:1, 1:2],
                             start=True, stop=True)
                sm.inc(i, "cPE")

            @blk.vector
            def _(v):
                sm.wait(v, "cPE")
                dve_chain(v, [
                    lambda v: v.tensor_copy(col(c_invsx), PS[:, 2048:2049]),
                    lambda v: v.tensor_copy(col(c_sx), PS[:, 2049:2050]),
                ])
                for cd in range(DC):
                    dve_chain(v, [
                        lambda v: v.tensor_scalar(X32[:, 0:S], A1[:, cd * S:(cd + 1) * S],
                                                  col(c_invsx), MAGIC, ALU.mult, ALU.add),
                        lambda v: v.tensor_scalar(WS[:, 0:min(S, 2048)], X32[:, 0:S],
                                                  MAGIC, 127.0, ALU.subtract, ALU.min),
                        lambda v: v.tensor_scalar(A4[:, cd * S:(cd + 1) * S],
                                                  WS[:, 0:min(S, 2048)],
                                                  -128.0, None, ALU.max),
                    ])

        with nc.Block() as blk:
            emit_taps(blk, [("xn", A1[:, 0:DC * S]), ("q", A2[:, 0:DC * S]),
                            ("k", A3[:, 0:DC * S]), ("v", A5[:, 0:NT * 1040]),
                            ("xq", A4[:, 0:DC * S]), ("gmax1", scal(3))])

        with nc.Block() as blk:
            sSQ0 = sm.n["sSQ"]
            sU0 = sm.n["sU"]
            sRD0 = sm.n["sRD"]

            @blk.scalar
            def _(sc):
                for n in range(2 * DC):
                    p, which = divmod(n, 2)
                    if n >= 2:
                        sm.wait(sc, "sU", sU0 + 2 * (n - 1))
                    src = A2 if which == 0 else A3
                    i = sc.activation(EX[:, which * S:which * S + S],
                                      src[:, p * S:(p + 1) * S], AF.Square)
                    sm.inc(i, "sSQ")

            def c0_region(idx):
                rb = 64 * (idx % 2)
                cb = 2048 * ((idx // 2) % 2)
                return rb, cb

            @blk.tensor
            def _(t):
                for n in range(2 * DC):
                    p, which = divmod(n, 2)
                    sm.wait(t, "sSQ", sSQ0 + n + 1)
                    for half in range(2):
                        idx = 2 * n + half
                        if idx >= 4:
                            sm.wait(t, "sRD", sRD0 + idx - 3)
                        rb, cb = c0_region(idx)
                        for c0, w in cchunks(S):
                            i = t.matmul(
                                PS[rb:rb + 64, cb + c0:cb + c0 + w],
                                ONES[64 * half:64 * half + 64, 0:64],
                                EX[64 * half:64 * half + 64,
                                   which * S + c0:which * S + c0 + w],
                                start=True, stop=True)
                        sm.inc(i, "sU")

            @blk.vector
            def _(v):
                for n in range(2 * DC):
                    p, which = divmod(n, 2)
                    for half in range(2):
                        idx = 2 * n + half
                        sm.wait(v, "sU", sU0 + idx + 1)
                        h = 2 * p + half
                        dstc = (80 if which == 0 else 96) + h
                        rb, cb = c0_region(idx)
                        i = v.tensor_reduce(COLS[0:1, dstc:dstc + 1],
                                            PS[rb:rb + 1, cb:cb + S], AX.X, ALU.max)
                        sm.inc(i, "sRD")

            @blk.vector
            def _(v):
                sm.wait(v, "sRD", sRD0 + 4 * DC)
                i = v.tensor_tensor(COLS[0:1, 112:128], COLS[0:1, 80:96],
                                    COLS[0:1, 96:112], ALU.mult)
                sm.inc(i, "cDVE")

            dve_c0 = sm.n["cDVE"]

            @blk.scalar
            def _(sc):
                sm.wait(sc, "cDVE", dve_c0)
                act_chain(sc, [
                    lambda sc: sc.activation(COLS[0:1, 80:96], COLS[0:1, 112:128],
                                             AF.Ln),
                    lambda sc: sc.activation(COLS[0:1, 96:112], COLS[0:1, 80:96],
                                             AF.Exp, scale=0.5),
                ])

            @blk.vector
            def _(v):
                sm.wait(v, "cACT")
                i = v.tensor_scalar(SC16[0:1, 2:18], COLS[0:1, 96:112],
                                    -1.0 / 32.0, None, ALU.mult)
                sm.inc(i, "cDVE")

            dve_c0b = sm.n["cDVE"]

            @blk.tensor
            def _(t):
                sm.wait(t, "cDVE", dve_c0b)
                i = t.matmul(PS[:, 2048:2064], ONES[0:1, 0:128], SC16[0:1, 2:18],
                             start=True, stop=True)
                sm.inc(i, "cPE")

            pe_c0 = sm.n["cPE"]

            @blk.vector
            def _(v):
                sm.wait(v, "cPE", pe_c0)
                dve_chain(v, [
                    lambda v: v.tensor_copy(COLS[:, c_bias:c_bias + 16], PS[:, 2048:2064]),
                    lambda v: v.tensor_tensor(
                        COLS[:, 104:112],
                        COLS[:, c_bias:c_bias + 16].rearrange("p (a b) -> p a b", b=2)[:, :, 0],
                        COLS[:, c_bias:c_bias + 16].rearrange("p (a b) -> p a b", b=2)[:, :, 1],
                        ALU.min),
                ])

            bias_dve = sm.n["cDVE"]

            sST0 = sm.n["sST"]
            sEXP0 = sm.n["sEXP"]
            sAV0 = sm.n["sAV"]
            sBC0 = sm.n["sBC"]
            sDR0 = sm.n["sDR"]
            sDN0 = sm.n["sDN"]
            sNM0 = sm.n["sNM"]
            dG0 = sm.n["dG"]

            def st_bank(half, c):
                return (2 * half + c % 2) * 512

            def av_bank(half, k):
                return 2048 + (2 * (k % 2) + half) * 512

            def bc_bank(half):
                return half * 1024

            @blk.tensor
            def _(t):
                sm.wait(t, "cDVE", bias_dve)
                def emit_st(t, k, p, s_, c):
                    sm.wait(t, "sEXP", sEXP0 + 2 * max(0, NT * k + c - 1))
                    for half in range(2):
                        i = t.matmul(
                            PS[:, st_bank(half, c):st_bank(half, c) + SLAB],
                            A3[64 * half:64 * half + 64,
                               p * S + c * 128:p * S + (c + 1) * 128],
                            A2[64 * half:64 * half + 64,
                               p * S + s_ * SLAB:p * S + (s_ + 1) * SLAB],
                            start=True, stop=True)
                    sm.inc(i, "sST")

                def emit_av(t, k, p, cp):
                    sm.wait(t, "sEXP", sEXP0 + 2 * (NT * k + cp + 1))
                    if cp == 0:
                        sm.wait(t, "sNM", sNM0 + max(0, 2 * k - 2))
                    for half in range(2):
                        hh = 2 * p + half
                        exb = ((cp % 2) * 2 + half) * SLAB
                        i = t.matmul(
                            PS[0:65, av_bank(half, k):av_bank(half, k) + SLAB],
                            A5[:, cp * 1040 + hh * 65:cp * 1040 + (hh + 1) * 65],
                            EX[:, exb:exb + SLAB],
                            start=(cp == 0), stop=(cp == NT - 1))
                    sm.inc(i, "sAV")

                def emit_bc(t, j, h):
                    sm.wait(t, "sDR", sDR0 + 2 * j + h + 1)
                    if j >= 2:
                        sm.wait(t, "sDN", sDN0 + 4 * (j - 2) + (1 if h == 0 else 3))
                    i = t.matmul(
                        PS[64:128, av_bank(h, j):av_bank(h, j) + SLAB],
                        ONES[64:65, 0:64],
                        WS[64:65, 1024 + h * 512:1024 + h * 512 + SLAB],
                        start=True, stop=True)
                    sm.inc(i, "sBC")

                deep = NT >= 8
                for k in range(KTOT):
                    p, s_ = divmod(k, NSLAB)
                    for c in range(NT):
                        emit_st(t, k, p, s_, c)
                        if c >= 2:
                            emit_av(t, k, p, c - 2)
                        if deep and k >= 1:
                            if c == 2:
                                emit_bc(t, k - 1, 0)
                            elif c == 3:
                                emit_bc(t, k - 1, 1)
                    for cp in (NT - 2, NT - 1):
                        if cp < 0:
                            continue
                        emit_av(t, k, p, cp)
                    if not deep and k >= 1:
                        emit_bc(t, k - 1, 0)
                        emit_bc(t, k - 1, 1)
                emit_bc(t, KTOT - 1, 0)
                emit_bc(t, KTOT - 1, 1)

            st_total = sm.n["sST"]

            @blk.scalar
            def _(sc):
                def emit_ln(sc, j, h):
                    sm.wait(sc, "sBC", sBC0 + 2 * j + h + 1)
                    xslab = X32[0:64, h * 1024:h * 1024 + SLAB]
                    i = sc.activation(xslab,
                                      PS[64:128, av_bank(h, j):av_bank(h, j) + SLAB],
                                      AF.Ln)
                    sm.inc(i, "sDN")

                def emit_rep(sc, j, h):
                    sm.wait(sc, "sDN")
                    if h == 1:
                        sm.wait(sc, "sNM", sNM0 + 2 * j + 1)
                    xslab = X32[0:64, h * 1024:h * 1024 + SLAB]
                    i = sc.activation(WS[0:64, 0:SLAB], xslab, AF.Exp, scale=-1.0)
                    sm.inc(i, "sDN")

                deep = NT >= 8
                for k in range(KTOT):
                    for c in range(NT):
                        sm.wait(sc, "sST", sST0 + NT * k + c + 1)
                        if c >= 2:
                            sm.wait(sc, "sAV", sAV0 + NT * k + c - 1)
                        pp = k // NSLAB
                        i = sc.activation(
                            EX[:, (c % 2) * 2 * SLAB:(c % 2) * 2 * SLAB + 2 * SLAB]
                            .rearrange("p (a r) -> p a r", a=2),
                            PS[:, 0:2048].rearrange("p (a q) -> p a q", a=2)
                            [:, :, (c % 2) * 512:(c % 2) * 512 + SLAB],
                            AF.Exp, bias=col(104 + pp),
                            scale=1.0 / 32.0)
                        sm.inc(i, "sEXP", 2)
                        if deep and k >= 1:
                            if c == 3:
                                emit_ln(sc, k - 1, 0)
                            elif c == 4:
                                emit_rep(sc, k - 1, 0)
                            elif c == 6:
                                emit_ln(sc, k - 1, 1)
                            elif c == 7:
                                emit_rep(sc, k - 1, 1)
                    if not deep and k >= 1:
                        for h in range(2):
                            emit_ln(sc, k - 1, h)
                            emit_rep(sc, k - 1, h)
                for h in range(2):
                    emit_ln(sc, KTOT - 1, h)
                    emit_rep(sc, KTOT - 1, h)

            @blk.vector
            def _(v):
                def emit_tt(v, j):
                    pj, sj = divmod(j, NSLAB)
                    sm.wait(v, "sDN", sDN0 + 4 * j + 2)
                    i = v.tensor_tensor(
                        A1[0:64, pj * S + sj * SLAB:pj * S + (sj + 1) * SLAB],
                        PS[0:64, av_bank(0, j):av_bank(0, j) + SLAB],
                        WS[0:64, 0:SLAB], ALU.mult)
                    sm.inc(i, "sNM")
                    sm.wait(v, "sDN", sDN0 + 4 * j + 4)
                    if j >= 1:
                        sm.wait(v, "dG", dG0 + 16 * j)
                    i = v.tensor_tensor(
                        WS[0:64, 2048 - SLAB:2048],
                        PS[0:64, av_bank(1, j):av_bank(1, j) + SLAB],
                        WS[0:64, 0:SLAB], ALU.mult)
                    sm.inc(i, "sNM")

                for k in range(KTOT):
                    if k >= 1:
                        emit_tt(v, k - 1)
                    for half in range(2):
                        sm.wait(v, "sAV", sAV0 + NT * (k + 1))
                        if k >= 1:
                            sm.wait(v, "sBC", sBC0 + 2 * (k - 1) + half + 1)
                        i = v.tensor_copy(
                            WS[64:65, 1024 + half * 512:1024 + half * 512 + SLAB],
                            PS[64:65, av_bank(half, k):av_bank(half, k) + SLAB])
                        sm.inc(i, "sDR")
                emit_tt(v, KTOT - 1)

            @blk.gpsimd
            def _(g):
                def emit_shift(g, j):
                    pj, sj = divmod(j, NSLAB)
                    sm.wait(g, "sNM", sNM0 + 2 * j + 2)
                    sm.wait(g, "dG")
                    i = g.dma_start(
                        out=A1[64:128, pj * S + sj * SLAB:pj * S + (sj + 1) * SLAB],
                        in_=WS[0:64, 2048 - SLAB:2048])
                    sm.inc(i, "dG", 16)

                for k in range(1, KTOT):
                    emit_shift(g, k - 1)
                emit_shift(g, KTOT - 1)
                sm.wait(g, "dG")

            emit_taps(blk, [("attn", A1[:, 0:DC * S])])

        with nc.Block() as blk:
            sWQ0 = sm.n["sWQ"]
            sWT0 = sm.n["sWT"]
            sG0 = sm.n["sG"]
            sU0 = sm.n["sU"]
            sSIL0 = sm.n["sSIL"]
            sHP0 = sm.n["sHP"]
            sT0 = sm.n["sT1"]
            dGa0 = sm.n["dGa"]
            dGb0 = sm.n["dGb"]

            def zslot(it, wsel):
                return 1024 + (wsel * 2 + it % 2) * 1024

            def emit_chain(v, it, only_wsel=None):
                for wsel, ccol in ((0, c_cg), (1, c_cu)):
                    if only_wsel is not None and wsel != only_wsel:
                        continue
                    wsem = "dGa" if wsel == 0 else "dGb"
                    sm.wait(v, wsem, (dGa0 if wsel == 0 else dGb0) + (it + 1) * 16)
                    xs = X32[:, wsel * 1024:wsel * 1024 + 1024]
                    if it >= 2:
                        sm.wait(v, "sT1", sT0 + 2 * (2 * (it - 2) + wsel + 1))
                    if it >= 2:
                        sm.wait(v, "sSIL", sSIL0 + NFQ * (it - 1))
                        sm.wait(v, "sHP", sHP0 + NFQ * (it - 1))
                    dve_chain(v, [
                        lambda v: v.tensor_reduce(col(c_tmp + 2), xs, AX.X,
                                                  ALU.max, apply_absolute_value=True),
                        lambda v: v.reciprocal(col(c_tmp + 3), col(c_tmp + 2)),
                        lambda v: v.tensor_scalar(col(c_tmp + 4), col(c_tmp + 3),
                                                  127.0, None, ALU.mult),
                        lambda v: v.tensor_scalar(col(ccol + it % 2), col(c_tmp + 2),
                                                  col(c_sx), 1.0 / 127.0,
                                                  ALU.mult, ALU.mult),
                        lambda v: v.tensor_scalar(PS[:, 2048:3072], xs,
                                                  col(c_tmp + 4), MAGIC,
                                                  ALU.mult, ALU.add),
                        lambda v: v.tensor_scalar(
                            EX[:, zslot(it, wsel):zslot(it, wsel) + 1024],
                            PS[:, 2048:3072], MAGIC, None, ALU.subtract),
                    ], last_sem="sWQ")

            @blk.vector
            def _(v):
                i = v.memset(COLS[:, c_hmax:c_hmax + 2], 0.0)
                sm.inc(i, "cDVE")
                gi = 0
                emit_chain(v, 0)
                for it in range(IT):
                    if it + 1 < IT and NFQ == 1:
                        emit_chain(v, it + 1)
                    for fq in range(NFQ):
                        if it + 1 < IT and NFQ > 1 and fq in (0, 1):
                            emit_chain(v, it + 1, only_wsel=fq)
                        m = gi
                        gi += 1
                        t0 = fq * FQ
                        sm.wait(v, "sU", sU0 + m + 1)
                        sm.wait(v, "sSIL", sSIL0 + m + 1)
                        gbu = 1024 + (m % 2) * 512
                        exs = (m % 2) * 512
                        dve_chain(v, [
                            lambda v: v.scalar_tensor_tensor(
                                hp_ap(it, t0, FQ), PS[:, gbu:gbu + FQ],
                                col(c_cu + it % 2), EX[:, exs:exs + FQ],
                                ALU.mult, ALU.mult),
                            lambda v: v.tensor_reduce(col(c_tmp + 5), hp_ap(it, t0, FQ),
                                                      AX.X, ALU.max,
                                                      apply_absolute_value=True),
                            lambda v: v.tensor_tensor(col(c_hmax + (m + 1) % 2),
                                                      col(c_hmax + m % 2),
                                                      col(c_tmp + 5), ALU.max),
                        ], last_sem="sHP")

            @blk.gpsimd
            def _(g):
                for it in range(IT):
                    for wsel, w_dram in ((0, wg_h), (1, wu_h)):
                        wsem = "dGa" if wsel == 0 else "dGb"
                        sm.wait(g, "sWQ", sWQ0 + 2 * it + wsel)
                        sm.wait(g, wsem)
                        i = g.dma_start(out=X32[:, wsel * 1024:wsel * 1024 + 1024],
                                        in_=w_dram[it * 128:(it + 1) * 128, :])
                        sm.inc(i, wsem, 16)

            def emit_transp(t, it):
                for wsel in range(2):
                    sm.wait(t, "sWQ", sWQ0 + 2 * it + wsel + 1)
                    zb = zslot(it, wsel)
                    for b in range(2):
                        nb = 2 * (2 * it + wsel) + b
                        if nb >= 2:
                            sm.wait(t, "sWT", sWT0 + nb - 1)
                        pbank = 3072 + (nb % 2) * 512
                        for kk in range(4):
                            cd = b * 4 + kk
                            i = t.matmul(
                                PS[:, pbank + kk * 128:pbank + (kk + 1) * 128],
                                EX[:, zb + cd * 128:zb + (cd + 1) * 128], IDE,
                                start=True, stop=True)
                        sm.inc(i, "sT1")

            @blk.tensor
            def _(t):
                gi = 0
                emit_transp(t, 0)
                for it in range(IT):
                    for fq in range(NFQ):
                        m = gi
                        gi += 1
                        t0 = fq * FQ
                        sm.wait(t, "sWT", sWT0 + 4 * (it + 1))
                        wtg = (2 * (it % 2) + 0) * 1024
                        wtu = (2 * (it % 2) + 1) * 1024
                        gbg = (m % 2) * 512
                        gbu = 1024 + (m % 2) * 512
                        if m >= 2:
                            sm.wait(t, "sSIL", sSIL0 + m - 1)
                        for cd in range(DC):
                            i = t.matmul(PS[:, gbg:gbg + FQ],
                                         WT[:, wtg + cd * 128:wtg + (cd + 1) * 128],
                                         A4[:, cd * S + t0:cd * S + t0 + FQ],
                                         start=(cd == 0), stop=(cd == DC - 1))
                        sm.inc(i, "sG")
                        if m >= 2:
                            sm.wait(t, "sHP", sHP0 + m - 1)
                        for cd in range(DC):
                            i = t.matmul(PS[:, gbu:gbu + FQ],
                                         WT[:, wtu + cd * 128:wtu + (cd + 1) * 128],
                                         A4[:, cd * S + t0:cd * S + t0 + FQ],
                                         start=(cd == 0), stop=(cd == DC - 1))
                        sm.inc(i, "sU")
                    if it + 1 < IT:
                        emit_transp(t, it + 1)

            def emit_evacs(sc, it):
                for wsel in range(2):
                    if it >= 2:
                        sm.wait(sc, "sU", sU0 + NFQ * (it - 1))
                    for b in range(2):
                        nb = 2 * (2 * it + wsel) + b
                        sm.wait(sc, "sT1", sT0 + nb + 1)
                        pbank = 3072 + (nb % 2) * 512
                        wtb = (2 * (it % 2) + wsel) * 1024
                        i = sc.copy(WT[:, wtb + b * 512:wtb + (b + 1) * 512],
                                    PS[:, pbank:pbank + 512])
                        sm.inc(i, "sWT")

            @blk.scalar
            def _(sc):
                gi = 0
                emit_evacs(sc, 0)
                for it in range(IT):
                    for fq in range(NFQ):
                        m = gi
                        gi += 1
                        sm.wait(sc, "sG", sG0 + m + 1)
                        if m >= 2:
                            sm.wait(sc, "sHP", sHP0 + m - 1)
                        gbg = (m % 2) * 512
                        exs = (m % 2) * 512
                        i = sc.activation(EX[:, exs:exs + FQ], PS[:, gbg:gbg + FQ],
                                          AF.Silu, scale=col(c_cg + it % 2))
                        sm.inc(i, "sSIL")
                        if fq == NFQ - 1 and it + 1 < IT:
                            emit_evacs(sc, it + 1)

            NGRAN = IT * NFQ

            @blk.gpsimd
            def _(g):
                sm.wait(g, "sHP", sHP0 + NGRAN)
                i = g.dma_start(out=scr.ap()[0:1, :].rearrange("a n -> n a"),
                                in_=col(c_hmax + NGRAN % 2))
                sm.inc(i, "dG", 16)
                sm.wait(g, "dG")
                i = g.dma_start(out=ROW[0:1, :], in_=scr.ap()[0:1, :])
                sm.inc(i, "dG", 16)

            @blk.vector
            def _(v):
                sm.wait(v, "dG")
                i = v.tensor_reduce(scal(2), ROW[0:1, :], AX.X, ALU.max)
                sm.inc(i, "cDVE")

            @blk.gpsimd
            def _(g):
                sm.wait(g, "cDVE")
                i = g.dma_start(out=cc2_in.ap()[0:1, :], in_=scal(2))
                sm.inc(i, "dG", 16)
                sm.wait(g, "dG")
                i = g.collective_compute(
                    "AllReduce", ALU.max,
                    replica_groups=[list(range(n_cores))],
                    ins=[cc2_in.ap().opt()], outs=[cc2_out.ap().opt()])
                sm.inc(i, "cc2")

        p3_wq = sm.n["sWQ"]
        with nc.Block() as blk:
            dWD0 = sm.n["dWD"]

            @blk.gpsimd
            def _(g):
                sm.wait(g, "sWQ", p3_wq)
                sm.wait(g, "dWD")
                i = g.dma_start(out=X32[:, 0:II], in_=wd_h[0:128, :])
                sm.inc(i, "dWD", 16)

            load_weight_T(blk, wo_h)
            wt_ready_o = sm.n["sWT"]
            sG0 = sm.n["sG"]
            sHP0 = sm.n["sHP"]
            sHQ0 = sm.n["sROPE"]
            sSP0 = sm.n["sSP"]
            sWQ0 = sm.n["sWQ"]
            sT0 = sm.n["sT1"]
            sSC0 = sm.n["sSC"]
            cell = {}

            batches = []
            bstart = 0
            while bstart < IT:
                batches.append((bstart, min(4, IT - bstart)))
                bstart += 4
            NB = len(batches)

            def wdq_dst(ci, th):
                if ci < 16:
                    return A4[:, ci * 1024 + th * 128:ci * 1024 + (th + 1) * 128]
                if ci < 20:
                    return A5[:, 6 * S + (ci - 16) * 1024 + th * 128:
                              6 * S + (ci - 16) * 1024 + (th + 1) * 128]
                return WS[:, (ci - 20) * 1024 + th * 128:
                          (ci - 20) * 1024 + (th + 1) * 128]

            @blk.gpsimd
            def _(g):
                sm.wait(g, "cc2")
                sm.wait(g, "dG")
                i = g.dma_start(out=scal(3), in_=cc2_out.ap()[0:1, :])
                sm.inc(i, "dG", 16)

            if "gmax2" in tap_h:
                @blk.gpsimd
                def _(g):
                    sm.wait(g, "dG")
                    i = g.dma_start(out=tap_h["gmax2"][0:1, :], in_=scal(3))
                    sm.inc(i, "dG", 16)

            FG = min(512, S)
            NFG = IT * max(1, S // FG)

            def emit_fq_pl(g, fi2):
                it2, g2 = divmod(fi2, max(1, S // FG))
                t0 = g2 * FG
                if fi2 == 0:
                    sm.wait(g, "sSC", sSC0 + 1)
                else:
                    sm.wait(g, "sROPE", sHQ0 + fi2)
                i = g.tensor_scalar(PQ[:, 0:FG], hp_ap(it2, t0, FG),
                                    col(c_invs2), MAGIC, ALU.mult, ALU.add)
                sm.inc(i, "cPL")
                sm.wait(g, "cPL")
                i = g.tensor_scalar(EX[:, 0:FG], PQ[:, 0:FG],
                                    MAGIC, 127.0, ALU.subtract, ALU.min)
                sm.inc(i, "cPL")
                sm.wait(g, "cPL")
                i = g.tensor_scalar(hp_ap(it2, t0, FG), EX[:, 0:FG],
                                    -128.0, None, ALU.max)
                sm.inc(i, "sROPE")

            @blk.gpsimd
            def _(g):
                fi = 0
                for th in range(1, 8):
                    sm.wait(g, "sWQ", sWQ0 + th)
                    sm.wait(g, "dWD")
                    i = g.dma_start(out=X32[:, 0:II],
                                    in_=wd_h[th * 128:(th + 1) * 128, :])
                    sm.inc(i, "dWD", 16)
                    while fi < th * NFG // 8:
                        emit_fq_pl(g, fi)
                        fi += 1
                while fi < NFG:
                    emit_fq_pl(g, fi)
                    fi += 1

            @blk.scalar
            def _(sc):
                i = sc.activation(scal(4), scal(1), AF.Sigmoid)
                sm.inc(i, "cACT")

            act_sig = sm.n["cACT"]
            dg_g2 = sm.n["dG"]

            def emit_wd_transp(t, th):
                sm.wait(t, "sWQ", sWQ0 + th + 1)
                for bi, (c_lo, bn) in enumerate(batches):
                    nb = th * NB + bi
                    if nb >= 2:
                        sm.wait(t, "sWT", wt_ready_o + nb - 1)
                    pbank = 3072 + (nb % 2) * 512
                    for kk in range(bn):
                        ci = c_lo + kk
                        i = t.matmul(PS[:, pbank + kk * 128:pbank + (kk + 1) * 128],
                                     EX[:, 1024 + ci * 128:1024 + (ci + 1) * 128],
                                     IDE, start=True, stop=True)
                    sm.inc(i, "sT1")

            def emit_bc2(t):
                sm.wait(t, "sSP", sSP0 + 1)
                sm.wait(t, "sWQ", sWQ0 + 1)
                i = t.matmul(PS[:, 2048:2049], ONES[0:1, 0:128], SC16[0:1, 0:1],
                             start=True, stop=True)
                i = t.matmul(PS[:, 2049:2050], ONES[0:1, 0:128], SC16[0:1, 1:2],
                             start=True, stop=True)
                sm.inc(i, "cPE")
                cell["pe_p5"] = sm.n["cPE"]

            @blk.tensor
            def _(t):
                sm.wait(t, "sWT", wt_ready_o)
                nth = 0
                bc_done = False
                for tt in range(NT):
                    gb = (tt % 2) * 1024
                    if tt >= 2:
                        sm.wait(t, "sHP", sHP0 + tt - 1)
                    for cd in range(DC):
                        for c0, w in cchunks(D):
                            i = t.matmul(
                                PS[:, gb + c0:gb + c0 + w],
                                A1[:, cd * S + tt * 128:cd * S + (tt + 1) * 128],
                                WT[:, cd * 1024 + c0:cd * 1024 + c0 + w],
                                start=(cd == 0), stop=(cd == DC - 1))
                    sm.inc(i, "sG")
                    if tt == 1 and nth == 0:
                        emit_wd_transp(t, 0)
                        nth = 1
                    elif tt == 2 and not bc_done:
                        emit_bc2(t)
                        bc_done = True
                    elif tt >= 5 and tt % 2 == 1 and nth < 8 and nth == (tt - 3) // 2:
                        emit_wd_transp(t, nth)
                        nth += 1
                if not bc_done:
                    emit_bc2(t)
                while nth < 8:
                    emit_wd_transp(t, nth)
                    nth += 1

            def emit_wd_chain(v, th):
                sm.wait(v, "dWD", dWD0 + 16 * (th + 1))
                if th >= 1:
                    sm.wait(v, "sT1", sT0 + th * NB)
                ops = [
                    lambda v: v.tensor_reduce(col(c_rmax8 + th), X32[:, 0:II],
                                              AX.X, ALU.max,
                                              apply_absolute_value=True),
                    lambda v: v.reciprocal(col(c_tmp + 3), col(c_rmax8 + th)),
                    lambda v: v.tensor_scalar(col(c_tmp + 4), col(c_tmp + 3),
                                              127.0, None, ALU.mult),
                ]
                for q in range(3):
                    w = min(1024, II - q * 1024)
                    ops.append(lambda v, q=q, w=w: v.tensor_scalar(
                        PS[:, 2048:2048 + w], X32[:, q * 1024:q * 1024 + w],
                        col(c_tmp + 4), MAGIC, ALU.mult, ALU.add))
                    ops.append(lambda v, q=q, w=w: v.tensor_scalar(
                        EX[:, 1024 + q * 1024:1024 + q * 1024 + w],
                        PS[:, 2048:2048 + w], MAGIC, None, ALU.subtract))
                dve_chain(v, ops, last_sem="sWQ")

            def emit_scale_prep(v):
                sm.wait(v, "cACT", act_sig)
                sm.wait(v, "dG", dg_g2)
                dve_chain(v, [
                    lambda v: v.tensor_tensor(scal(5), scal(4), scal(3), ALU.mult),
                    lambda v: v.reciprocal(scal(6), scal(5)),
                    lambda v: v.tensor_scalar(SC16[0:1, 0:1], scal(6), 127.0, None,
                                              ALU.mult),
                    lambda v: v.tensor_scalar(SC16[0:1, 1:2], scal(5), 1.0 / 127.0,
                                              None, ALU.mult),
                ], last_sem="sSP")
                sm.wait(v, "cPE", cell["pe_p5"])
                dve_chain(v, [
                    lambda v: v.tensor_copy(col(c_invs2), PS[:, 2048:2049]),
                    lambda v: v.tensor_copy(col(c_s2), PS[:, 2049:2050]),
                ])
                i = v.tensor_copy(col(c_tmp + 5), col(c_invs2))
                sm.inc(i, "sSC")

            ao_ds0 = [sm.n["dSa"], sm.n["dSb"]]
            sAOE0 = sm.n["sAOE"]

            @blk.sync
            def _(sy):
                for tt in range(NT):
                    for half in range(2):
                        dsem = "dSa" if half == 0 else "dSb"
                        if half == 0:
                            sm.wait(sy, "sAOE", sAOE0 + tt + 1)
                        else:
                            sm.wait(sy, "sHP", sHP0 + tt + 1)
                        sm.wait(sy, dsem)
                        i = sy.dma_start(
                            out=ao_d.ap()[tt * 128:(tt + 1) * 128,
                                          half * 512:(half + 1) * 512],
                            in_=EX[:, 3840 + half * 512:3840 + (half + 1) * 512])
                        sm.inc(i, dsem, 16)

            @blk.vector
            def _(v):
                vth = 1
                sp_done = False
                emit_wd_chain(v, 0)
                for tt in range(NT):
                    gb = (tt % 2) * 1024
                    sm.wait(v, "sG", sG0 + tt + 1)
                    for half in range(2):
                        dsem = "dSa" if half == 0 else "dSb"
                        if tt >= 1:
                            sm.wait(v, dsem, ao_ds0[half] + tt * 16)
                        i = v.tensor_copy(
                            EX[:, 3840 + half * 512:3840 + (half + 1) * 512],
                            PS[:, gb + half * 512:gb + half * 512 + 512])
                        sm.inc(i, "sAOE" if half == 0 else "sHP")
                    if tt == 2 and not sp_done:
                        emit_scale_prep(v)
                        sp_done = True
                    elif tt >= 4 and tt % 2 == 0 and vth < 8 and vth == (tt - 2) // 2:
                        emit_wd_chain(v, vth)
                        vth += 1
                if not sp_done:
                    emit_scale_prep(v)
                while vth < 8:
                    emit_wd_chain(v, vth)
                    vth += 1

            @blk.scalar
            def _(sc):
                sm.wait(sc, "sWT", wt_ready_o)
                for th in range(8):
                    for bi, (c_lo, bn) in enumerate(batches):
                        nb = th * NB + bi
                        sm.wait(sc, "sT1", sT0 + nb + 1)
                        pbank = 3072 + (nb % 2) * 512
                        for kk in range(bn):
                            ci = c_lo + kk
                            i = sc.copy(wdq_dst(ci, th),
                                        PS[:, pbank + kk * 128:pbank + (kk + 1) * 128])
                        sm.inc(i, "sWT")

        wd_done_wt = sm.n["sWT"]
        wd_done_wq = sm.n["sWQ"]
        fq_total = sm.n["sROPE"]

        if "ao" in tap_h:
            with nc.Block() as blk:
                @blk.gpsimd
                def _(g):
                    sm.wait(g, "dSa")
                    sm.wait(g, "dSb")
                    for tt in range(NT):
                        sm.wait(g, "dG")
                        i = g.dma_start(out=tap_h["ao"][:, tt * D:(tt + 1) * D],
                                        in_=ao_d.ap()[tt * 128:(tt + 1) * 128, :])
                        sm.inc(i, "dG", 16)

        with nc.Block() as blk:
            @blk.vector
            def _(v):
                dve_chain(v, [
                    lambda v: v.tensor_scalar(COLS[:, c_rsd:c_rsd + 8],
                                              COLS[:, c_rmax8:c_rmax8 + 8],
                                              col(c_s2), 1.0 / 127.0,
                                              ALU.mult, ALU.mult),
                ])

            rsd_dve = sm.n["cDVE"]

            @blk.gpsimd
            def _(g):
                sm.wait(g, "cDVE", rsd_dve)
                for th in range(8):
                    sm.wait(g, "dG")
                    i = g.dma_start(out=scr.ap()[th:th + 1, :].rearrange("a n -> n a"),
                                    in_=col(c_rsd + th))
                    sm.inc(i, "dG", 16)
                sm.wait(g, "dG")
                i = g.dma_start(out=BVR[0:1, :],
                                in_=scr.ap().rearrange("a n -> (a n)").rearrange(
                                    "(x y) -> x y", x=1))
                sm.inc(i, "dG", 16)

            dg_rsf = sm.n["dG"]

            @blk.tensor
            def _(t):
                sm.wait(t, "dG", dg_rsf)
                for c0, w in cchunks(D):
                    i = t.matmul(PS[:, 2048 + c0:2048 + c0 + w], ONES[0:1, 0:128],
                                 BVR[0:1, c0:c0 + w], start=True, stop=True)
                sm.inc(i, "cPE")

            pe_rsf = sm.n["cPE"]

            @blk.vector
            def _(v):
                sm.wait(v, "cPE", pe_rsf)
                i = v.tensor_copy(EX[:, 0:1024], PS[:, 2048:3072])
                sm.inc(i, "cDVE")

            rsf_dve = sm.n["cDVE"]

            sG0 = sm.n["sG"]
            sNM0 = sm.n["sNM"]

            def wdq_rhs(ci):
                if ci < 16:
                    return A4[:, ci * 1024:(ci + 1) * 1024]
                if ci < 20:
                    return A5[:, 6 * S + (ci - 16) * 1024:6 * S + (ci - 15) * 1024]
                return WS[:, (ci - 20) * 1024:(ci - 19) * 1024]

            @blk.tensor
            def _(t):
                sm.wait(t, "cDVE", rsf_dve)
                sm.wait(t, "sWT", wd_done_wt)
                sm.wait(t, "sROPE", fq_total)
                for tt in range(NT):
                    gb = (tt % 2) * 1024
                    if tt >= 2:
                        sm.wait(t, "sNM", sNM0 + tt - 1)
                    for ci in range(IT):
                        rhs = wdq_rhs(ci)
                        for c0, w in cchunks(D):
                            i = t.matmul(
                                PS[:, gb + c0:gb + c0 + w],
                                hp_ap(ci, tt * 128, 128),
                                rhs[:, c0:c0 + w],
                                start=(ci == 0), stop=(ci == IT - 1))
                    sm.inc(i, "sG")

            dsa0 = sm.n["dSa"]
            dsb0 = sm.n["dSb"]
            dga0 = sm.n["dGa"]
            dgb0 = sm.n["dGb"]

            @blk.gpsimd
            def _(g):
                sm.wait(g, "dSa")
                sm.wait(g, "dSb")
                for tt in range(NT):
                    gsem = "dGa" if tt % 2 == 0 else "dGb"
                    sm.wait(g, gsem)
                    i = g.dma_start(out=A1[:, tt * 1024:(tt + 1) * 1024],
                                    in_=ao_d.ap()[tt * 128:(tt + 1) * 128, :])
                    sm.inc(i, gsem, 16)

            @blk.vector
            def _(v):
                sm.wait(v, "cDVE", rsf_dve)
                for tt in range(NT):
                    dsem = "dSa" if tt % 2 == 0 else "dSb"
                    dbase = (dsa0 if tt % 2 == 0 else dsb0) + (tt // 2 + 1) * 16
                    gb = (tt % 2) * 1024
                    sm.wait(v, "sG", sG0 + tt + 1)
                    if tt >= 1:
                        sm.wait(v, "sNM", sNM0 + tt)
                    sm.wait(v, dsem, dbase)
                    gsem = "dGa" if tt % 2 == 0 else "dGb"
                    gbase = (dga0 if tt % 2 == 0 else dgb0) + (tt // 2 + 1) * 16
                    sm.wait(v, gsem, gbase)
                    xs = X32[:, (tt % 2) * 1024:(tt % 2) * 1024 + 1024]
                    dve_chain(v, [
                        lambda v: v.tensor_tensor(PS[:, 2048:3072],
                                                  PS[:, gb:gb + 1024],
                                                  EX[:, 0:1024], ALU.mult),
                        lambda v: v.tensor_tensor(PS[:, 2048:3072],
                                                  PS[:, 2048:3072],
                                                  A1[:, tt * 1024:(tt + 1) * 1024],
                                                  ALU.add),
                        lambda v: v.tensor_tensor(xs, PS[:, 2048:3072], xs, ALU.add),
                    ], last_sem="sNM")

            @blk.sync
            def _(sy):
                for tt in range(min(NT, 2)):
                    dsem = "dSa" if tt % 2 == 0 else "dSb"
                    sm.wait(sy, dsem)
                    i = sy.dma_start(out=X32[:, (tt % 2) * 1024:(tt % 2) * 1024 + 1024],
                                     in_=x_h[tt * 128:(tt + 1) * 128, :])
                    sm.inc(i, dsem, 16)
                for tt in range(NT):
                    osem = "dOa" if tt % 2 == 0 else "dOb"
                    sm.wait(sy, "sNM", sNM0 + tt + 1)
                    sm.wait(sy, osem)
                    i = sy.dma_start(out=out_h[tt * 128:(tt + 1) * 128, :],
                                     in_=X32[:, (tt % 2) * 1024:(tt % 2) * 1024 + 1024])
                    sm.inc(i, osem, 16)
                    if tt + 2 < NT:
                        dsem = "dSa" if tt % 2 == 0 else "dSb"
                        sm.wait(sy, osem)
                        sm.wait(sy, dsem)
                        i = sy.dma_start(out=X32[:, (tt % 2) * 1024:(tt % 2) * 1024 + 1024],
                                         in_=x_h[(tt + 2) * 128:(tt + 3) * 128, :])
                        sm.inc(i, dsem, 16)

            if "hp" in tap_h:
                @blk.gpsimd
                def _(g):
                    sm.wait(g, "sROPE", fq_total)
                    for it2 in range(IT):
                        sm.wait(g, "dG")
                        i = g.dma_start(out=tap_h["hp"][:, it2 * S:(it2 + 1) * S],
                                        in_=hp_ap(it2, 0, S))
                        sm.inc(i, "dG", 16)

            @blk.gpsimd
            def _(g):
                sm.wait(g, "dOa")
                sm.wait(g, "dOb")
                sm.wait(g, "dG")

    return nc, tap_h


def kernel(L0_attn_x, L0_attn_Wq, L0_attn_bq, L0_attn_Wk, L0_attn_bk,
           L0_attn_Wv, L0_attn_bv, L0_attn_Wo, L0_ffn_clip_x, L0_ffn_clip_down,
           L0_ffn_w_gate, L0_ffn_w_up, L0_ffn_w_down):
    from concourse.bass_utils import run_bass_kernel_spmd
    x = np.ascontiguousarray(np.asarray(L0_attn_x, dtype=np.float32))
    B, S_, D_ = x.shape
    assert D_ == D and B == 8
    shared = dict(
        Wq=np.ascontiguousarray(L0_attn_Wq, dtype=np.float32),
        bq=np.ascontiguousarray(L0_attn_bq, dtype=np.float32),
        Wk=np.ascontiguousarray(L0_attn_Wk, dtype=np.float32),
        bk=np.ascontiguousarray(L0_attn_bk, dtype=np.float32),
        Wv=np.ascontiguousarray(L0_attn_Wv, dtype=np.float32),
        bv=np.ascontiguousarray(L0_attn_bv, dtype=np.float32),
        Wo=np.ascontiguousarray(L0_attn_Wo, dtype=np.float32),
        clip_x=np.ascontiguousarray(L0_ffn_clip_x, dtype=np.float32),
        clip_down=np.ascontiguousarray(L0_ffn_clip_down, dtype=np.float32),
        w_gate=np.ascontiguousarray(L0_ffn_w_gate, dtype=np.float32),
        w_up=np.ascontiguousarray(L0_ffn_w_up, dtype=np.float32),
        w_down=np.ascontiguousarray(L0_ffn_w_down, dtype=np.float32),
    )
    nc, _ = build(S=S_, n_cores=B)
    in_maps = [dict(shared, x=np.ascontiguousarray(x[b])) for b in range(B)]
    res = run_bass_kernel_spmd(nc, in_maps, core_ids=list(range(B)))
    out = np.stack([np.asarray(res.results[b]["out"], dtype=np.float32)
                    for b in range(B)], axis=0)
    return out


# version 17
# speedup vs baseline: 1.1719x; 1.1719x over previous
import numpy as np
import contextlib
import concourse.bass as bass
import concourse.mybir as mybir

F32 = mybir.dt.float32
F16 = mybir.dt.float16
AF = mybir.ActivationFunctionType
ALU = mybir.AluOpType
AX = mybir.AxisListType

MAGIC = 1.5 * 2.0**23
THETA = 10000.0
TWO_PI = 2.0 * np.pi
C1 = 6.28125
C2 = float(np.float32(np.frombuffer(np.uint32(
    np.frombuffer(np.float32(TWO_PI - C1).tobytes(), dtype=np.uint32)[0]
    & 0xFFFFF000).tobytes(), dtype=np.float32)[0]))
C3 = float(TWO_PI - C1 - C2)
INV2PI = float(np.float32(1.0 / TWO_PI))
LNT32 = float(np.log(THETA) / 32.0)
PI = float(np.pi)

D = 1024
H = 16
HD = 64
DC = 8
II = 2816
IT = 22
EPS = 1e-5

c_jcol, c_hi, c_jm, c_pair, c_sgn, c_f = 0, 1, 2, 3, 4, 5
c_bq, c_bk = 6, 14
c_negmu, c_rstd = 22, 24
c_xmax = 26
c_invsx, c_sx = 28, 29
c_hmax = 30
c_invs2, c_s2 = 32, 33
c_cg, c_cu = 34, 38
c_rsd = 42
c_rmax8 = 56
c_bias = 48
c_sc = 64
c_eps = 79
c_st6 = 80
c_mv = 92
c_tmp = 96
c_mq2 = 104

SEMS = ["dS", "dSa", "dSb", "dG", "dGa", "dGb", "dOa", "dOb",
        "cPE", "cDVE", "cACT", "cPL", "cc1", "cc2",
        "sST", "sEXP", "sAV", "sBC", "sDR", "sDN", "sNM",
        "sG", "sU", "sSIL", "sHP", "sWQ", "sWT", "sQE", "sT1", "sPERM",
        "sROPE", "sSQ", "sRD", "dWD", "sSP", "sAOE", "sSC"]


class Sems:
    def __init__(self, nc, st):
        self.h, self.n = {}, {}
        for name in SEMS:
            self.h[name] = st.enter_context(nc.semaphore(name))
            self.n[name] = 0

    def inc(self, instr, s, k=1):
        instr.then_inc(self.h[s], k)
        self.n[s] += k
        return self.n[s]

    def wait(self, eng, s, n=None):
        v = self.n[s] if n is None else n
        if v > 0:
            eng.wait_ge(self.h[s], v)


def build(S=2048, n_cores=8, taps=()):
    NT = S // 128
    SLAB = min(512, S)
    NSLAB = S // SLAB
    QG = min(1024, S)
    NQG = S // QG
    FQ = min(512, S)
    NFQ = S // FQ
    KTOT = DC * NSLAB

    nc = bass.Bass("TRN2", target_bir_lowering=False, debug=False)

    x_h = nc.dram_tensor("x", [S, D], F32, kind="ExternalInput").ap()
    wq_h = nc.dram_tensor("Wq", [D, D], F32, kind="ExternalInput").ap()
    bq_h = nc.dram_tensor("bq", [D], F32, kind="ExternalInput").ap()
    wk_h = nc.dram_tensor("Wk", [D, D], F32, kind="ExternalInput").ap()
    bk_h = nc.dram_tensor("bk", [D], F32, kind="ExternalInput").ap()
    wv_h = nc.dram_tensor("Wv", [D, D], F32, kind="ExternalInput").ap()
    bv_h = nc.dram_tensor("bv", [D], F32, kind="ExternalInput").ap()
    wo_h = nc.dram_tensor("Wo", [D, D], F32, kind="ExternalInput").ap()
    clipx_h = nc.dram_tensor("clip_x", [1], F32, kind="ExternalInput").ap()
    clipd_h = nc.dram_tensor("clip_down", [1], F32, kind="ExternalInput").ap()
    wg_h = nc.dram_tensor("w_gate", [II, D], F32, kind="ExternalInput").ap()
    wu_h = nc.dram_tensor("w_up", [II, D], F32, kind="ExternalInput").ap()
    wd_h = nc.dram_tensor("w_down", [D, II], F32, kind="ExternalInput").ap()
    out_h = nc.dram_tensor("out", [S, D], F32, kind="ExternalOutput").ap()

    tap_h = {}
    def mktap(name, shape, dt):
        if name in taps:
            tap_h[name] = nc.dram_tensor("tap_" + name, list(shape), dt,
                                         kind="ExternalOutput").ap()

    mktap("xn", [128, DC * S], F16)
    mktap("q", [128, DC * S], F16)
    mktap("k", [128, DC * S], F16)
    mktap("v", [128, NT * 1040], F16)
    mktap("xq", [128, DC * S], F16)
    mktap("attn", [128, DC * S], F16)
    mktap("hp", [128, IT * S], F16)
    mktap("ao", [128, DC * S], F16)
    mktap("gmax1", [1, 1], F32)
    mktap("gmax2", [1, 1], F32)

    scr = nc.dram_tensor("scr", [8, 128], F32)
    cc1_in = nc.dram_tensor("cc1_in", [1, 1], F32)
    cc1_out = nc.dram_tensor("cc1_out", [1, 1], F32)
    cc2_in = nc.dram_tensor("cc2_in", [1, 1], F32)
    cc2_out = nc.dram_tensor("cc2_out", [1, 1], F32)
    ao_d = nc.dram_tensor("ao_d", [S, D], F16)

    A_COLS = max(DC * S, 16 * 1024)

    st = contextlib.ExitStack()
    with st:
        A1 = st.enter_context(nc.sbuf_tensor("A1", [128, A_COLS], F16))
        A4 = st.enter_context(nc.sbuf_tensor("A4", [128, A_COLS], F16))
        A5 = st.enter_context(nc.sbuf_tensor("A5", [128, max(NT * 1040, 6 * S + 4096)], F16))
        WT = st.enter_context(nc.sbuf_tensor("WTa", [128, 8192], F16))
        WS = st.enter_context(nc.sbuf_tensor("WSa", [128, 2048], F16))
        X32 = st.enter_context(nc.sbuf_tensor("X32", [128, 2816], F32))
        EX = st.enter_context(nc.sbuf_tensor("EXa", [128, max(5120, 2 * S)], F16))
        COLS = st.enter_context(nc.sbuf_tensor("COLS", [128, 128], F32))
        CN = st.enter_context(nc.sbuf_tensor("CN", [128, 384], F16))
        ROW = st.enter_context(nc.sbuf_tensor("ROWa", [1, 128], F32))
        SC16 = st.enter_context(nc.sbuf_tensor("SC16", [1, 32], F16))
        BVR = st.enter_context(nc.sbuf_tensor("BVR", [1, D], F16))
        PQ = st.enter_context(nc.sbuf_tensor("PQ", [128, 512], F32))
        PS = st.enter_context(nc.psum_tensor("PS", [128, 4096], F32))
        sm = Sems(nc, st)

        IDE = CN[:, 0:128]
        PERM = CN[:, 128:256]
        ONES = CN[:, 256:384]

        def col(c):
            return COLS[:, c:c + 1]

        def scal(c):
            return COLS[0:1, c_sc + c:c_sc + c + 1]

        def cchunks(W):
            return [(c0, min(512, W - c0)) for c0 in range(0, W, 512)]

        def dve_chain(v, ops, last_sem=None):
            for j, fn in enumerate(ops):
                i = fn(v)
                if last_sem is not None and j == len(ops) - 1:
                    sm.inc(i, last_sem)
                else:
                    sm.inc(i, "cDVE")
                    sm.wait(v, "cDVE")

        def act_chain(sc, ops, last_sem=None):
            for j, fn in enumerate(ops):
                i = fn(sc)
                if last_sem is not None and j == len(ops) - 1:
                    sm.inc(i, last_sem)
                else:
                    sm.inc(i, "cACT")
                    sm.wait(sc, "cACT")

        def pl_chain(g, ops):
            for fn in ops:
                i = fn(g)
                sm.inc(i, "cPL")
                sm.wait(g, "cPL")

        def hp_ap(it, t0, w):
            if it < 8:
                return A2[:, it * S + t0:it * S + t0 + w]
            if it < 16:
                return A3[:, (it - 8) * S + t0:(it - 8) * S + t0 + w]
            return A5[:, (it - 16) * S + t0:(it - 16) * S + t0 + w]

        def emit_taps(blk, items):
            live = [(n, ap) for n, ap in items if n in tap_h]
            if not live:
                return
            @blk.gpsimd
            def _(g):
                sm.wait(g, "cDVE")
                sm.wait(g, "cACT")
                for n, ap in live:
                    sm.wait(g, "dG")
                    i = g.dma_start(out=tap_h[n][:, :] if tap_h[n].shape[0] > 1
                                    else tap_h[n][0:1, :], in_=ap)
                    sm.inc(i, "dG", 16)
                sm.wait(g, "dG")

        p0 = contextlib.ExitStack()
        A2 = A3 = None
        with p0:
            F32T = p0.enter_context(nc.sbuf_tensor("F32T", [128, 2 * S], F32))
            SA = X32[:, 0:S]
            SB = F32T[:, 0:S]
            SCc = F32T[:, S:2 * S]

            with nc.Block() as blk:
                @blk.sync
                def _(sy):
                    i = sy.dma_start(out=COLS[:, c_bq:c_bq + 8],
                                     in_=bq_h.rearrange("(a p) -> p a", p=128),
                                     allow_slow_non_contiguous=True)
                    sm.inc(i, "dS", 16)
                    i = sy.dma_start(out=COLS[:, c_bk:c_bk + 8],
                                     in_=bk_h.rearrange("(a p) -> p a", p=128),
                                     allow_slow_non_contiguous=True)
                    sm.inc(i, "dS", 16)
                    i = sy.dma_start(out=COLS[0:1, c_sc + 0:c_sc + 1],
                                     in_=clipx_h.rearrange("(a b) -> a b", a=1))
                    sm.inc(i, "dS", 16)
                    i = sy.dma_start(out=COLS[0:1, c_sc + 1:c_sc + 2],
                                     in_=clipd_h.rearrange("(a b) -> a b", a=1))
                    sm.inc(i, "dS", 16)

                @blk.gpsimd
                def _(g):
                    g.memset(col(c_eps), EPS)
                    g.iota(col(c_jcol), [[0, 1]], channel_multiplier=1,
                           allow_small_or_imprecise_dtypes=True)
                    i = g.iota(SA, [[1, S]], channel_multiplier=0,
                               allow_small_or_imprecise_dtypes=True)
                    sm.inc(i, "cPL")
                    g.memset(CN[:, 0:256], 0.0)
                    i = g.memset(ONES, 1.0)
                    sm.inc(i, "cPL")
                    sm.wait(g, "cPL")
                    pl_chain(g, [
                        lambda g: g.affine_select(out=IDE, in_=IDE,
                                                  compare_op=ALU.not_equal,
                                                  fill=1.0, base=0,
                                                  pattern=[[-1, 128]],
                                                  channel_multiplier=1),
                        lambda g: g.memset(WS[:, 0:256], 0.0),
                        lambda g: g.tensor_copy(WS[:, 1:128], IDE[:, 0:127]),
                        lambda g: g.tensor_copy(WS[:, 128:255], IDE[:, 1:128]),
                        lambda g: g.memset(
                            A5[:, 0:NT * 1040].rearrange("p (t h c) -> p t h c",
                                                         t=NT, h=H)[:, :, :, 64:65],
                            1.0),
                    ])

                @blk.vector
                def _(v):
                    sm.wait(v, "cPL", 1)
                    dve_chain(v, [
                        lambda v: v.tensor_scalar(col(c_hi), col(c_jcol), 1.0 / 64.0,
                                                  -0.5 + 0.25 / 64.0, ALU.mult, ALU.add),
                        lambda v: v.tensor_scalar(col(c_hi), col(c_hi), MAGIC, MAGIC,
                                                  ALU.add, ALU.subtract),
                        lambda v: v.scalar_tensor_tensor(col(c_jm), col(c_hi), -64.0,
                                                         col(c_jcol), ALU.mult, ALU.add),
                        lambda v: v.tensor_scalar(col(c_pair), col(c_jm), 0.5, -0.25,
                                                  ALU.mult, ALU.add),
                        lambda v: v.tensor_scalar(col(c_pair), col(c_pair), MAGIC, MAGIC,
                                                  ALU.add, ALU.subtract),
                        lambda v: v.scalar_tensor_tensor(col(c_sgn), col(c_pair), 2.0,
                                                         col(c_jm), ALU.mult, ALU.subtract),
                        lambda v: v.tensor_scalar(col(c_sgn), col(c_sgn), 2.0, 1.0,
                                                  ALU.mult, ALU.add),
                    ])

                @blk.scalar
                def _(sc):
                    sm.wait(sc, "cDVE")
                    i = sc.activation(col(c_f), col(c_pair), AF.Exp, scale=-LNT32)
                    sm.inc(i, "cACT")

                @blk.vector
                def _(v):
                    sm.wait(v, "cACT")
                    dve_chain(v, [
                        lambda v: v.tensor_scalar(SB, SA, col(c_f), None, ALU.mult),
                        lambda v: v.tensor_scalar(SA, SB, INV2PI, None, ALU.mult),
                        lambda v: v.tensor_scalar(SCc, SA, MAGIC, MAGIC, ALU.add, ALU.subtract),
                        lambda v: v.scalar_tensor_tensor(SA, SCc, -C1, SB, ALU.mult, ALU.add),
                        lambda v: v.scalar_tensor_tensor(SB, SCc, -C2, SA, ALU.mult, ALU.add),
                        lambda v: v.scalar_tensor_tensor(SCc, SCc, -C3, SB, ALU.mult, ALU.add)
                        if False else
                        v.scalar_tensor_tensor(SB, SCc, -C3, SB, ALU.mult, ALU.add)
                        if False else
                        v.scalar_tensor_tensor(SA, SCc, -C3, SB, ALU.mult, ALU.add),
                        lambda v: v.tensor_scalar(SA, SA, PI, -PI, ALU.min, ALU.max)
                        if False else
                        v.tensor_scalar(SB, SA, PI, -PI, ALU.min, ALU.max),
                        lambda v: v.tensor_scalar(SA, SB, 1.0, None, ALU.mult),
                        lambda v: v.tensor_scalar(SB, SA, col(c_sgn), None, ALU.mult),
                    ])

                @blk.scalar
                def _(sc):
                    sm.wait(sc, "cDVE")
                    i = sc.activation(EX[:, 0:S], SB, AF.Sin)
                    sm.inc(i, "cACT")

                @blk.vector
                def _(v):
                    sm.wait(v, "cACT")
                    dve_chain(v, [
                        lambda v: v.tensor_scalar(SB, SA, PI / 2.0, None, ALU.add),
                        lambda v: v.tensor_scalar(SCc, SB, PI, None, ALU.is_gt),
                        lambda v: v.scalar_tensor_tensor(SA, SCc, -TWO_PI, SB,
                                                         ALU.mult, ALU.add),
                        lambda v: v.tensor_scalar(SCc, SA, PI, -PI, ALU.min, ALU.max),
                        lambda v: v.tensor_scalar(SA, SCc, 1.0, None, ALU.mult),
                    ])

                @blk.scalar
                def _(sc):
                    sm.wait(sc, "cDVE")
                    i = sc.activation(EX[:, S:2 * S], SA, AF.Sin)
                    sm.inc(i, "cACT")

                @blk.vector
                def _(v):
                    sm.wait(v, "cACT")
                    sm.wait(v, "cPL", 6)
                    dve_chain(v, [
                        lambda v: v.tensor_scalar(col(c_tmp), col(c_sgn), 0.5, 0.5,
                                                  ALU.mult, ALU.add),
                        lambda v: v.tensor_tensor(WS[:, 256:384], WS[:, 0:128],
                                                  WS[:, 128:256], ALU.subtract),
                        lambda v: v.tensor_scalar(WS[:, 256:384], WS[:, 256:384],
                                                  col(c_tmp), None, ALU.mult),
                        lambda v: v.tensor_tensor(PERM, WS[:, 256:384], WS[:, 128:256],
                                                  ALU.add),
                        lambda v: v.memset(COLS[:, c_xmax:c_xmax + 2], 0.0),
                    ])

                act_pre = sm.n["cACT"]
                dve_pre = sm.n["cDVE"]
                VPT, APT = 7, 2

                @blk.sync
                def _(sy):
                    for tt in range(NT):
                        dsem = "dSa" if tt % 2 == 0 else "dSb"
                        if tt < 2:
                            sm.wait(sy, "cDVE", dve_pre)
                            sm.wait(sy, "cACT", act_pre)
                        else:
                            sm.wait(sy, "cDVE", dve_pre + (tt - 2) * VPT + 5)
                        sm.wait(sy, dsem)
                        i = sy.dma_start(out=X32[:, (tt % 2) * 1024:(tt % 2) * 1024 + 1024],
                                         in_=x_h[tt * 128:(tt + 1) * 128, :])
                        sm.inc(i, dsem, 16)

                @blk.vector
                def _(v):
                    for tt in range(NT):
                        dsem = "dSa" if tt % 2 == 0 else "dSb"
                        xs = X32[:, (tt % 2) * 1024:(tt % 2) * 1024 + 1024]
                        ws = WS[:, (tt % 2) * 1024:(tt % 2) * 1024 + 1024]
                        sm.wait(v, dsem, (tt // 2 + 1) * 16)
                        dve_chain(v, [
                            lambda v: v.bn_stats(COLS[:, c_st6:c_st6 + 6], xs[:, 0:512]),
                            lambda v: v.bn_stats(COLS[:, c_st6 + 6:c_st6 + 12],
                                                 xs[:, 512:1024]),
                            lambda v: v.bn_aggr(COLS[:, c_mv:c_mv + 2],
                                                COLS[:, c_st6:c_st6 + 12]),
                            lambda v: v.tensor_scalar(col(c_negmu + tt % 2),
                                                      col(c_mv), -1.0, None, ALU.mult),
                        ])
                        sm.wait(v, "cACT", act_pre + tt * APT + 2)
                        if tt >= 2:
                            sm.wait(v, "sG", tt - 1)
                        dve_chain(v, [
                            lambda v: v.tensor_scalar(ws, xs, col(c_negmu + tt % 2),
                                                      col(c_rstd + tt % 2),
                                                      ALU.add, ALU.mult),
                            lambda v: v.tensor_reduce(col(c_tmp), ws, AX.X, ALU.max,
                                                      apply_absolute_value=True),
                            lambda v: v.tensor_tensor(col(c_xmax + (tt + 1) % 2),
                                                      col(c_xmax + tt % 2),
                                                      col(c_tmp), ALU.max),
                        ])

                @blk.scalar
                def _(sc):
                    for tt in range(NT):
                        sm.wait(sc, "cDVE", dve_pre + tt * VPT + 4)
                        act_chain(sc, [
                            lambda sc: sc.activation(col(c_tmp + 1), col(c_mv + 1),
                                                     AF.Ln, bias=col(c_eps)),
                            lambda sc: sc.activation(col(c_rstd + tt % 2),
                                                     col(c_tmp + 1), AF.Exp, scale=-0.5),
                        ])
                        sm.wait(sc, "sG", tt + 1)
                        bank = (tt % 2) * 2048
                        i = sc.copy(
                            A1[:, 0:DC * S].rearrange("p (c s) -> p c s", c=DC)[:, :, tt * 128:(tt + 1) * 128],
                            PS[:, bank:bank + 1024].rearrange("p (c b) -> p c b", c=DC))
                        sm.inc(i, "sWT")

                @blk.tensor
                def _(t):
                    sm.wait(t, "cPL", 3)
                    for tt in range(NT):
                        ws = WS[:, (tt % 2) * 1024:(tt % 2) * 1024 + 1024]
                        sm.wait(t, "cDVE", dve_pre + tt * VPT + 5)
                        if tt >= 2:
                            sm.wait(t, "sWT", tt - 1)
                        bank = (tt % 2) * 2048
                        for cd in range(DC):
                            i = t.matmul(PS[:, bank + cd * 128:bank + (cd + 1) * 128],
                                         ws[:, cd * 128:(cd + 1) * 128], IDE,
                                         start=True, stop=True)
                        sm.inc(i, "sG")

                @blk.gpsimd
                def _(g):
                    sm.wait(g, "cDVE", dve_pre + NT * VPT)
                    i = g.dma_start(out=scr.ap()[0:1, :].rearrange("a n -> n a"),
                                    in_=col(c_xmax + NT % 2))
                    sm.inc(i, "dG", 16)
                    sm.wait(g, "dG")
                    i = g.dma_start(out=ROW[0:1, :], in_=scr.ap()[0:1, :])
                    sm.inc(i, "dG", 16)

                @blk.vector
                def _(v):
                    sm.wait(v, "dG")
                    i = v.tensor_reduce(scal(2), ROW[0:1, :], AX.X, ALU.max)
                    sm.inc(i, "cDVE")

                @blk.gpsimd
                def _(g):
                    sm.wait(g, "cDVE")
                    i = g.dma_start(out=cc1_in.ap()[0:1, :], in_=scal(2))
                    sm.inc(i, "dG", 16)
                    sm.wait(g, "dG")
                    i = g.collective_compute(
                        "AllReduce", ALU.max,
                        replica_groups=[list(range(n_cores))],
                        ins=[cc1_in.ap().opt()], outs=[cc1_out.ap().opt()])
                    sm.inc(i, "cc1")

        A2 = st.enter_context(nc.sbuf_tensor("A2", [128, max(DC * S, 8 * S)], F16))
        A3 = st.enter_context(nc.sbuf_tensor("A3", [128, max(DC * S, 8 * S)], F16))

        def load_weight_T(blk, w_dram, wt_free_sem=None, wt_free_val=0):
            pe0 = sm.n["sU"]
            ev0 = sm.n["sWT"]

            @blk.gpsimd
            def _(g):
                for cj in range(8):
                    wsem = "dGa" if cj % 2 == 0 else "dGb"
                    sm.wait(g, "sU", pe0 + max(0, (cj - 1) * 2))
                    sm.wait(g, wsem)
                    i = g.dma_start(out=WS[:, (cj % 2) * 1024:(cj % 2) * 1024 + 1024],
                                    in_=w_dram[cj * 128:(cj + 1) * 128, :])
                    sm.inc(i, wsem, 16)

            dg_a = sm.n["dGa"] - 4 * 16
            dg_b = sm.n["dGb"] - 4 * 16

            @blk.tensor
            def _(t):
                for cj in range(8):
                    wsem = "dGa" if cj % 2 == 0 else "dGb"
                    base = (dg_a if cj % 2 == 0 else dg_b) + (cj // 2 + 1) * 16
                    sm.wait(t, wsem, base)
                    ws = WS[:, (cj % 2) * 1024:(cj % 2) * 1024 + 1024]
                    for b in range(2):
                        nb = 2 * cj + b
                        if nb >= 2:
                            sm.wait(t, "sWT", ev0 + nb - 1)
                        pbank = 3072 + (nb % 2) * 512
                        for kk in range(4):
                            cd = b * 4 + kk
                            i = t.matmul(PS[:, pbank + kk * 128:pbank + (kk + 1) * 128],
                                         ws[:, cd * 128:(cd + 1) * 128], IDE,
                                         start=True, stop=True)
                        sm.inc(i, "sU")

            @blk.scalar
            def _(sc):
                if wt_free_sem is not None:
                    sm.wait(sc, wt_free_sem, wt_free_val)
                for cj in range(8):
                    for b in range(2):
                        nb = 2 * cj + b
                        sm.wait(sc, "sU", pe0 + nb + 1)
                        pbank = 3072 + (nb % 2) * 512
                        i = sc.copy(
                            WT[:, 0:8192].rearrange("p (c b) -> p c b", c=DC)
                            [:, b * 4:(b + 1) * 4, cj * 128:(cj + 1) * 128],
                            PS[:, pbank:pbank + 512].rearrange("p (c b) -> p c b", c=4))
                        sm.inc(i, "sWT")

        with nc.Block() as blk:
            def qk_proj(dst, bias_c):
                NG = 8 * NQG
                sST0 = sm.n["sST"]
                sQE0 = sm.n["sQE"]
                sT10 = sm.n["sT1"]
                sPERM0 = sm.n["sPERM"]
                sROPE0 = sm.n["sROPE"]
                wt_ready = sm.n["sWT"]

                @blk.tensor
                def _(t):
                    sm.wait(t, "sWT", wt_ready)
                    for g in range(NG + 1):
                        if g < NG:
                            cj, qg = divmod(g, NQG)
                            gb = (g % 2) * 1024
                            if g >= 2:
                                sm.wait(t, "sQE", sQE0 + g - 1)
                            for cd in range(DC):
                                for c0, w in cchunks(QG):
                                    i = t.matmul(
                                        PS[:, gb + c0:gb + c0 + w],
                                        WT[:, cd * 1024 + cj * 128:cd * 1024 + (cj + 1) * 128],
                                        A1[:, cd * S + qg * QG + c0:cd * S + qg * QG + c0 + w],
                                        start=(cd == 0), stop=(cd == DC - 1))
                            sm.inc(i, "sST")
                        if g >= 1:
                            gp = g - 1
                            sm.wait(t, "sT1", sT10 + gp + 1)
                            if gp >= 1:
                                sm.wait(t, "sROPE", sROPE0 + gp)
                            rs = A4[:, (gp % 2) * 1024:(gp % 2) * 1024 + QG]
                            for c0, w in cchunks(QG):
                                i = t.matmul(PS[:, 2048 + c0:2048 + c0 + w],
                                             PERM, rs[:, c0:c0 + w],
                                             start=True, stop=True)
                            sm.inc(i, "sPERM")

                sSQ0 = sm.n["sSQ"]

                @blk.scalar
                def _(sc):
                    sm.wait(sc, "dS", 64)
                    for g in range(NG):
                        cj = g // NQG
                        gb = (g % 2) * 1024
                        sm.wait(sc, "sST", sST0 + g + 1)
                        sm.wait(sc, "sT1", sT10 + max(0, g - 1))
                        sm.wait(sc, "sSQ", sSQ0 + max(0, g - 1))
                        i = sc.activation(X32[:, gb:gb + QG], PS[:, gb:gb + QG],
                                          AF.Identity, bias=col(bias_c + cj))
                        sm.inc(i, "sQE")

                @blk.vector
                def _(v):
                    for g in range(NG):
                        cj, qg = divmod(g, NQG)
                        t0 = qg * QG
                        gb = (g % 2) * 1024
                        xr = X32[:, gb:gb + QG]
                        rs = A4[:, gb:gb + QG]
                        sm.wait(v, "sQE", sQE0 + g + 1)
                        if g >= 2:
                            sm.wait(v, "sPERM", sPERM0 + g - 1)
                        i = v.tensor_tensor(rs, xr, EX[:, t0:t0 + QG], ALU.mult)
                        sm.inc(i, "sT1")
                        sm.wait(v, "sPERM", sPERM0 + g + 1)
                        i = v.tensor_tensor(rs, xr, EX[:, S + t0:S + t0 + QG], ALU.mult)
                        sm.inc(i, "sSQ")
                        sm.wait(v, "sSQ")
                        i = v.tensor_tensor(dst[:, cj * S + t0:cj * S + t0 + QG],
                                            rs, PS[:, 2048:2048 + QG], ALU.add)
                        sm.inc(i, "sROPE")

            load_weight_T(blk, wq_h)
            qk_proj(A2, c_bq)
            load_weight_T(blk, wk_h, wt_free_sem="sST", wt_free_val=sm.n["sST"])
            qk_proj(A3, c_bk)
            load_weight_T(blk, wv_h, wt_free_sem="sST", wt_free_val=sm.n["sST"])

            @blk.gpsimd
            def _(g):
                sm.wait(g, "dGa")
                i = g.dma_start(out=BVR[0:1, :], in_=bv_h.rearrange("(a b) -> a b", a=1))
                sm.inc(i, "dGa", 16)

            dg_bv = sm.n["dGa"]
            rope_done = sm.n["sROPE"]
            wtv_ready = sm.n["sWT"]
            sG0 = sm.n["sG"]
            sU0v = sm.n["sHP"]

            @blk.tensor
            def _(t):
                sm.wait(t, "dGa", dg_bv)
                sm.wait(t, "sROPE", rope_done)
                sm.wait(t, "sWT", wtv_ready)
                for c0, w in cchunks(D):
                    i = t.matmul(PS[:, 2048 + c0:2048 + c0 + w], ONES[0:1, 0:128],
                                 BVR[0:1, c0:c0 + w], start=True, stop=True)
                sm.inc(i, "sG")
                for tt in range(NT):
                    gb = (tt % 2) * 1024
                    if tt >= 2:
                        sm.wait(t, "sHP", sU0v + tt - 1)
                    for cd in range(DC):
                        for c0, w in cchunks(D):
                            i = t.matmul(
                                PS[:, gb + c0:gb + c0 + w],
                                A1[:, cd * S + tt * 128:cd * S + (tt + 1) * 128],
                                WT[:, cd * 1024 + c0:cd * 1024 + c0 + w],
                                start=(cd == 0), stop=(cd == DC - 1))
                    sm.inc(i, "sG")

            @blk.vector
            def _(v):
                sm.wait(v, "sG", sG0 + 1)
                i = v.tensor_copy(WS[:, 0:1024], PS[:, 2048:3072])
                sm.inc(i, "cDVE")
                sm.wait(v, "cDVE")
                for tt in range(NT):
                    gb = (tt % 2) * 1024
                    sm.wait(v, "sG", sG0 + tt + 2)
                    i = v.tensor_tensor(
                        A5[:, tt * 1040:(tt + 1) * 1040].rearrange(
                            "p (h c) -> p h c", h=H)[:, :, 0:64],
                        PS[:, gb:gb + 1024].rearrange("p (h c) -> p h c", h=H),
                        WS[:, 0:1024].rearrange("p (h c) -> p h c", h=H),
                        ALU.add)
                    sm.inc(i, "sHP")

            @blk.gpsimd
            def _(g):
                sm.wait(g, "cc1")
                i = g.dma_start(out=scal(3), in_=cc1_out.ap()[0:1, :])
                sm.inc(i, "dG", 16)

            @blk.scalar
            def _(sc):
                sm.wait(sc, "dS", 64)
                i = sc.activation(scal(4), scal(0), AF.Sigmoid)
                sm.inc(i, "cACT")

            @blk.vector
            def _(v):
                sm.wait(v, "cACT")
                sm.wait(v, "dG")
                dve_chain(v, [
                    lambda v: v.tensor_tensor(scal(5), scal(4), scal(3), ALU.mult),
                    lambda v: v.reciprocal(scal(6), scal(5)),
                    lambda v: v.tensor_scalar(SC16[0:1, 0:1], scal(6), 127.0, None,
                                              ALU.mult),
                    lambda v: v.tensor_scalar(SC16[0:1, 1:2], scal(5), 1.0 / 127.0, None,
                                              ALU.mult),
                ])

            v_done = sm.n["sHP"]
            dve_sc = sm.n["cDVE"]

            @blk.tensor
            def _(t):
                sm.wait(t, "cDVE", dve_sc)
                sm.wait(t, "sHP", v_done)
                i = t.matmul(PS[:, 2048:2049], ONES[0:1, 0:128], SC16[0:1, 0:1],
                             start=True, stop=True)
                sm.inc(i, "cPE")
                i = t.matmul(PS[:, 2049:2050], ONES[0:1, 0:128], SC16[0:1, 1:2],
                             start=True, stop=True)
                sm.inc(i, "cPE")

            @blk.vector
            def _(v):
                sm.wait(v, "cPE")
                dve_chain(v, [
                    lambda v: v.tensor_copy(col(c_invsx), PS[:, 2048:2049]),
                    lambda v: v.tensor_copy(col(c_sx), PS[:, 2049:2050]),
                ])
                for cd in range(DC):
                    dve_chain(v, [
                        lambda v: v.tensor_scalar(X32[:, 0:S], A1[:, cd * S:(cd + 1) * S],
                                                  col(c_invsx), MAGIC, ALU.mult, ALU.add),
                        lambda v: v.tensor_scalar(WS[:, 0:min(S, 2048)], X32[:, 0:S],
                                                  MAGIC, 127.0, ALU.subtract, ALU.min),
                        lambda v: v.tensor_scalar(A4[:, cd * S:(cd + 1) * S],
                                                  WS[:, 0:min(S, 2048)],
                                                  -128.0, None, ALU.max),
                    ])

        with nc.Block() as blk:
            emit_taps(blk, [("xn", A1[:, 0:DC * S]), ("q", A2[:, 0:DC * S]),
                            ("k", A3[:, 0:DC * S]), ("v", A5[:, 0:NT * 1040]),
                            ("xq", A4[:, 0:DC * S]), ("gmax1", scal(3))])

        with nc.Block() as blk:
            sSQ0 = sm.n["sSQ"]
            sU0 = sm.n["sU"]
            sRD0 = sm.n["sRD"]

            @blk.scalar
            def _(sc):
                for n in range(2 * DC):
                    p, which = divmod(n, 2)
                    if n >= 2:
                        sm.wait(sc, "sU", sU0 + 2 * (n - 1))
                    src = A2 if which == 0 else A3
                    i = sc.activation(EX[:, which * S:which * S + S],
                                      src[:, p * S:(p + 1) * S], AF.Square)
                    sm.inc(i, "sSQ")

            def c0_region(idx):
                rb = 64 * (idx % 2)
                cb = 2048 * ((idx // 2) % 2)
                return rb, cb

            @blk.tensor
            def _(t):
                for n in range(2 * DC):
                    p, which = divmod(n, 2)
                    sm.wait(t, "sSQ", sSQ0 + n + 1)
                    for half in range(2):
                        idx = 2 * n + half
                        if idx >= 4:
                            sm.wait(t, "sRD", sRD0 + idx - 3)
                        rb, cb = c0_region(idx)
                        for c0, w in cchunks(S):
                            i = t.matmul(
                                PS[rb:rb + 64, cb + c0:cb + c0 + w],
                                ONES[64 * half:64 * half + 64, 0:64],
                                EX[64 * half:64 * half + 64,
                                   which * S + c0:which * S + c0 + w],
                                start=True, stop=True)
                        sm.inc(i, "sU")

            @blk.vector
            def _(v):
                for n in range(2 * DC):
                    p, which = divmod(n, 2)
                    for half in range(2):
                        idx = 2 * n + half
                        sm.wait(v, "sU", sU0 + idx + 1)
                        h = 2 * p + half
                        dstc = (80 if which == 0 else 96) + h
                        rb, cb = c0_region(idx)
                        i = v.tensor_reduce(COLS[0:1, dstc:dstc + 1],
                                            PS[rb:rb + 1, cb:cb + S], AX.X, ALU.max)
                        sm.inc(i, "sRD")

            @blk.vector
            def _(v):
                sm.wait(v, "sRD", sRD0 + 4 * DC)
                i = v.tensor_tensor(COLS[0:1, 112:128], COLS[0:1, 80:96],
                                    COLS[0:1, 96:112], ALU.mult)
                sm.inc(i, "cDVE")

            dve_c0 = sm.n["cDVE"]

            @blk.scalar
            def _(sc):
                sm.wait(sc, "cDVE", dve_c0)
                act_chain(sc, [
                    lambda sc: sc.activation(COLS[0:1, 80:96], COLS[0:1, 112:128],
                                             AF.Ln),
                    lambda sc: sc.activation(COLS[0:1, 96:112], COLS[0:1, 80:96],
                                             AF.Exp, scale=0.5),
                ])

            @blk.vector
            def _(v):
                sm.wait(v, "cACT")
                i = v.tensor_scalar(SC16[0:1, 2:18], COLS[0:1, 96:112],
                                    -1.0 / 32.0, None, ALU.mult)
                sm.inc(i, "cDVE")

            dve_c0b = sm.n["cDVE"]

            @blk.tensor
            def _(t):
                sm.wait(t, "cDVE", dve_c0b)
                i = t.matmul(PS[:, 2048:2064], ONES[0:1, 0:128], SC16[0:1, 2:18],
                             start=True, stop=True)
                sm.inc(i, "cPE")

            pe_c0 = sm.n["cPE"]

            @blk.vector
            def _(v):
                sm.wait(v, "cPE", pe_c0)
                dve_chain(v, [
                    lambda v: v.tensor_copy(COLS[:, c_bias:c_bias + 16], PS[:, 2048:2064]),
                    lambda v: v.tensor_tensor(
                        COLS[:, 104:112],
                        COLS[:, c_bias:c_bias + 16].rearrange("p (a b) -> p a b", b=2)[:, :, 0],
                        COLS[:, c_bias:c_bias + 16].rearrange("p (a b) -> p a b", b=2)[:, :, 1],
                        ALU.min),
                ])

            bias_dve = sm.n["cDVE"]

            sST0 = sm.n["sST"]
            sEXP0 = sm.n["sEXP"]
            sAV0 = sm.n["sAV"]
            sBC0 = sm.n["sBC"]
            sDR0 = sm.n["sDR"]
            sDN0 = sm.n["sDN"]
            sNM0 = sm.n["sNM"]
            dG0 = sm.n["dG"]

            def st_bank(half, c):
                return (2 * half + c % 2) * 512

            def av_bank(half, k):
                return 2048 + (2 * (k % 2) + half) * 512

            def bc_bank(half):
                return half * 1024

            @blk.tensor
            def _(t):
                sm.wait(t, "cDVE", bias_dve)
                def emit_st(t, k, p, s_, c):
                    sm.wait(t, "sEXP", sEXP0 + 2 * max(0, NT * k + c - 1))
                    for half in range(2):
                        i = t.matmul(
                            PS[:, st_bank(half, c):st_bank(half, c) + SLAB],
                            A3[64 * half:64 * half + 64,
                               p * S + c * 128:p * S + (c + 1) * 128],
                            A2[64 * half:64 * half + 64,
                               p * S + s_ * SLAB:p * S + (s_ + 1) * SLAB],
                            start=True, stop=True)
                    sm.inc(i, "sST")

                def emit_av(t, k, p, cp):
                    sm.wait(t, "sEXP", sEXP0 + 2 * (NT * k + cp + 1))
                    if cp == 0:
                        sm.wait(t, "sNM", sNM0 + max(0, 2 * k - 2))
                    for half in range(2):
                        hh = 2 * p + half
                        exb = ((cp % 2) * 2 + half) * SLAB
                        i = t.matmul(
                            PS[0:65, av_bank(half, k):av_bank(half, k) + SLAB],
                            A5[:, cp * 1040 + hh * 65:cp * 1040 + (hh + 1) * 65],
                            EX[:, exb:exb + SLAB],
                            start=(cp == 0), stop=(cp == NT - 1))
                    sm.inc(i, "sAV")

                def emit_bc(t, j, h):
                    sm.wait(t, "sDR", sDR0 + 2 * j + h + 1)
                    if j >= 2:
                        sm.wait(t, "sDN", sDN0 + 4 * (j - 2) + (1 if h == 0 else 3))
                    i = t.matmul(
                        PS[64:128, av_bank(h, j):av_bank(h, j) + SLAB],
                        ONES[64:65, 0:64],
                        WS[64:65, 1024 + h * 512:1024 + h * 512 + SLAB],
                        start=True, stop=True)
                    sm.inc(i, "sBC")

                deep = NT >= 8
                for k in range(KTOT):
                    p, s_ = divmod(k, NSLAB)
                    for c in range(NT):
                        emit_st(t, k, p, s_, c)
                        if c >= 2:
                            emit_av(t, k, p, c - 2)
                        if deep and k >= 1:
                            if c == 2:
                                emit_bc(t, k - 1, 0)
                            elif c == 3:
                                emit_bc(t, k - 1, 1)
                    for cp in (NT - 2, NT - 1):
                        if cp < 0:
                            continue
                        emit_av(t, k, p, cp)
                    if not deep and k >= 1:
                        emit_bc(t, k - 1, 0)
                        emit_bc(t, k - 1, 1)
                emit_bc(t, KTOT - 1, 0)
                emit_bc(t, KTOT - 1, 1)

            st_total = sm.n["sST"]

            @blk.scalar
            def _(sc):
                def emit_ln(sc, j, h):
                    sm.wait(sc, "sBC", sBC0 + 2 * j + h + 1)
                    xslab = X32[0:64, h * 1024:h * 1024 + SLAB]
                    i = sc.activation(xslab,
                                      PS[64:128, av_bank(h, j):av_bank(h, j) + SLAB],
                                      AF.Ln)
                    sm.inc(i, "sDN")

                def emit_rep(sc, j, h):
                    sm.wait(sc, "sDN")
                    if h == 1:
                        sm.wait(sc, "sNM", sNM0 + 2 * j + 1)
                    xslab = X32[0:64, h * 1024:h * 1024 + SLAB]
                    i = sc.activation(WS[0:64, 0:SLAB], xslab, AF.Exp, scale=-1.0)
                    sm.inc(i, "sDN")

                deep = NT >= 8
                for k in range(KTOT):
                    for c in range(NT):
                        sm.wait(sc, "sST", sST0 + NT * k + c + 1)
                        if c >= 2:
                            sm.wait(sc, "sAV", sAV0 + NT * k + c - 1)
                        pp = k // NSLAB
                        i = sc.activation(
                            EX[:, (c % 2) * 2 * SLAB:(c % 2) * 2 * SLAB + 2 * SLAB]
                            .rearrange("p (a r) -> p a r", a=2),
                            PS[:, 0:2048].rearrange("p (a q) -> p a q", a=2)
                            [:, :, (c % 2) * 512:(c % 2) * 512 + SLAB],
                            AF.Exp, bias=col(104 + pp),
                            scale=1.0 / 32.0)
                        sm.inc(i, "sEXP", 2)
                        if deep and k >= 1:
                            if c == 3:
                                emit_ln(sc, k - 1, 0)
                            elif c == 4:
                                emit_rep(sc, k - 1, 0)
                            elif c == 6:
                                emit_ln(sc, k - 1, 1)
                            elif c == 7:
                                emit_rep(sc, k - 1, 1)
                    if not deep and k >= 1:
                        for h in range(2):
                            emit_ln(sc, k - 1, h)
                            emit_rep(sc, k - 1, h)
                for h in range(2):
                    emit_ln(sc, KTOT - 1, h)
                    emit_rep(sc, KTOT - 1, h)

            @blk.vector
            def _(v):
                def emit_tt(v, j):
                    pj, sj = divmod(j, NSLAB)
                    sm.wait(v, "sDN", sDN0 + 4 * j + 2)
                    i = v.tensor_tensor(
                        A1[0:64, pj * S + sj * SLAB:pj * S + (sj + 1) * SLAB],
                        PS[0:64, av_bank(0, j):av_bank(0, j) + SLAB],
                        WS[0:64, 0:SLAB], ALU.mult)
                    sm.inc(i, "sNM")
                    sm.wait(v, "sDN", sDN0 + 4 * j + 4)
                    if j >= 1:
                        sm.wait(v, "dG", dG0 + 16 * j)
                    i = v.tensor_tensor(
                        WS[0:64, 2048 - SLAB:2048],
                        PS[0:64, av_bank(1, j):av_bank(1, j) + SLAB],
                        WS[0:64, 0:SLAB], ALU.mult)
                    sm.inc(i, "sNM")

                for k in range(KTOT):
                    if k >= 1:
                        emit_tt(v, k - 1)
                    for half in range(2):
                        sm.wait(v, "sAV", sAV0 + NT * (k + 1))
                        if k >= 1:
                            sm.wait(v, "sBC", sBC0 + 2 * (k - 1) + half + 1)
                        i = v.tensor_copy(
                            WS[64:65, 1024 + half * 512:1024 + half * 512 + SLAB],
                            PS[64:65, av_bank(half, k):av_bank(half, k) + SLAB])
                        sm.inc(i, "sDR")
                emit_tt(v, KTOT - 1)

            @blk.gpsimd
            def _(g):
                def emit_shift(g, j):
                    pj, sj = divmod(j, NSLAB)
                    sm.wait(g, "sNM", sNM0 + 2 * j + 2)
                    sm.wait(g, "dG")
                    i = g.dma_start(
                        out=A1[64:128, pj * S + sj * SLAB:pj * S + (sj + 1) * SLAB],
                        in_=WS[0:64, 2048 - SLAB:2048])
                    sm.inc(i, "dG", 16)

                for k in range(1, KTOT):
                    emit_shift(g, k - 1)
                emit_shift(g, KTOT - 1)
                sm.wait(g, "dG")

            emit_taps(blk, [("attn", A1[:, 0:DC * S])])

        with nc.Block() as blk:
            sWQ0 = sm.n["sWQ"]
            sWT0 = sm.n["sWT"]
            sG0 = sm.n["sG"]
            sU0 = sm.n["sU"]
            sSIL0 = sm.n["sSIL"]
            sHP0 = sm.n["sHP"]
            sT0 = sm.n["sT1"]
            dGa0 = sm.n["dGa"]
            dGb0 = sm.n["dGb"]

            def zslot(it, wsel):
                return 1024 + (wsel * 2 + it % 2) * 1024

            def emit_chain(v, it, only_wsel=None):
                for wsel, ccol in ((0, c_cg), (1, c_cu)):
                    if only_wsel is not None and wsel != only_wsel:
                        continue
                    wsem = "dGa" if wsel == 0 else "dGb"
                    sm.wait(v, wsem, (dGa0 if wsel == 0 else dGb0) + (it + 1) * 16)
                    xs = X32[:, wsel * 1024:wsel * 1024 + 1024]
                    if it >= 2:
                        sm.wait(v, "sT1", sT0 + 2 * (2 * (it - 2) + wsel + 1))
                    if it >= 2:
                        sm.wait(v, "sSIL", sSIL0 + NFQ * (it - 1))
                        sm.wait(v, "sHP", sHP0 + NFQ * (it - 1))
                    dve_chain(v, [
                        lambda v: v.tensor_reduce(col(c_tmp + 2), xs, AX.X,
                                                  ALU.max, apply_absolute_value=True),
                        lambda v: v.reciprocal(col(c_tmp + 3), col(c_tmp + 2)),
                        lambda v: v.tensor_scalar(col(c_tmp + 4), col(c_tmp + 3),
                                                  127.0, None, ALU.mult),
                        lambda v: v.tensor_scalar(col(ccol + it % 2), col(c_tmp + 2),
                                                  col(c_sx), 1.0 / 127.0,
                                                  ALU.mult, ALU.mult),
                        lambda v: v.tensor_scalar(PS[:, 2048:3072], xs,
                                                  col(c_tmp + 4), MAGIC,
                                                  ALU.mult, ALU.add),
                        lambda v: v.tensor_scalar(
                            EX[:, zslot(it, wsel):zslot(it, wsel) + 1024],
                            PS[:, 2048:3072], MAGIC, None, ALU.subtract),
                    ], last_sem="sWQ")

            @blk.vector
            def _(v):
                i = v.memset(COLS[:, c_hmax:c_hmax + 2], 0.0)
                sm.inc(i, "cDVE")
                gi = 0
                emit_chain(v, 0)
                for it in range(IT):
                    if it + 1 < IT and NFQ == 1:
                        emit_chain(v, it + 1)
                    for fq in range(NFQ):
                        if it + 1 < IT and NFQ > 1 and fq in (0, 1):
                            emit_chain(v, it + 1, only_wsel=fq)
                        m = gi
                        gi += 1
                        t0 = fq * FQ
                        sm.wait(v, "sU", sU0 + m + 1)
                        sm.wait(v, "sSIL", sSIL0 + m + 1)
                        gbu = 1024 + (m % 2) * 512
                        exs = (m % 2) * 512
                        dve_chain(v, [
                            lambda v: v.scalar_tensor_tensor(
                                hp_ap(it, t0, FQ), PS[:, gbu:gbu + FQ],
                                col(c_cu + it % 2), EX[:, exs:exs + FQ],
                                ALU.mult, ALU.mult),
                            lambda v: v.tensor_reduce(col(c_tmp + 5), hp_ap(it, t0, FQ),
                                                      AX.X, ALU.max,
                                                      apply_absolute_value=True),
                            lambda v: v.tensor_tensor(col(c_hmax + (m + 1) % 2),
                                                      col(c_hmax + m % 2),
                                                      col(c_tmp + 5), ALU.max),
                        ], last_sem="sHP")

            @blk.gpsimd
            def _(g):
                for it in range(IT):
                    for wsel, w_dram in ((0, wg_h), (1, wu_h)):
                        wsem = "dGa" if wsel == 0 else "dGb"
                        sm.wait(g, "sWQ", sWQ0 + 2 * it + wsel)
                        sm.wait(g, wsem)
                        i = g.dma_start(out=X32[:, wsel * 1024:wsel * 1024 + 1024],
                                        in_=w_dram[it * 128:(it + 1) * 128, :])
                        sm.inc(i, wsem, 16)

            def emit_transp(t, it):
                for wsel in range(2):
                    sm.wait(t, "sWQ", sWQ0 + 2 * it + wsel + 1)
                    zb = zslot(it, wsel)
                    for b in range(2):
                        nb = 2 * (2 * it + wsel) + b
                        if nb >= 2:
                            sm.wait(t, "sWT", sWT0 + nb - 1)
                        pbank = 3072 + (nb % 2) * 512
                        for kk in range(4):
                            cd = b * 4 + kk
                            i = t.matmul(
                                PS[:, pbank + kk * 128:pbank + (kk + 1) * 128],
                                EX[:, zb + cd * 128:zb + (cd + 1) * 128], IDE,
                                start=True, stop=True)
                        sm.inc(i, "sT1")

            @blk.tensor
            def _(t):
                gi = 0
                emit_transp(t, 0)
                for it in range(IT):
                    for fq in range(NFQ):
                        m = gi
                        gi += 1
                        t0 = fq * FQ
                        sm.wait(t, "sWT", sWT0 + 4 * (it + 1))
                        wtg = (2 * (it % 2) + 0) * 1024
                        wtu = (2 * (it % 2) + 1) * 1024
                        gbg = (m % 2) * 512
                        gbu = 1024 + (m % 2) * 512
                        if m >= 2:
                            sm.wait(t, "sSIL", sSIL0 + m - 1)
                        for cd in range(DC):
                            i = t.matmul(PS[:, gbg:gbg + FQ],
                                         WT[:, wtg + cd * 128:wtg + (cd + 1) * 128],
                                         A4[:, cd * S + t0:cd * S + t0 + FQ],
                                         start=(cd == 0), stop=(cd == DC - 1))
                        sm.inc(i, "sG")
                        if m >= 2:
                            sm.wait(t, "sHP", sHP0 + m - 1)
                        for cd in range(DC):
                            i = t.matmul(PS[:, gbu:gbu + FQ],
                                         WT[:, wtu + cd * 128:wtu + (cd + 1) * 128],
                                         A4[:, cd * S + t0:cd * S + t0 + FQ],
                                         start=(cd == 0), stop=(cd == DC - 1))
                        sm.inc(i, "sU")
                    if it + 1 < IT:
                        emit_transp(t, it + 1)

            def emit_evacs(sc, it):
                for wsel in range(2):
                    if it >= 2:
                        sm.wait(sc, "sU", sU0 + NFQ * (it - 1))
                    for b in range(2):
                        nb = 2 * (2 * it + wsel) + b
                        sm.wait(sc, "sT1", sT0 + nb + 1)
                        pbank = 3072 + (nb % 2) * 512
                        wtb = (2 * (it % 2) + wsel) * 1024
                        i = sc.copy(WT[:, wtb + b * 512:wtb + (b + 1) * 512],
                                    PS[:, pbank:pbank + 512])
                        sm.inc(i, "sWT")

            @blk.scalar
            def _(sc):
                gi = 0
                emit_evacs(sc, 0)
                for it in range(IT):
                    for fq in range(NFQ):
                        m = gi
                        gi += 1
                        sm.wait(sc, "sG", sG0 + m + 1)
                        if m >= 2:
                            sm.wait(sc, "sHP", sHP0 + m - 1)
                        gbg = (m % 2) * 512
                        exs = (m % 2) * 512
                        i = sc.activation(EX[:, exs:exs + FQ], PS[:, gbg:gbg + FQ],
                                          AF.Silu, scale=col(c_cg + it % 2))
                        sm.inc(i, "sSIL")
                        if fq == NFQ - 1 and it + 1 < IT:
                            emit_evacs(sc, it + 1)

            NGRAN = IT * NFQ

            @blk.gpsimd
            def _(g):
                sm.wait(g, "sHP", sHP0 + NGRAN)
                i = g.dma_start(out=scr.ap()[0:1, :].rearrange("a n -> n a"),
                                in_=col(c_hmax + NGRAN % 2))
                sm.inc(i, "dG", 16)
                sm.wait(g, "dG")
                i = g.dma_start(out=ROW[0:1, :], in_=scr.ap()[0:1, :])
                sm.inc(i, "dG", 16)

            @blk.vector
            def _(v):
                sm.wait(v, "dG")
                i = v.tensor_reduce(scal(2), ROW[0:1, :], AX.X, ALU.max)
                sm.inc(i, "cDVE")

            @blk.gpsimd
            def _(g):
                sm.wait(g, "cDVE")
                i = g.dma_start(out=cc2_in.ap()[0:1, :], in_=scal(2))
                sm.inc(i, "dG", 16)
                sm.wait(g, "dG")
                i = g.collective_compute(
                    "AllReduce", ALU.max,
                    replica_groups=[list(range(n_cores))],
                    ins=[cc2_in.ap().opt()], outs=[cc2_out.ap().opt()])
                sm.inc(i, "cc2")

        p3_wq = sm.n["sWQ"]
        with nc.Block() as blk:
            dWD0 = sm.n["dWD"]

            @blk.gpsimd
            def _(g):
                sm.wait(g, "sWQ", p3_wq)
                sm.wait(g, "dWD")
                i = g.dma_start(out=X32[:, 0:II], in_=wd_h[0:128, :])
                sm.inc(i, "dWD", 16)

            load_weight_T(blk, wo_h)
            wt_ready_o = sm.n["sWT"]
            sG0 = sm.n["sG"]
            sHP0 = sm.n["sHP"]
            sHQ0 = sm.n["sROPE"]
            sSP0 = sm.n["sSP"]
            sWQ0 = sm.n["sWQ"]
            sT0 = sm.n["sT1"]
            sSC0 = sm.n["sSC"]
            cell = {}

            batches = []
            bstart = 0
            while bstart < IT:
                batches.append((bstart, min(4, IT - bstart)))
                bstart += 4
            NB = len(batches)

            def wdq_dst(ci, th):
                if ci < 16:
                    return A4[:, ci * 1024 + th * 128:ci * 1024 + (th + 1) * 128]
                if ci < 20:
                    return A5[:, 6 * S + (ci - 16) * 1024 + th * 128:
                              6 * S + (ci - 16) * 1024 + (th + 1) * 128]
                return WS[:, (ci - 20) * 1024 + th * 128:
                          (ci - 20) * 1024 + (th + 1) * 128]

            @blk.gpsimd
            def _(g):
                sm.wait(g, "cc2")
                sm.wait(g, "dG")
                i = g.dma_start(out=scal(3), in_=cc2_out.ap()[0:1, :])
                sm.inc(i, "dG", 16)

            if "gmax2" in tap_h:
                @blk.gpsimd
                def _(g):
                    sm.wait(g, "dG")
                    i = g.dma_start(out=tap_h["gmax2"][0:1, :], in_=scal(3))
                    sm.inc(i, "dG", 16)

            FG = 256
            NFG = IT * max(1, S // FG)
            sQE0 = sm.n["sQE"]

            def emit_fq_pl(g, fi2):
                it2, g2 = divmod(fi2, max(1, S // FG))
                t0 = g2 * FG
                pq = PQ[:, (fi2 % 2) * 256:(fi2 % 2) * 256 + FG]
                if fi2 == 0:
                    sm.wait(g, "sSC", sSC0 + 1)
                if fi2 >= 2:
                    sm.wait(g, "sQE", sQE0 + fi2 - 1)
                i = g.tensor_scalar(pq, hp_ap(it2, t0, FG),
                                    col(c_invs2), MAGIC, ALU.mult, ALU.add)
                sm.inc(i, "cPL")
                sm.wait(g, "cPL")
                i = g.tensor_scalar(pq, pq, MAGIC + 127.0, None, ALU.min)
                sm.inc(i, "sROPE")

            def emit_fq_c(v, fi2):
                it2, g2 = divmod(fi2, max(1, S // FG))
                t0 = g2 * FG
                pq = PQ[:, (fi2 % 2) * 256:(fi2 % 2) * 256 + FG]
                sm.wait(v, "sROPE", sHQ0 + fi2 + 1)
                i = v.tensor_scalar(hp_ap(it2, t0, FG), pq,
                                    MAGIC, -128.0, ALU.subtract, ALU.max)
                sm.inc(i, "sQE")

            @blk.gpsimd
            def _(g):
                fi = 0
                for th in range(1, 8):
                    sm.wait(g, "sWQ", sWQ0 + th)
                    sm.wait(g, "dWD")
                    i = g.dma_start(out=X32[:, 0:II],
                                    in_=wd_h[th * 128:(th + 1) * 128, :])
                    sm.inc(i, "dWD", 16)
                    while fi < th * NFG // 8:
                        emit_fq_pl(g, fi)
                        fi += 1
                while fi < NFG:
                    emit_fq_pl(g, fi)
                    fi += 1

            @blk.scalar
            def _(sc):
                i = sc.activation(scal(4), scal(1), AF.Sigmoid)
                sm.inc(i, "cACT")

            act_sig = sm.n["cACT"]
            dg_g2 = sm.n["dG"]

            def emit_wd_transp(t, th):
                sm.wait(t, "sWQ", sWQ0 + th + 1)
                for bi, (c_lo, bn) in enumerate(batches):
                    nb = th * NB + bi
                    if nb >= 2:
                        sm.wait(t, "sWT", wt_ready_o + nb - 1)
                    pbank = 3072 + (nb % 2) * 512
                    for kk in range(bn):
                        ci = c_lo + kk
                        i = t.matmul(PS[:, pbank + kk * 128:pbank + (kk + 1) * 128],
                                     EX[:, 1024 + ci * 128:1024 + (ci + 1) * 128],
                                     IDE, start=True, stop=True)
                    sm.inc(i, "sT1")

            def emit_bc2(t):
                sm.wait(t, "sSP", sSP0 + 1)
                sm.wait(t, "sWQ", sWQ0 + 1)
                i = t.matmul(PS[:, 2048:2049], ONES[0:1, 0:128], SC16[0:1, 0:1],
                             start=True, stop=True)
                i = t.matmul(PS[:, 2049:2050], ONES[0:1, 0:128], SC16[0:1, 1:2],
                             start=True, stop=True)
                sm.inc(i, "cPE")
                cell["pe_p5"] = sm.n["cPE"]

            @blk.tensor
            def _(t):
                sm.wait(t, "sWT", wt_ready_o)
                nth = 0
                bc_done = False
                for tt in range(NT):
                    gb = (tt % 2) * 1024
                    if tt >= 2:
                        sm.wait(t, "sHP", sHP0 + tt - 1)
                    for cd in range(DC):
                        for c0, w in cchunks(D):
                            i = t.matmul(
                                PS[:, gb + c0:gb + c0 + w],
                                A1[:, cd * S + tt * 128:cd * S + (tt + 1) * 128],
                                WT[:, cd * 1024 + c0:cd * 1024 + c0 + w],
                                start=(cd == 0), stop=(cd == DC - 1))
                    sm.inc(i, "sG")
                    if tt == 1 and nth == 0:
                        emit_wd_transp(t, 0)
                        nth = 1
                    elif tt == 2 and not bc_done:
                        emit_bc2(t)
                        bc_done = True
                    elif tt >= 5 and tt % 2 == 1 and nth < 8 and nth == (tt - 3) // 2:
                        emit_wd_transp(t, nth)
                        nth += 1
                if not bc_done:
                    emit_bc2(t)
                while nth < 8:
                    emit_wd_transp(t, nth)
                    nth += 1

            def emit_wd_chain(v, th):
                sm.wait(v, "dWD", dWD0 + 16 * (th + 1))
                if th >= 1:
                    sm.wait(v, "sT1", sT0 + th * NB)
                ops = [
                    lambda v: v.tensor_reduce(col(c_rmax8 + th), X32[:, 0:II],
                                              AX.X, ALU.max,
                                              apply_absolute_value=True),
                    lambda v: v.reciprocal(col(c_tmp + 3), col(c_rmax8 + th)),
                    lambda v: v.tensor_scalar(col(c_tmp + 4), col(c_tmp + 3),
                                              127.0, None, ALU.mult),
                ]
                for q in range(3):
                    w = min(1024, II - q * 1024)
                    ops.append(lambda v, q=q, w=w: v.tensor_scalar(
                        PS[:, 2048:2048 + w], X32[:, q * 1024:q * 1024 + w],
                        col(c_tmp + 4), MAGIC, ALU.mult, ALU.add))
                    ops.append(lambda v, q=q, w=w: v.tensor_scalar(
                        EX[:, 1024 + q * 1024:1024 + q * 1024 + w],
                        PS[:, 2048:2048 + w], MAGIC, None, ALU.subtract))
                dve_chain(v, ops, last_sem="sWQ")

            def emit_scale_prep(v):
                sm.wait(v, "cACT", act_sig)
                sm.wait(v, "dG", dg_g2)
                dve_chain(v, [
                    lambda v: v.tensor_tensor(scal(5), scal(4), scal(3), ALU.mult),
                    lambda v: v.reciprocal(scal(6), scal(5)),
                    lambda v: v.tensor_scalar(SC16[0:1, 0:1], scal(6), 127.0, None,
                                              ALU.mult),
                    lambda v: v.tensor_scalar(SC16[0:1, 1:2], scal(5), 1.0 / 127.0,
                                              None, ALU.mult),
                ], last_sem="sSP")
                sm.wait(v, "cPE", cell["pe_p5"])
                dve_chain(v, [
                    lambda v: v.tensor_copy(col(c_invs2), PS[:, 2048:2049]),
                    lambda v: v.tensor_copy(col(c_s2), PS[:, 2049:2050]),
                ])
                i = v.tensor_copy(col(c_tmp + 5), col(c_invs2))
                sm.inc(i, "sSC")

            ao_ds0 = [sm.n["dSa"], sm.n["dSb"]]
            sAOE0 = sm.n["sAOE"]

            @blk.sync
            def _(sy):
                for tt in range(NT):
                    for half in range(2):
                        dsem = "dSa" if half == 0 else "dSb"
                        if half == 0:
                            sm.wait(sy, "sAOE", sAOE0 + tt + 1)
                        else:
                            sm.wait(sy, "sHP", sHP0 + tt + 1)
                        sm.wait(sy, dsem)
                        i = sy.dma_start(
                            out=ao_d.ap()[tt * 128:(tt + 1) * 128,
                                          half * 512:(half + 1) * 512],
                            in_=EX[:, 3840 + half * 512:3840 + (half + 1) * 512])
                        sm.inc(i, dsem, 16)

            @blk.vector
            def _(v):
                vth = 1
                cfi = 0
                sp_done = False
                emit_wd_chain(v, 0)
                for tt in range(NT):
                    gb = (tt % 2) * 1024
                    sm.wait(v, "sG", sG0 + tt + 1)
                    for half in range(2):
                        dsem = "dSa" if half == 0 else "dSb"
                        if tt >= 1:
                            sm.wait(v, dsem, ao_ds0[half] + tt * 16)
                        i = v.tensor_copy(
                            EX[:, 3840 + half * 512:3840 + (half + 1) * 512],
                            PS[:, gb + half * 512:gb + half * 512 + 512])
                        sm.inc(i, "sAOE" if half == 0 else "sHP")
                    if tt == 2 and not sp_done:
                        emit_scale_prep(v)
                        sp_done = True
                    elif tt >= 4 and tt % 2 == 0 and vth < 8 and vth == (tt - 2) // 2:
                        emit_wd_chain(v, vth)
                        while cfi < vth * NFG // 8:
                            emit_fq_c(v, cfi)
                            cfi += 1
                        vth += 1
                if not sp_done:
                    emit_scale_prep(v)
                while vth < 8:
                    emit_wd_chain(v, vth)
                    while cfi < vth * NFG // 8:
                        emit_fq_c(v, cfi)
                        cfi += 1
                    vth += 1
                while cfi < NFG:
                    emit_fq_c(v, cfi)
                    cfi += 1

            @blk.scalar
            def _(sc):
                sm.wait(sc, "sWT", wt_ready_o)
                for th in range(8):
                    for bi, (c_lo, bn) in enumerate(batches):
                        nb = th * NB + bi
                        sm.wait(sc, "sT1", sT0 + nb + 1)
                        pbank = 3072 + (nb % 2) * 512
                        for kk in range(bn):
                            ci = c_lo + kk
                            i = sc.copy(wdq_dst(ci, th),
                                        PS[:, pbank + kk * 128:pbank + (kk + 1) * 128])
                        sm.inc(i, "sWT")

        wd_done_wt = sm.n["sWT"]
        wd_done_wq = sm.n["sWQ"]
        fq_total = sm.n["sQE"]

        if "ao" in tap_h:
            with nc.Block() as blk:
                @blk.gpsimd
                def _(g):
                    sm.wait(g, "dSa")
                    sm.wait(g, "dSb")
                    for tt in range(NT):
                        sm.wait(g, "dG")
                        i = g.dma_start(out=tap_h["ao"][:, tt * D:(tt + 1) * D],
                                        in_=ao_d.ap()[tt * 128:(tt + 1) * 128, :])
                        sm.inc(i, "dG", 16)

        with nc.Block() as blk:
            @blk.vector
            def _(v):
                dve_chain(v, [
                    lambda v: v.tensor_scalar(COLS[:, c_rsd:c_rsd + 8],
                                              COLS[:, c_rmax8:c_rmax8 + 8],
                                              col(c_s2), 1.0 / 127.0,
                                              ALU.mult, ALU.mult),
                ])

            rsd_dve = sm.n["cDVE"]

            @blk.gpsimd
            def _(g):
                sm.wait(g, "cDVE", rsd_dve)
                for th in range(8):
                    sm.wait(g, "dG")
                    i = g.dma_start(out=scr.ap()[th:th + 1, :].rearrange("a n -> n a"),
                                    in_=col(c_rsd + th))
                    sm.inc(i, "dG", 16)
                sm.wait(g, "dG")
                i = g.dma_start(out=BVR[0:1, :],
                                in_=scr.ap().rearrange("a n -> (a n)").rearrange(
                                    "(x y) -> x y", x=1))
                sm.inc(i, "dG", 16)

            dg_rsf = sm.n["dG"]

            @blk.tensor
            def _(t):
                sm.wait(t, "dG", dg_rsf)
                for c0, w in cchunks(D):
                    i = t.matmul(PS[:, 2048 + c0:2048 + c0 + w], ONES[0:1, 0:128],
                                 BVR[0:1, c0:c0 + w], start=True, stop=True)
                sm.inc(i, "cPE")

            pe_rsf = sm.n["cPE"]

            @blk.vector
            def _(v):
                sm.wait(v, "cPE", pe_rsf)
                i = v.tensor_copy(EX[:, 0:1024], PS[:, 2048:3072])
                sm.inc(i, "cDVE")

            rsf_dve = sm.n["cDVE"]

            sG0 = sm.n["sG"]
            sNM0 = sm.n["sNM"]

            def wdq_rhs(ci):
                if ci < 16:
                    return A4[:, ci * 1024:(ci + 1) * 1024]
                if ci < 20:
                    return A5[:, 6 * S + (ci - 16) * 1024:6 * S + (ci - 15) * 1024]
                return WS[:, (ci - 20) * 1024:(ci - 19) * 1024]

            @blk.tensor
            def _(t):
                sm.wait(t, "cDVE", rsf_dve)
                sm.wait(t, "sWT", wd_done_wt)
                sm.wait(t, "sQE", fq_total)
                for tt in range(NT):
                    gb = (tt % 2) * 1024
                    if tt >= 2:
                        sm.wait(t, "sNM", sNM0 + tt - 1)
                    for ci in range(IT):
                        rhs = wdq_rhs(ci)
                        for c0, w in cchunks(D):
                            i = t.matmul(
                                PS[:, gb + c0:gb + c0 + w],
                                hp_ap(ci, tt * 128, 128),
                                rhs[:, c0:c0 + w],
                                start=(ci == 0), stop=(ci == IT - 1))
                    sm.inc(i, "sG")

            dsa0 = sm.n["dSa"]
            dsb0 = sm.n["dSb"]
            dga0 = sm.n["dGa"]
            dgb0 = sm.n["dGb"]

            @blk.gpsimd
            def _(g):
                sm.wait(g, "dSa")
                sm.wait(g, "dSb")
                for tt in range(NT):
                    gsem = "dGa" if tt % 2 == 0 else "dGb"
                    sm.wait(g, gsem)
                    i = g.dma_start(out=A1[:, tt * 1024:(tt + 1) * 1024],
                                    in_=ao_d.ap()[tt * 128:(tt + 1) * 128, :])
                    sm.inc(i, gsem, 16)

            @blk.vector
            def _(v):
                sm.wait(v, "cDVE", rsf_dve)
                for tt in range(NT):
                    dsem = "dSa" if tt % 2 == 0 else "dSb"
                    dbase = (dsa0 if tt % 2 == 0 else dsb0) + (tt // 2 + 1) * 16
                    gb = (tt % 2) * 1024
                    sm.wait(v, "sG", sG0 + tt + 1)
                    if tt >= 1:
                        sm.wait(v, "sNM", sNM0 + tt)
                    sm.wait(v, dsem, dbase)
                    gsem = "dGa" if tt % 2 == 0 else "dGb"
                    gbase = (dga0 if tt % 2 == 0 else dgb0) + (tt // 2 + 1) * 16
                    sm.wait(v, gsem, gbase)
                    xs = X32[:, (tt % 2) * 1024:(tt % 2) * 1024 + 1024]
                    dve_chain(v, [
                        lambda v: v.tensor_tensor(PS[:, 2048:3072],
                                                  PS[:, gb:gb + 1024],
                                                  EX[:, 0:1024], ALU.mult),
                        lambda v: v.tensor_tensor(PS[:, 2048:3072],
                                                  PS[:, 2048:3072],
                                                  A1[:, tt * 1024:(tt + 1) * 1024],
                                                  ALU.add),
                        lambda v: v.tensor_tensor(xs, PS[:, 2048:3072], xs, ALU.add),
                    ], last_sem="sNM")

            @blk.sync
            def _(sy):
                for tt in range(min(NT, 2)):
                    dsem = "dSa" if tt % 2 == 0 else "dSb"
                    sm.wait(sy, dsem)
                    i = sy.dma_start(out=X32[:, (tt % 2) * 1024:(tt % 2) * 1024 + 1024],
                                     in_=x_h[tt * 128:(tt + 1) * 128, :])
                    sm.inc(i, dsem, 16)
                for tt in range(NT):
                    osem = "dOa" if tt % 2 == 0 else "dOb"
                    sm.wait(sy, "sNM", sNM0 + tt + 1)
                    sm.wait(sy, osem)
                    i = sy.dma_start(out=out_h[tt * 128:(tt + 1) * 128, :],
                                     in_=X32[:, (tt % 2) * 1024:(tt % 2) * 1024 + 1024])
                    sm.inc(i, osem, 16)
                    if tt + 2 < NT:
                        dsem = "dSa" if tt % 2 == 0 else "dSb"
                        sm.wait(sy, osem)
                        sm.wait(sy, dsem)
                        i = sy.dma_start(out=X32[:, (tt % 2) * 1024:(tt % 2) * 1024 + 1024],
                                         in_=x_h[(tt + 2) * 128:(tt + 3) * 128, :])
                        sm.inc(i, dsem, 16)

            if "hp" in tap_h:
                @blk.gpsimd
                def _(g):
                    sm.wait(g, "sQE", fq_total)
                    for it2 in range(IT):
                        sm.wait(g, "dG")
                        i = g.dma_start(out=tap_h["hp"][:, it2 * S:(it2 + 1) * S],
                                        in_=hp_ap(it2, 0, S))
                        sm.inc(i, "dG", 16)

            @blk.gpsimd
            def _(g):
                sm.wait(g, "dOa")
                sm.wait(g, "dOb")
                sm.wait(g, "dG")

    return nc, tap_h


def kernel(L0_attn_x, L0_attn_Wq, L0_attn_bq, L0_attn_Wk, L0_attn_bk,
           L0_attn_Wv, L0_attn_bv, L0_attn_Wo, L0_ffn_clip_x, L0_ffn_clip_down,
           L0_ffn_w_gate, L0_ffn_w_up, L0_ffn_w_down):
    from concourse.bass_utils import run_bass_kernel_spmd
    x = np.ascontiguousarray(np.asarray(L0_attn_x, dtype=np.float32))
    B, S_, D_ = x.shape
    assert D_ == D and B == 8
    shared = dict(
        Wq=np.ascontiguousarray(L0_attn_Wq, dtype=np.float32),
        bq=np.ascontiguousarray(L0_attn_bq, dtype=np.float32),
        Wk=np.ascontiguousarray(L0_attn_Wk, dtype=np.float32),
        bk=np.ascontiguousarray(L0_attn_bk, dtype=np.float32),
        Wv=np.ascontiguousarray(L0_attn_Wv, dtype=np.float32),
        bv=np.ascontiguousarray(L0_attn_bv, dtype=np.float32),
        Wo=np.ascontiguousarray(L0_attn_Wo, dtype=np.float32),
        clip_x=np.ascontiguousarray(L0_ffn_clip_x, dtype=np.float32),
        clip_down=np.ascontiguousarray(L0_ffn_clip_down, dtype=np.float32),
        w_gate=np.ascontiguousarray(L0_ffn_w_gate, dtype=np.float32),
        w_up=np.ascontiguousarray(L0_ffn_w_up, dtype=np.float32),
        w_down=np.ascontiguousarray(L0_ffn_w_down, dtype=np.float32),
    )
    nc, _ = build(S=S_, n_cores=B)
    in_maps = [dict(shared, x=np.ascontiguousarray(x[b])) for b in range(B)]
    res = run_bass_kernel_spmd(nc, in_maps, core_ids=list(range(B)))
    out = np.stack([np.asarray(res.results[b]["out"], dtype=np.float32)
                    for b in range(B)], axis=0)
    return out


# version 18
# speedup vs baseline: 1.4510x; 1.2382x over previous
import numpy as np
import contextlib
import concourse.bass as bass
import concourse.mybir as mybir

F32 = mybir.dt.float32
F16 = mybir.dt.float16
AF = mybir.ActivationFunctionType
ALU = mybir.AluOpType
AX = mybir.AxisListType

MAGIC = 1.5 * 2.0**23
THETA = 10000.0
TWO_PI = 2.0 * np.pi
C1 = 6.28125
C2 = float(np.float32(np.frombuffer(np.uint32(
    np.frombuffer(np.float32(TWO_PI - C1).tobytes(), dtype=np.uint32)[0]
    & 0xFFFFF000).tobytes(), dtype=np.float32)[0]))
C3 = float(TWO_PI - C1 - C2)
INV2PI = float(np.float32(1.0 / TWO_PI))
LNT32 = float(np.log(THETA) / 32.0)
PI = float(np.pi)

D = 1024
H = 16
HD = 64
DC = 8
II = 2816
IT = 22
EPS = 1e-5

c_jcol, c_hi, c_jm, c_pair, c_sgn, c_f = 0, 1, 2, 3, 4, 5
c_bq, c_bk = 6, 14
c_negmu, c_rstd = 22, 24
c_xmax = 26
c_invsx, c_sx = 28, 29
c_hmax = 30
c_invs2, c_s2 = 32, 33
c_cg, c_cu = 34, 38
c_rsd = 42
c_rmax8 = 56
c_bias = 48
c_sc = 64
c_eps = 79
c_st6 = 80
c_mv = 92
c_tmp = 96
c_mq2 = 104

SEMS = ["dS", "dSa", "dSb", "dG", "dGa", "dGb", "dOa", "dOb",
        "cPE", "cDVE", "cACT", "cPL", "cc1", "cc2",
        "sST", "sEXP", "sAV", "sBC", "sDR", "sDN", "sNM",
        "sG", "sU", "sSIL", "sHP", "sWQ", "sWT", "sQE", "sT1", "sPERM",
        "sROPE", "sSQ", "sRD", "dWD", "sSP", "sAOE", "sSC"]


class Sems:
    def __init__(self, nc, st):
        self.h, self.n = {}, {}
        for name in SEMS:
            self.h[name] = st.enter_context(nc.semaphore(name))
            self.n[name] = 0

    def inc(self, instr, s, k=1):
        instr.then_inc(self.h[s], k)
        self.n[s] += k
        return self.n[s]

    def wait(self, eng, s, n=None):
        v = self.n[s] if n is None else n
        if v > 0:
            eng.wait_ge(self.h[s], v)


def build(S=2048, n_cores=8, taps=()):
    NT = S // 128
    SLAB = min(512, S)
    NSLAB = S // SLAB
    QG = min(1024, S)
    NQG = S // QG
    FQ = min(512, S)
    NFQ = S // FQ
    KTOT = DC * NSLAB

    nc = bass.Bass("TRN2", target_bir_lowering=False, debug=False)

    x_h = nc.dram_tensor("x", [S, D], F32, kind="ExternalInput").ap()
    wq_h = nc.dram_tensor("Wq", [D, D], F32, kind="ExternalInput").ap()
    bq_h = nc.dram_tensor("bq", [D], F32, kind="ExternalInput").ap()
    wk_h = nc.dram_tensor("Wk", [D, D], F32, kind="ExternalInput").ap()
    bk_h = nc.dram_tensor("bk", [D], F32, kind="ExternalInput").ap()
    wv_h = nc.dram_tensor("Wv", [D, D], F32, kind="ExternalInput").ap()
    bv_h = nc.dram_tensor("bv", [D], F32, kind="ExternalInput").ap()
    wo_h = nc.dram_tensor("Wo", [D, D], F32, kind="ExternalInput").ap()
    clipx_h = nc.dram_tensor("clip_x", [1], F32, kind="ExternalInput").ap()
    clipd_h = nc.dram_tensor("clip_down", [1], F32, kind="ExternalInput").ap()
    wg_h = nc.dram_tensor("w_gate", [II, D], F32, kind="ExternalInput").ap()
    wu_h = nc.dram_tensor("w_up", [II, D], F32, kind="ExternalInput").ap()
    wd_h = nc.dram_tensor("w_down", [D, II], F32, kind="ExternalInput").ap()
    out_h = nc.dram_tensor("out", [S, D], F32, kind="ExternalOutput").ap()

    tap_h = {}
    def mktap(name, shape, dt):
        if name in taps:
            tap_h[name] = nc.dram_tensor("tap_" + name, list(shape), dt,
                                         kind="ExternalOutput").ap()

    mktap("xn", [128, DC * S], F16)
    mktap("q", [128, DC * S], F16)
    mktap("k", [128, DC * S], F16)
    mktap("v", [128, NT * 1040], F16)
    mktap("xq", [128, DC * S], F16)
    mktap("attn", [128, DC * S], F16)
    mktap("hp", [128, IT * S], F16)
    mktap("ao", [128, DC * S], F16)
    mktap("gmax1", [1, 1], F32)
    mktap("gmax2", [1, 1], F32)

    scr = nc.dram_tensor("scr", [8, 128], F32)
    cc1_in = nc.dram_tensor("cc1_in", [1, 1], F32)
    cc1_out = nc.dram_tensor("cc1_out", [1, 1], F32)
    cc2_in = nc.dram_tensor("cc2_in", [1, 1], F32)
    cc2_out = nc.dram_tensor("cc2_out", [1, 1], F32)
    ao_d = nc.dram_tensor("ao_d", [S, D], F16)

    A_COLS = max(DC * S, 16 * 1024)

    st = contextlib.ExitStack()
    with st:
        A1 = st.enter_context(nc.sbuf_tensor("A1", [128, A_COLS], F16))
        A4 = st.enter_context(nc.sbuf_tensor("A4", [128, A_COLS], F16))
        A5 = st.enter_context(nc.sbuf_tensor("A5", [128, max(NT * 1040, 6 * S + 4096)], F16))
        WT = st.enter_context(nc.sbuf_tensor("WTa", [128, 8192], F16))
        WS = st.enter_context(nc.sbuf_tensor("WSa", [128, 2048], F16))
        X32 = st.enter_context(nc.sbuf_tensor("X32", [128, 2816], F32))
        EX = st.enter_context(nc.sbuf_tensor("EXa", [128, max(5120, 2 * S)], F16))
        COLS = st.enter_context(nc.sbuf_tensor("COLS", [128, 128], F32))
        CN = st.enter_context(nc.sbuf_tensor("CN", [128, 384], F16))
        ROW = st.enter_context(nc.sbuf_tensor("ROWa", [1, 128], F32))
        SC16 = st.enter_context(nc.sbuf_tensor("SC16", [1, 32], F16))
        BVR = st.enter_context(nc.sbuf_tensor("BVR", [1, D], F16))
        PQ = st.enter_context(nc.sbuf_tensor("PQ", [128, 512], F32))
        MCOL = st.enter_context(nc.sbuf_tensor("MCOL", [128, 1], F32))
        PS = st.enter_context(nc.psum_tensor("PS", [128, 4096], F32))
        sm = Sems(nc, st)

        IDE = CN[:, 0:128]
        PERM = CN[:, 128:256]
        ONES = CN[:, 256:384]

        def col(c):
            return COLS[:, c:c + 1]

        def scal(c):
            return COLS[0:1, c_sc + c:c_sc + c + 1]

        def cchunks(W):
            return [(c0, min(512, W - c0)) for c0 in range(0, W, 512)]

        def dve_chain(v, ops, last_sem=None):
            for j, fn in enumerate(ops):
                i = fn(v)
                if last_sem is not None and j == len(ops) - 1:
                    sm.inc(i, last_sem)
                else:
                    sm.inc(i, "cDVE")
                    sm.wait(v, "cDVE")

        def act_chain(sc, ops, last_sem=None):
            for j, fn in enumerate(ops):
                i = fn(sc)
                if last_sem is not None and j == len(ops) - 1:
                    sm.inc(i, last_sem)
                else:
                    sm.inc(i, "cACT")
                    sm.wait(sc, "cACT")

        def pl_chain(g, ops):
            for fn in ops:
                i = fn(g)
                sm.inc(i, "cPL")
                sm.wait(g, "cPL")

        def hp_ap(it, t0, w):
            if it < 8:
                return A2[:, it * S + t0:it * S + t0 + w]
            if it < 16:
                return A3[:, (it - 8) * S + t0:(it - 8) * S + t0 + w]
            return A5[:, (it - 16) * S + t0:(it - 16) * S + t0 + w]

        def emit_taps(blk, items):
            live = [(n, ap) for n, ap in items if n in tap_h]
            if not live:
                return
            @blk.gpsimd
            def _(g):
                sm.wait(g, "cDVE")
                sm.wait(g, "cACT")
                for n, ap in live:
                    sm.wait(g, "dG")
                    i = g.dma_start(out=tap_h[n][:, :] if tap_h[n].shape[0] > 1
                                    else tap_h[n][0:1, :], in_=ap)
                    sm.inc(i, "dG", 16)
                sm.wait(g, "dG")

        p0 = contextlib.ExitStack()
        A2 = A3 = None
        with p0:
            F32T = p0.enter_context(nc.sbuf_tensor("F32T", [128, 2 * S], F32))
            SA = X32[:, 0:S]
            SB = F32T[:, 0:S]
            SCc = F32T[:, S:2 * S]

            with nc.Block() as blk:
                @blk.sync
                def _(sy):
                    i = sy.dma_start(out=COLS[:, c_bq:c_bq + 8],
                                     in_=bq_h.rearrange("(a p) -> p a", p=128),
                                     allow_slow_non_contiguous=True)
                    sm.inc(i, "dS", 16)
                    i = sy.dma_start(out=COLS[:, c_bk:c_bk + 8],
                                     in_=bk_h.rearrange("(a p) -> p a", p=128),
                                     allow_slow_non_contiguous=True)
                    sm.inc(i, "dS", 16)
                    i = sy.dma_start(out=COLS[0:1, c_sc + 0:c_sc + 1],
                                     in_=clipx_h.rearrange("(a b) -> a b", a=1))
                    sm.inc(i, "dS", 16)
                    i = sy.dma_start(out=COLS[0:1, c_sc + 1:c_sc + 2],
                                     in_=clipd_h.rearrange("(a b) -> a b", a=1))
                    sm.inc(i, "dS", 16)

                @blk.gpsimd
                def _(g):
                    g.memset(col(c_eps), EPS)
                    g.iota(col(c_jcol), [[0, 1]], channel_multiplier=1,
                           allow_small_or_imprecise_dtypes=True)
                    i = g.iota(SA, [[1, S]], channel_multiplier=0,
                               allow_small_or_imprecise_dtypes=True)
                    sm.inc(i, "cPL")
                    g.memset(CN[:, 0:256], 0.0)
                    i = g.memset(ONES, 1.0)
                    sm.inc(i, "cPL")
                    sm.wait(g, "cPL")
                    pl_chain(g, [
                        lambda g: g.affine_select(out=IDE, in_=IDE,
                                                  compare_op=ALU.not_equal,
                                                  fill=1.0, base=0,
                                                  pattern=[[-1, 128]],
                                                  channel_multiplier=1),
                        lambda g: g.memset(WS[:, 0:256], 0.0),
                        lambda g: g.tensor_copy(WS[:, 1:128], IDE[:, 0:127]),
                        lambda g: g.tensor_copy(WS[:, 128:255], IDE[:, 1:128]),
                        lambda g: g.memset(
                            A5[:, 0:NT * 1040].rearrange("p (t h c) -> p t h c",
                                                         t=NT, h=H)[:, :, :, 64:65],
                            1.0),
                        lambda g: g.memset(MCOL[:, 0:1], -MAGIC),
                    ])

                @blk.vector
                def _(v):
                    sm.wait(v, "cPL", 1)
                    dve_chain(v, [
                        lambda v: v.tensor_scalar(col(c_hi), col(c_jcol), 1.0 / 64.0,
                                                  -0.5 + 0.25 / 64.0, ALU.mult, ALU.add),
                        lambda v: v.tensor_scalar(col(c_hi), col(c_hi), MAGIC, MAGIC,
                                                  ALU.add, ALU.subtract),
                        lambda v: v.scalar_tensor_tensor(col(c_jm), col(c_hi), -64.0,
                                                         col(c_jcol), ALU.mult, ALU.add),
                        lambda v: v.tensor_scalar(col(c_pair), col(c_jm), 0.5, -0.25,
                                                  ALU.mult, ALU.add),
                        lambda v: v.tensor_scalar(col(c_pair), col(c_pair), MAGIC, MAGIC,
                                                  ALU.add, ALU.subtract),
                        lambda v: v.scalar_tensor_tensor(col(c_sgn), col(c_pair), 2.0,
                                                         col(c_jm), ALU.mult, ALU.subtract),
                        lambda v: v.tensor_scalar(col(c_sgn), col(c_sgn), 2.0, 1.0,
                                                  ALU.mult, ALU.add),
                    ])

                @blk.scalar
                def _(sc):
                    sm.wait(sc, "cDVE")
                    i = sc.activation(col(c_f), col(c_pair), AF.Exp, scale=-LNT32)
                    sm.inc(i, "cACT")

                @blk.vector
                def _(v):
                    sm.wait(v, "cACT")
                    dve_chain(v, [
                        lambda v: v.tensor_scalar(SB, SA, col(c_f), None, ALU.mult),
                        lambda v: v.tensor_scalar(SA, SB, INV2PI, None, ALU.mult),
                        lambda v: v.tensor_scalar(SCc, SA, MAGIC, MAGIC, ALU.add, ALU.subtract),
                        lambda v: v.scalar_tensor_tensor(SA, SCc, -C1, SB, ALU.mult, ALU.add),
                        lambda v: v.scalar_tensor_tensor(SB, SCc, -C2, SA, ALU.mult, ALU.add),
                        lambda v: v.scalar_tensor_tensor(SCc, SCc, -C3, SB, ALU.mult, ALU.add)
                        if False else
                        v.scalar_tensor_tensor(SB, SCc, -C3, SB, ALU.mult, ALU.add)
                        if False else
                        v.scalar_tensor_tensor(SA, SCc, -C3, SB, ALU.mult, ALU.add),
                        lambda v: v.tensor_scalar(SA, SA, PI, -PI, ALU.min, ALU.max)
                        if False else
                        v.tensor_scalar(SB, SA, PI, -PI, ALU.min, ALU.max),
                        lambda v: v.tensor_scalar(SA, SB, 1.0, None, ALU.mult),
                        lambda v: v.tensor_scalar(SB, SA, col(c_sgn), None, ALU.mult),
                    ])

                @blk.scalar
                def _(sc):
                    sm.wait(sc, "cDVE")
                    i = sc.activation(EX[:, 0:S], SB, AF.Sin)
                    sm.inc(i, "cACT")

                @blk.vector
                def _(v):
                    sm.wait(v, "cACT")
                    dve_chain(v, [
                        lambda v: v.tensor_scalar(SB, SA, PI / 2.0, None, ALU.add),
                        lambda v: v.tensor_scalar(SCc, SB, PI, None, ALU.is_gt),
                        lambda v: v.scalar_tensor_tensor(SA, SCc, -TWO_PI, SB,
                                                         ALU.mult, ALU.add),
                        lambda v: v.tensor_scalar(SCc, SA, PI, -PI, ALU.min, ALU.max),
                        lambda v: v.tensor_scalar(SA, SCc, 1.0, None, ALU.mult),
                    ])

                @blk.scalar
                def _(sc):
                    sm.wait(sc, "cDVE")
                    i = sc.activation(EX[:, S:2 * S], SA, AF.Sin)
                    sm.inc(i, "cACT")

                @blk.vector
                def _(v):
                    sm.wait(v, "cACT")
                    sm.wait(v, "cPL", 6)
                    dve_chain(v, [
                        lambda v: v.tensor_scalar(col(c_tmp), col(c_sgn), 0.5, 0.5,
                                                  ALU.mult, ALU.add),
                        lambda v: v.tensor_tensor(WS[:, 256:384], WS[:, 0:128],
                                                  WS[:, 128:256], ALU.subtract),
                        lambda v: v.tensor_scalar(WS[:, 256:384], WS[:, 256:384],
                                                  col(c_tmp), None, ALU.mult),
                        lambda v: v.tensor_tensor(PERM, WS[:, 256:384], WS[:, 128:256],
                                                  ALU.add),
                        lambda v: v.memset(COLS[:, c_xmax:c_xmax + 2], 0.0),
                    ])

                act_pre = sm.n["cACT"]
                dve_pre = sm.n["cDVE"]
                VPT, APT = 7, 2

                @blk.sync
                def _(sy):
                    for tt in range(NT):
                        dsem = "dSa" if tt % 2 == 0 else "dSb"
                        if tt < 2:
                            sm.wait(sy, "cDVE", dve_pre)
                            sm.wait(sy, "cACT", act_pre)
                        else:
                            sm.wait(sy, "cDVE", dve_pre + (tt - 2) * VPT + 5)
                        sm.wait(sy, dsem)
                        i = sy.dma_start(out=X32[:, (tt % 2) * 1024:(tt % 2) * 1024 + 1024],
                                         in_=x_h[tt * 128:(tt + 1) * 128, :])
                        sm.inc(i, dsem, 16)

                @blk.vector
                def _(v):
                    for tt in range(NT):
                        dsem = "dSa" if tt % 2 == 0 else "dSb"
                        xs = X32[:, (tt % 2) * 1024:(tt % 2) * 1024 + 1024]
                        ws = WS[:, (tt % 2) * 1024:(tt % 2) * 1024 + 1024]
                        sm.wait(v, dsem, (tt // 2 + 1) * 16)
                        dve_chain(v, [
                            lambda v: v.bn_stats(COLS[:, c_st6:c_st6 + 6], xs[:, 0:512]),
                            lambda v: v.bn_stats(COLS[:, c_st6 + 6:c_st6 + 12],
                                                 xs[:, 512:1024]),
                            lambda v: v.bn_aggr(COLS[:, c_mv:c_mv + 2],
                                                COLS[:, c_st6:c_st6 + 12]),
                            lambda v: v.tensor_scalar(col(c_negmu + tt % 2),
                                                      col(c_mv), -1.0, None, ALU.mult),
                        ])
                        sm.wait(v, "cACT", act_pre + tt * APT + 2)
                        if tt >= 2:
                            sm.wait(v, "sG", tt - 1)
                        dve_chain(v, [
                            lambda v: v.tensor_scalar(ws, xs, col(c_negmu + tt % 2),
                                                      col(c_rstd + tt % 2),
                                                      ALU.add, ALU.mult),
                            lambda v: v.tensor_reduce(col(c_tmp), ws, AX.X, ALU.max,
                                                      apply_absolute_value=True),
                            lambda v: v.tensor_tensor(col(c_xmax + (tt + 1) % 2),
                                                      col(c_xmax + tt % 2),
                                                      col(c_tmp), ALU.max),
                        ])

                @blk.scalar
                def _(sc):
                    for tt in range(NT):
                        sm.wait(sc, "cDVE", dve_pre + tt * VPT + 4)
                        act_chain(sc, [
                            lambda sc: sc.activation(col(c_tmp + 1), col(c_mv + 1),
                                                     AF.Ln, bias=col(c_eps)),
                            lambda sc: sc.activation(col(c_rstd + tt % 2),
                                                     col(c_tmp + 1), AF.Exp, scale=-0.5),
                        ])
                        sm.wait(sc, "sG", tt + 1)
                        bank = (tt % 2) * 2048
                        i = sc.copy(
                            A1[:, 0:DC * S].rearrange("p (c s) -> p c s", c=DC)[:, :, tt * 128:(tt + 1) * 128],
                            PS[:, bank:bank + 1024].rearrange("p (c b) -> p c b", c=DC))
                        sm.inc(i, "sWT")

                @blk.tensor
                def _(t):
                    sm.wait(t, "cPL", 3)
                    for tt in range(NT):
                        ws = WS[:, (tt % 2) * 1024:(tt % 2) * 1024 + 1024]
                        sm.wait(t, "cDVE", dve_pre + tt * VPT + 5)
                        if tt >= 2:
                            sm.wait(t, "sWT", tt - 1)
                        bank = (tt % 2) * 2048
                        for cd in range(DC):
                            i = t.matmul(PS[:, bank + cd * 128:bank + (cd + 1) * 128],
                                         ws[:, cd * 128:(cd + 1) * 128], IDE,
                                         start=True, stop=True)
                        sm.inc(i, "sG")

                @blk.gpsimd
                def _(g):
                    sm.wait(g, "cDVE", dve_pre + NT * VPT)
                    i = g.dma_start(out=scr.ap()[0:1, :].rearrange("a n -> n a"),
                                    in_=col(c_xmax + NT % 2))
                    sm.inc(i, "dG", 16)
                    sm.wait(g, "dG")
                    i = g.dma_start(out=ROW[0:1, :], in_=scr.ap()[0:1, :])
                    sm.inc(i, "dG", 16)

                @blk.vector
                def _(v):
                    sm.wait(v, "dG")
                    i = v.tensor_reduce(scal(2), ROW[0:1, :], AX.X, ALU.max)
                    sm.inc(i, "cDVE")

                @blk.gpsimd
                def _(g):
                    sm.wait(g, "cDVE")
                    i = g.dma_start(out=cc1_in.ap()[0:1, :], in_=scal(2))
                    sm.inc(i, "dG", 16)
                    sm.wait(g, "dG")
                    i = g.collective_compute(
                        "AllReduce", ALU.max,
                        replica_groups=[list(range(n_cores))],
                        ins=[cc1_in.ap().opt()], outs=[cc1_out.ap().opt()])
                    sm.inc(i, "cc1")

        A2 = st.enter_context(nc.sbuf_tensor("A2", [128, max(DC * S, 8 * S)], F16))
        A3 = st.enter_context(nc.sbuf_tensor("A3", [128, max(DC * S, 8 * S)], F16))

        def load_weight_T(blk, w_dram, wt_free_sem=None, wt_free_val=0):
            pe0 = sm.n["sU"]
            ev0 = sm.n["sWT"]

            @blk.gpsimd
            def _(g):
                for cj in range(8):
                    wsem = "dGa" if cj % 2 == 0 else "dGb"
                    sm.wait(g, "sU", pe0 + max(0, (cj - 1) * 2))
                    sm.wait(g, wsem)
                    i = g.dma_start(out=WS[:, (cj % 2) * 1024:(cj % 2) * 1024 + 1024],
                                    in_=w_dram[cj * 128:(cj + 1) * 128, :])
                    sm.inc(i, wsem, 16)

            dg_a = sm.n["dGa"] - 4 * 16
            dg_b = sm.n["dGb"] - 4 * 16

            @blk.tensor
            def _(t):
                for cj in range(8):
                    wsem = "dGa" if cj % 2 == 0 else "dGb"
                    base = (dg_a if cj % 2 == 0 else dg_b) + (cj // 2 + 1) * 16
                    sm.wait(t, wsem, base)
                    ws = WS[:, (cj % 2) * 1024:(cj % 2) * 1024 + 1024]
                    for b in range(2):
                        nb = 2 * cj + b
                        if nb >= 2:
                            sm.wait(t, "sWT", ev0 + nb - 1)
                        pbank = 3072 + (nb % 2) * 512
                        for kk in range(4):
                            cd = b * 4 + kk
                            i = t.matmul(PS[:, pbank + kk * 128:pbank + (kk + 1) * 128],
                                         ws[:, cd * 128:(cd + 1) * 128], IDE,
                                         start=True, stop=True)
                        sm.inc(i, "sU")

            @blk.scalar
            def _(sc):
                if wt_free_sem is not None:
                    sm.wait(sc, wt_free_sem, wt_free_val)
                for cj in range(8):
                    for b in range(2):
                        nb = 2 * cj + b
                        sm.wait(sc, "sU", pe0 + nb + 1)
                        pbank = 3072 + (nb % 2) * 512
                        i = sc.copy(
                            WT[:, 0:8192].rearrange("p (c b) -> p c b", c=DC)
                            [:, b * 4:(b + 1) * 4, cj * 128:(cj + 1) * 128],
                            PS[:, pbank:pbank + 512].rearrange("p (c b) -> p c b", c=4))
                        sm.inc(i, "sWT")

        with nc.Block() as blk:
            def qk_proj(dst, bias_c):
                NG = 8 * NQG
                sST0 = sm.n["sST"]
                sQE0 = sm.n["sQE"]
                sT10 = sm.n["sT1"]
                sPERM0 = sm.n["sPERM"]
                sROPE0 = sm.n["sROPE"]
                wt_ready = sm.n["sWT"]

                @blk.tensor
                def _(t):
                    sm.wait(t, "sWT", wt_ready)
                    for g in range(NG + 1):
                        if g < NG:
                            cj, qg = divmod(g, NQG)
                            gb = (g % 2) * 1024
                            if g >= 2:
                                sm.wait(t, "sQE", sQE0 + g - 1)
                            for cd in range(DC):
                                for c0, w in cchunks(QG):
                                    i = t.matmul(
                                        PS[:, gb + c0:gb + c0 + w],
                                        WT[:, cd * 1024 + cj * 128:cd * 1024 + (cj + 1) * 128],
                                        A1[:, cd * S + qg * QG + c0:cd * S + qg * QG + c0 + w],
                                        start=(cd == 0), stop=(cd == DC - 1))
                            sm.inc(i, "sST")
                        if g >= 1:
                            gp = g - 1
                            sm.wait(t, "sT1", sT10 + gp + 1)
                            if gp >= 1:
                                sm.wait(t, "sROPE", sROPE0 + gp)
                            rs = A4[:, (gp % 2) * 1024:(gp % 2) * 1024 + QG]
                            for c0, w in cchunks(QG):
                                i = t.matmul(PS[:, 2048 + c0:2048 + c0 + w],
                                             PERM, rs[:, c0:c0 + w],
                                             start=True, stop=True)
                            sm.inc(i, "sPERM")

                sSQ0 = sm.n["sSQ"]

                @blk.scalar
                def _(sc):
                    sm.wait(sc, "dS", 64)
                    for g in range(NG):
                        cj = g // NQG
                        gb = (g % 2) * 1024
                        sm.wait(sc, "sST", sST0 + g + 1)
                        sm.wait(sc, "sT1", sT10 + max(0, g - 1))
                        sm.wait(sc, "sSQ", sSQ0 + max(0, g - 1))
                        i = sc.activation(X32[:, gb:gb + QG], PS[:, gb:gb + QG],
                                          AF.Identity, bias=col(bias_c + cj))
                        sm.inc(i, "sQE")

                @blk.vector
                def _(v):
                    for g in range(NG):
                        cj, qg = divmod(g, NQG)
                        t0 = qg * QG
                        gb = (g % 2) * 1024
                        xr = X32[:, gb:gb + QG]
                        rs = A4[:, gb:gb + QG]
                        sm.wait(v, "sQE", sQE0 + g + 1)
                        if g >= 2:
                            sm.wait(v, "sPERM", sPERM0 + g - 1)
                        i = v.tensor_tensor(rs, xr, EX[:, t0:t0 + QG], ALU.mult)
                        sm.inc(i, "sT1")
                        sm.wait(v, "sPERM", sPERM0 + g + 1)
                        i = v.tensor_tensor(rs, xr, EX[:, S + t0:S + t0 + QG], ALU.mult)
                        sm.inc(i, "sSQ")
                        sm.wait(v, "sSQ")
                        i = v.tensor_tensor(dst[:, cj * S + t0:cj * S + t0 + QG],
                                            rs, PS[:, 2048:2048 + QG], ALU.add)
                        sm.inc(i, "sROPE")

            load_weight_T(blk, wq_h)
            qk_proj(A2, c_bq)
            load_weight_T(blk, wk_h, wt_free_sem="sST", wt_free_val=sm.n["sST"])
            qk_proj(A3, c_bk)
            load_weight_T(blk, wv_h, wt_free_sem="sST", wt_free_val=sm.n["sST"])

            @blk.gpsimd
            def _(g):
                sm.wait(g, "dGa")
                i = g.dma_start(out=BVR[0:1, :], in_=bv_h.rearrange("(a b) -> a b", a=1))
                sm.inc(i, "dGa", 16)

            dg_bv = sm.n["dGa"]
            rope_done = sm.n["sROPE"]
            wtv_ready = sm.n["sWT"]
            sG0 = sm.n["sG"]
            sU0v = sm.n["sHP"]

            @blk.tensor
            def _(t):
                sm.wait(t, "dGa", dg_bv)
                sm.wait(t, "sROPE", rope_done)
                sm.wait(t, "sWT", wtv_ready)
                for c0, w in cchunks(D):
                    i = t.matmul(PS[:, 2048 + c0:2048 + c0 + w], ONES[0:1, 0:128],
                                 BVR[0:1, c0:c0 + w], start=True, stop=True)
                sm.inc(i, "sG")
                for tt in range(NT):
                    gb = (tt % 2) * 1024
                    if tt >= 2:
                        sm.wait(t, "sHP", sU0v + tt - 1)
                    for cd in range(DC):
                        for c0, w in cchunks(D):
                            i = t.matmul(
                                PS[:, gb + c0:gb + c0 + w],
                                A1[:, cd * S + tt * 128:cd * S + (tt + 1) * 128],
                                WT[:, cd * 1024 + c0:cd * 1024 + c0 + w],
                                start=(cd == 0), stop=(cd == DC - 1))
                    sm.inc(i, "sG")

            @blk.vector
            def _(v):
                sm.wait(v, "sG", sG0 + 1)
                i = v.tensor_copy(WS[:, 0:1024], PS[:, 2048:3072])
                sm.inc(i, "cDVE")
                sm.wait(v, "cDVE")
                for tt in range(NT):
                    gb = (tt % 2) * 1024
                    sm.wait(v, "sG", sG0 + tt + 2)
                    i = v.tensor_tensor(
                        A5[:, tt * 1040:(tt + 1) * 1040].rearrange(
                            "p (h c) -> p h c", h=H)[:, :, 0:64],
                        PS[:, gb:gb + 1024].rearrange("p (h c) -> p h c", h=H),
                        WS[:, 0:1024].rearrange("p (h c) -> p h c", h=H),
                        ALU.add)
                    sm.inc(i, "sHP")

            @blk.gpsimd
            def _(g):
                sm.wait(g, "cc1")
                i = g.dma_start(out=scal(3), in_=cc1_out.ap()[0:1, :])
                sm.inc(i, "dG", 16)

            @blk.scalar
            def _(sc):
                sm.wait(sc, "dS", 64)
                i = sc.activation(scal(4), scal(0), AF.Sigmoid)
                sm.inc(i, "cACT")

            @blk.vector
            def _(v):
                sm.wait(v, "cACT")
                sm.wait(v, "dG")
                dve_chain(v, [
                    lambda v: v.tensor_tensor(scal(5), scal(4), scal(3), ALU.mult),
                    lambda v: v.reciprocal(scal(6), scal(5)),
                    lambda v: v.tensor_scalar(SC16[0:1, 0:1], scal(6), 127.0, None,
                                              ALU.mult),
                    lambda v: v.tensor_scalar(SC16[0:1, 1:2], scal(5), 1.0 / 127.0, None,
                                              ALU.mult),
                ])

            v_done = sm.n["sHP"]
            dve_sc = sm.n["cDVE"]

            @blk.tensor
            def _(t):
                sm.wait(t, "cDVE", dve_sc)
                sm.wait(t, "sHP", v_done)
                i = t.matmul(PS[:, 2048:2049], ONES[0:1, 0:128], SC16[0:1, 0:1],
                             start=True, stop=True)
                sm.inc(i, "cPE")
                i = t.matmul(PS[:, 2049:2050], ONES[0:1, 0:128], SC16[0:1, 1:2],
                             start=True, stop=True)
                sm.inc(i, "cPE")

            @blk.vector
            def _(v):
                sm.wait(v, "cPE")
                dve_chain(v, [
                    lambda v: v.tensor_copy(col(c_invsx), PS[:, 2048:2049]),
                    lambda v: v.tensor_copy(col(c_sx), PS[:, 2049:2050]),
                ])
                for cd in range(DC):
                    dve_chain(v, [
                        lambda v: v.tensor_scalar(X32[:, 0:S], A1[:, cd * S:(cd + 1) * S],
                                                  col(c_invsx), MAGIC, ALU.mult, ALU.add),
                        lambda v: v.tensor_scalar(WS[:, 0:min(S, 2048)], X32[:, 0:S],
                                                  MAGIC, 127.0, ALU.subtract, ALU.min),
                        lambda v: v.tensor_scalar(A4[:, cd * S:(cd + 1) * S],
                                                  WS[:, 0:min(S, 2048)],
                                                  -128.0, None, ALU.max),
                    ])

        with nc.Block() as blk:
            emit_taps(blk, [("xn", A1[:, 0:DC * S]), ("q", A2[:, 0:DC * S]),
                            ("k", A3[:, 0:DC * S]), ("v", A5[:, 0:NT * 1040]),
                            ("xq", A4[:, 0:DC * S]), ("gmax1", scal(3))])

        with nc.Block() as blk:
            sSQ0 = sm.n["sSQ"]
            sU0 = sm.n["sU"]
            sRD0 = sm.n["sRD"]

            @blk.scalar
            def _(sc):
                for n in range(2 * DC):
                    p, which = divmod(n, 2)
                    if n >= 2:
                        sm.wait(sc, "sU", sU0 + 2 * (n - 1))
                    src = A2 if which == 0 else A3
                    i = sc.activation(EX[:, which * S:which * S + S],
                                      src[:, p * S:(p + 1) * S], AF.Square)
                    sm.inc(i, "sSQ")

            def c0_region(idx):
                rb = 64 * (idx % 2)
                cb = 2048 * ((idx // 2) % 2)
                return rb, cb

            @blk.tensor
            def _(t):
                for n in range(2 * DC):
                    p, which = divmod(n, 2)
                    sm.wait(t, "sSQ", sSQ0 + n + 1)
                    for half in range(2):
                        idx = 2 * n + half
                        if idx >= 4:
                            sm.wait(t, "sRD", sRD0 + idx - 3)
                        rb, cb = c0_region(idx)
                        for c0, w in cchunks(S):
                            i = t.matmul(
                                PS[rb:rb + 64, cb + c0:cb + c0 + w],
                                ONES[64 * half:64 * half + 64, 0:64],
                                EX[64 * half:64 * half + 64,
                                   which * S + c0:which * S + c0 + w],
                                start=True, stop=True)
                        sm.inc(i, "sU")

            @blk.vector
            def _(v):
                for n in range(2 * DC):
                    p, which = divmod(n, 2)
                    for half in range(2):
                        idx = 2 * n + half
                        sm.wait(v, "sU", sU0 + idx + 1)
                        h = 2 * p + half
                        dstc = (80 if which == 0 else 96) + h
                        rb, cb = c0_region(idx)
                        i = v.tensor_reduce(COLS[0:1, dstc:dstc + 1],
                                            PS[rb:rb + 1, cb:cb + S], AX.X, ALU.max)
                        sm.inc(i, "sRD")

            @blk.vector
            def _(v):
                sm.wait(v, "sRD", sRD0 + 4 * DC)
                i = v.tensor_tensor(COLS[0:1, 112:128], COLS[0:1, 80:96],
                                    COLS[0:1, 96:112], ALU.mult)
                sm.inc(i, "cDVE")

            dve_c0 = sm.n["cDVE"]

            @blk.scalar
            def _(sc):
                sm.wait(sc, "cDVE", dve_c0)
                act_chain(sc, [
                    lambda sc: sc.activation(COLS[0:1, 80:96], COLS[0:1, 112:128],
                                             AF.Ln),
                    lambda sc: sc.activation(COLS[0:1, 96:112], COLS[0:1, 80:96],
                                             AF.Exp, scale=0.5),
                ])

            @blk.vector
            def _(v):
                sm.wait(v, "cACT")
                i = v.tensor_scalar(SC16[0:1, 2:18], COLS[0:1, 96:112],
                                    -1.0 / 32.0, None, ALU.mult)
                sm.inc(i, "cDVE")

            dve_c0b = sm.n["cDVE"]

            @blk.tensor
            def _(t):
                sm.wait(t, "cDVE", dve_c0b)
                i = t.matmul(PS[:, 2048:2064], ONES[0:1, 0:128], SC16[0:1, 2:18],
                             start=True, stop=True)
                sm.inc(i, "cPE")

            pe_c0 = sm.n["cPE"]

            @blk.vector
            def _(v):
                sm.wait(v, "cPE", pe_c0)
                dve_chain(v, [
                    lambda v: v.tensor_copy(COLS[:, c_bias:c_bias + 16], PS[:, 2048:2064]),
                    lambda v: v.tensor_tensor(
                        COLS[:, 104:112],
                        COLS[:, c_bias:c_bias + 16].rearrange("p (a b) -> p a b", b=2)[:, :, 0],
                        COLS[:, c_bias:c_bias + 16].rearrange("p (a b) -> p a b", b=2)[:, :, 1],
                        ALU.min),
                ])

            bias_dve = sm.n["cDVE"]

            sST0 = sm.n["sST"]
            sEXP0 = sm.n["sEXP"]
            sAV0 = sm.n["sAV"]
            sBC0 = sm.n["sBC"]
            sDR0 = sm.n["sDR"]
            sDN0 = sm.n["sDN"]
            sNM0 = sm.n["sNM"]
            dG0 = sm.n["dG"]

            def st_bank(half, c):
                return (2 * half + c % 2) * 512

            def av_bank(half, k):
                return 2048 + (2 * (k % 2) + half) * 512

            def bc_bank(half):
                return half * 1024

            @blk.tensor
            def _(t):
                sm.wait(t, "cDVE", bias_dve)
                def emit_st(t, k, p, s_, c):
                    sm.wait(t, "sEXP", sEXP0 + 2 * max(0, NT * k + c - 1))
                    for half in range(2):
                        i = t.matmul(
                            PS[:, st_bank(half, c):st_bank(half, c) + SLAB],
                            A3[64 * half:64 * half + 64,
                               p * S + c * 128:p * S + (c + 1) * 128],
                            A2[64 * half:64 * half + 64,
                               p * S + s_ * SLAB:p * S + (s_ + 1) * SLAB],
                            start=True, stop=True)
                    sm.inc(i, "sST")

                def emit_av(t, k, p, cp):
                    sm.wait(t, "sEXP", sEXP0 + 2 * (NT * k + cp + 1))
                    if cp == 0:
                        sm.wait(t, "sNM", sNM0 + max(0, 2 * k - 2))
                    for half in range(2):
                        hh = 2 * p + half
                        exb = ((cp % 2) * 2 + half) * SLAB
                        i = t.matmul(
                            PS[0:65, av_bank(half, k):av_bank(half, k) + SLAB],
                            A5[:, cp * 1040 + hh * 65:cp * 1040 + (hh + 1) * 65],
                            EX[:, exb:exb + SLAB],
                            start=(cp == 0), stop=(cp == NT - 1))
                    sm.inc(i, "sAV")

                def emit_bc(t, j, h):
                    sm.wait(t, "sDR", sDR0 + 2 * j + h + 1)
                    if j >= 2:
                        sm.wait(t, "sDN", sDN0 + 4 * (j - 2) + (1 if h == 0 else 3))
                    i = t.matmul(
                        PS[64:128, av_bank(h, j):av_bank(h, j) + SLAB],
                        ONES[64:65, 0:64],
                        WS[64:65, 1024 + h * 512:1024 + h * 512 + SLAB],
                        start=True, stop=True)
                    sm.inc(i, "sBC")

                deep = NT >= 8
                for k in range(KTOT):
                    p, s_ = divmod(k, NSLAB)
                    for c in range(NT):
                        emit_st(t, k, p, s_, c)
                        if c >= 2:
                            emit_av(t, k, p, c - 2)
                        if deep and k >= 1:
                            if c == 2:
                                emit_bc(t, k - 1, 0)
                            elif c == 3:
                                emit_bc(t, k - 1, 1)
                    for cp in (NT - 2, NT - 1):
                        if cp < 0:
                            continue
                        emit_av(t, k, p, cp)
                    if not deep and k >= 1:
                        emit_bc(t, k - 1, 0)
                        emit_bc(t, k - 1, 1)
                emit_bc(t, KTOT - 1, 0)
                emit_bc(t, KTOT - 1, 1)

            st_total = sm.n["sST"]

            @blk.scalar
            def _(sc):
                def emit_ln(sc, j, h):
                    sm.wait(sc, "sBC", sBC0 + 2 * j + h + 1)
                    xslab = X32[0:64, h * 1024:h * 1024 + SLAB]
                    i = sc.activation(xslab,
                                      PS[64:128, av_bank(h, j):av_bank(h, j) + SLAB],
                                      AF.Ln)
                    sm.inc(i, "sDN")

                def emit_rep(sc, j, h):
                    sm.wait(sc, "sDN")
                    if h == 1:
                        sm.wait(sc, "sNM", sNM0 + 2 * j + 1)
                    xslab = X32[0:64, h * 1024:h * 1024 + SLAB]
                    i = sc.activation(WS[0:64, 0:SLAB], xslab, AF.Exp, scale=-1.0)
                    sm.inc(i, "sDN")

                deep = NT >= 8
                for k in range(KTOT):
                    for c in range(NT):
                        sm.wait(sc, "sST", sST0 + NT * k + c + 1)
                        if c >= 2:
                            sm.wait(sc, "sAV", sAV0 + NT * k + c - 1)
                        pp = k // NSLAB
                        i = sc.activation(
                            EX[:, (c % 2) * 2 * SLAB:(c % 2) * 2 * SLAB + 2 * SLAB]
                            .rearrange("p (a r) -> p a r", a=2),
                            PS[:, 0:2048].rearrange("p (a q) -> p a q", a=2)
                            [:, :, (c % 2) * 512:(c % 2) * 512 + SLAB],
                            AF.Exp, bias=col(104 + pp),
                            scale=1.0 / 32.0)
                        sm.inc(i, "sEXP", 2)
                        if deep and k >= 1:
                            if c == 3:
                                emit_ln(sc, k - 1, 0)
                            elif c == 4:
                                emit_rep(sc, k - 1, 0)
                            elif c == 6:
                                emit_ln(sc, k - 1, 1)
                            elif c == 7:
                                emit_rep(sc, k - 1, 1)
                    if not deep and k >= 1:
                        for h in range(2):
                            emit_ln(sc, k - 1, h)
                            emit_rep(sc, k - 1, h)
                for h in range(2):
                    emit_ln(sc, KTOT - 1, h)
                    emit_rep(sc, KTOT - 1, h)

            @blk.vector
            def _(v):
                def emit_tt(v, j):
                    pj, sj = divmod(j, NSLAB)
                    sm.wait(v, "sDN", sDN0 + 4 * j + 2)
                    i = v.tensor_tensor(
                        A1[0:64, pj * S + sj * SLAB:pj * S + (sj + 1) * SLAB],
                        PS[0:64, av_bank(0, j):av_bank(0, j) + SLAB],
                        WS[0:64, 0:SLAB], ALU.mult)
                    sm.inc(i, "sNM")
                    sm.wait(v, "sDN", sDN0 + 4 * j + 4)
                    if j >= 1:
                        sm.wait(v, "dG", dG0 + 16 * j)
                    i = v.tensor_tensor(
                        WS[0:64, 2048 - SLAB:2048],
                        PS[0:64, av_bank(1, j):av_bank(1, j) + SLAB],
                        WS[0:64, 0:SLAB], ALU.mult)
                    sm.inc(i, "sNM")

                for k in range(KTOT):
                    if k >= 1:
                        emit_tt(v, k - 1)
                    for half in range(2):
                        sm.wait(v, "sAV", sAV0 + NT * (k + 1))
                        if k >= 1:
                            sm.wait(v, "sBC", sBC0 + 2 * (k - 1) + half + 1)
                        i = v.tensor_copy(
                            WS[64:65, 1024 + half * 512:1024 + half * 512 + SLAB],
                            PS[64:65, av_bank(half, k):av_bank(half, k) + SLAB])
                        sm.inc(i, "sDR")
                emit_tt(v, KTOT - 1)

            @blk.gpsimd
            def _(g):
                def emit_shift(g, j):
                    pj, sj = divmod(j, NSLAB)
                    sm.wait(g, "sNM", sNM0 + 2 * j + 2)
                    sm.wait(g, "dG")
                    i = g.dma_start(
                        out=A1[64:128, pj * S + sj * SLAB:pj * S + (sj + 1) * SLAB],
                        in_=WS[0:64, 2048 - SLAB:2048])
                    sm.inc(i, "dG", 16)

                for k in range(1, KTOT):
                    emit_shift(g, k - 1)
                emit_shift(g, KTOT - 1)
                sm.wait(g, "dG")

            emit_taps(blk, [("attn", A1[:, 0:DC * S])])

        with nc.Block() as blk:
            sWQ0 = sm.n["sWQ"]
            sWT0 = sm.n["sWT"]
            sG0 = sm.n["sG"]
            sU0 = sm.n["sU"]
            sSIL0 = sm.n["sSIL"]
            sHP0 = sm.n["sHP"]
            sT0 = sm.n["sT1"]
            dGa0 = sm.n["dGa"]
            dGb0 = sm.n["dGb"]

            def zslot(it, wsel):
                return 1024 + (wsel * 2 + it % 2) * 1024

            def emit_chain(v, it, only_wsel=None):
                for wsel, ccol in ((0, c_cg), (1, c_cu)):
                    if only_wsel is not None and wsel != only_wsel:
                        continue
                    wsem = "dGa" if wsel == 0 else "dGb"
                    sm.wait(v, wsem, (dGa0 if wsel == 0 else dGb0) + (it + 1) * 16)
                    xs = X32[:, wsel * 1024:wsel * 1024 + 1024]
                    if it >= 2:
                        sm.wait(v, "sT1", sT0 + 2 * (2 * (it - 2) + wsel + 1))
                    if it >= 2:
                        sm.wait(v, "sSIL", sSIL0 + NFQ * (it - 1))
                        sm.wait(v, "sHP", sHP0 + NFQ * (it - 1))
                    dve_chain(v, [
                        lambda v: v.tensor_reduce(col(c_tmp + 2), xs, AX.X,
                                                  ALU.max, apply_absolute_value=True),
                        lambda v: v.reciprocal(col(c_tmp + 3), col(c_tmp + 2)),
                        lambda v: v.tensor_scalar(col(c_tmp + 4), col(c_tmp + 3),
                                                  127.0, None, ALU.mult),
                        lambda v: v.tensor_scalar(col(ccol + it % 2), col(c_tmp + 2),
                                                  col(c_sx), 1.0 / 127.0,
                                                  ALU.mult, ALU.mult),
                        lambda v: v.tensor_scalar(PS[:, 2048:3072], xs,
                                                  col(c_tmp + 4), MAGIC,
                                                  ALU.mult, ALU.add),
                        lambda v: v.tensor_scalar(
                            EX[:, zslot(it, wsel):zslot(it, wsel) + 1024],
                            PS[:, 2048:3072], MAGIC, None, ALU.subtract),
                    ], last_sem="sWQ")

            @blk.vector
            def _(v):
                i = v.memset(COLS[:, c_hmax:c_hmax + 2], 0.0)
                sm.inc(i, "cDVE")
                gi = 0
                emit_chain(v, 0)
                for it in range(IT):
                    if it + 1 < IT and NFQ == 1:
                        emit_chain(v, it + 1)
                    for fq in range(NFQ):
                        if it + 1 < IT and NFQ > 1 and fq in (0, 1):
                            emit_chain(v, it + 1, only_wsel=fq)
                        m = gi
                        gi += 1
                        t0 = fq * FQ
                        sm.wait(v, "sU", sU0 + m + 1)
                        sm.wait(v, "sSIL", sSIL0 + m + 1)
                        gbu = 1024 + (m % 2) * 512
                        exs = (m % 2) * 512
                        dve_chain(v, [
                            lambda v: v.scalar_tensor_tensor(
                                hp_ap(it, t0, FQ), PS[:, gbu:gbu + FQ],
                                col(c_cu + it % 2), EX[:, exs:exs + FQ],
                                ALU.mult, ALU.mult),
                            lambda v: v.tensor_reduce(col(c_tmp + 5), hp_ap(it, t0, FQ),
                                                      AX.X, ALU.max,
                                                      apply_absolute_value=True),
                            lambda v: v.tensor_tensor(col(c_hmax + (m + 1) % 2),
                                                      col(c_hmax + m % 2),
                                                      col(c_tmp + 5), ALU.max),
                        ], last_sem="sHP")

            @blk.gpsimd
            def _(g):
                for it in range(IT):
                    for wsel, w_dram in ((0, wg_h), (1, wu_h)):
                        wsem = "dGa" if wsel == 0 else "dGb"
                        sm.wait(g, "sWQ", sWQ0 + 2 * it + wsel)
                        sm.wait(g, wsem)
                        i = g.dma_start(out=X32[:, wsel * 1024:wsel * 1024 + 1024],
                                        in_=w_dram[it * 128:(it + 1) * 128, :])
                        sm.inc(i, wsem, 16)

            def emit_transp(t, it):
                for wsel in range(2):
                    sm.wait(t, "sWQ", sWQ0 + 2 * it + wsel + 1)
                    zb = zslot(it, wsel)
                    for b in range(2):
                        nb = 2 * (2 * it + wsel) + b
                        if nb >= 2:
                            sm.wait(t, "sWT", sWT0 + nb - 1)
                        pbank = 3072 + (nb % 2) * 512
                        for kk in range(4):
                            cd = b * 4 + kk
                            i = t.matmul(
                                PS[:, pbank + kk * 128:pbank + (kk + 1) * 128],
                                EX[:, zb + cd * 128:zb + (cd + 1) * 128], IDE,
                                start=True, stop=True)
                        sm.inc(i, "sT1")

            @blk.tensor
            def _(t):
                gi = 0
                emit_transp(t, 0)
                for it in range(IT):
                    for fq in range(NFQ):
                        m = gi
                        gi += 1
                        t0 = fq * FQ
                        sm.wait(t, "sWT", sWT0 + 4 * (it + 1))
                        wtg = (2 * (it % 2) + 0) * 1024
                        wtu = (2 * (it % 2) + 1) * 1024
                        gbg = (m % 2) * 512
                        gbu = 1024 + (m % 2) * 512
                        if m >= 2:
                            sm.wait(t, "sSIL", sSIL0 + m - 1)
                        for cd in range(DC):
                            i = t.matmul(PS[:, gbg:gbg + FQ],
                                         WT[:, wtg + cd * 128:wtg + (cd + 1) * 128],
                                         A4[:, cd * S + t0:cd * S + t0 + FQ],
                                         start=(cd == 0), stop=(cd == DC - 1))
                        sm.inc(i, "sG")
                        if m >= 2:
                            sm.wait(t, "sHP", sHP0 + m - 1)
                        for cd in range(DC):
                            i = t.matmul(PS[:, gbu:gbu + FQ],
                                         WT[:, wtu + cd * 128:wtu + (cd + 1) * 128],
                                         A4[:, cd * S + t0:cd * S + t0 + FQ],
                                         start=(cd == 0), stop=(cd == DC - 1))
                        sm.inc(i, "sU")
                    if it + 1 < IT:
                        emit_transp(t, it + 1)

            def emit_evacs(sc, it):
                for wsel in range(2):
                    if it >= 2:
                        sm.wait(sc, "sU", sU0 + NFQ * (it - 1))
                    for b in range(2):
                        nb = 2 * (2 * it + wsel) + b
                        sm.wait(sc, "sT1", sT0 + nb + 1)
                        pbank = 3072 + (nb % 2) * 512
                        wtb = (2 * (it % 2) + wsel) * 1024
                        i = sc.copy(WT[:, wtb + b * 512:wtb + (b + 1) * 512],
                                    PS[:, pbank:pbank + 512])
                        sm.inc(i, "sWT")

            @blk.scalar
            def _(sc):
                gi = 0
                emit_evacs(sc, 0)
                for it in range(IT):
                    for fq in range(NFQ):
                        m = gi
                        gi += 1
                        sm.wait(sc, "sG", sG0 + m + 1)
                        if m >= 2:
                            sm.wait(sc, "sHP", sHP0 + m - 1)
                        gbg = (m % 2) * 512
                        exs = (m % 2) * 512
                        i = sc.activation(EX[:, exs:exs + FQ], PS[:, gbg:gbg + FQ],
                                          AF.Silu, scale=col(c_cg + it % 2))
                        sm.inc(i, "sSIL")
                        if fq == NFQ - 1 and it + 1 < IT:
                            emit_evacs(sc, it + 1)

            NGRAN = IT * NFQ

            @blk.gpsimd
            def _(g):
                sm.wait(g, "sHP", sHP0 + NGRAN)
                i = g.dma_start(out=scr.ap()[0:1, :].rearrange("a n -> n a"),
                                in_=col(c_hmax + NGRAN % 2))
                sm.inc(i, "dG", 16)
                sm.wait(g, "dG")
                i = g.dma_start(out=ROW[0:1, :], in_=scr.ap()[0:1, :])
                sm.inc(i, "dG", 16)

            @blk.vector
            def _(v):
                sm.wait(v, "dG")
                i = v.tensor_reduce(scal(2), ROW[0:1, :], AX.X, ALU.max)
                sm.inc(i, "cDVE")

            @blk.gpsimd
            def _(g):
                sm.wait(g, "cDVE")
                i = g.dma_start(out=cc2_in.ap()[0:1, :], in_=scal(2))
                sm.inc(i, "dG", 16)
                sm.wait(g, "dG")
                i = g.collective_compute(
                    "AllReduce", ALU.max,
                    replica_groups=[list(range(n_cores))],
                    ins=[cc2_in.ap().opt()], outs=[cc2_out.ap().opt()])
                sm.inc(i, "cc2")

        p3_wq = sm.n["sWQ"]
        with nc.Block() as blk:
            dWD0 = sm.n["dWD"]

            @blk.gpsimd
            def _(g):
                sm.wait(g, "sWQ", p3_wq)
                sm.wait(g, "dWD")
                i = g.dma_start(out=X32[:, 0:II], in_=wd_h[0:128, :])
                sm.inc(i, "dWD", 16)

            load_weight_T(blk, wo_h)
            wt_ready_o = sm.n["sWT"]
            sG0 = sm.n["sG"]
            sHP0 = sm.n["sHP"]
            sHQ0 = sm.n["sROPE"]
            sSP0 = sm.n["sSP"]
            sWQ0 = sm.n["sWQ"]
            sT0 = sm.n["sT1"]
            sSC0 = sm.n["sSC"]
            cell = {}

            batches = []
            bstart = 0
            while bstart < IT:
                batches.append((bstart, min(4, IT - bstart)))
                bstart += 4
            NB = len(batches)

            def wdq_dst(ci, th):
                if ci < 16:
                    return A4[:, ci * 1024 + th * 128:ci * 1024 + (th + 1) * 128]
                if ci < 20:
                    return A5[:, 6 * S + (ci - 16) * 1024 + th * 128:
                              6 * S + (ci - 16) * 1024 + (th + 1) * 128]
                return WS[:, (ci - 20) * 1024 + th * 128:
                          (ci - 20) * 1024 + (th + 1) * 128]

            @blk.gpsimd
            def _(g):
                sm.wait(g, "cc2")
                sm.wait(g, "dG")
                i = g.dma_start(out=scal(3), in_=cc2_out.ap()[0:1, :])
                sm.inc(i, "dG", 16)

            if "gmax2" in tap_h:
                @blk.gpsimd
                def _(g):
                    sm.wait(g, "dG")
                    i = g.dma_start(out=tap_h["gmax2"][0:1, :], in_=scal(3))
                    sm.inc(i, "dG", 16)

            FG = 256
            NFG = IT * max(1, S // FG)
            sQE0 = sm.n["sQE"]
            sSIL0 = sm.n["sSIL"]

            def fq_src(fi2):
                it2, g2 = divmod(fi2, max(1, S // FG))
                return hp_ap(it2, g2 * FG, FG)

            def emit_fq_0(v, fi2):
                exc = EX[:, (fi2 % 2) * 256:(fi2 % 2) * 256 + FG]
                if fi2 >= 2:
                    sm.wait(v, "sROPE", sHQ0 + fi2 - 1)
                i = v.tensor_scalar(exc, fq_src(fi2), col(c_tmp + 6),
                                    col(c_tmp + 7), ALU.min, ALU.max)
                sm.inc(i, "sQE")

            def emit_fq_pl(g, fi2):
                exc = EX[:, (fi2 % 2) * 256:(fi2 % 2) * 256 + FG]
                pq = PQ[:, (fi2 % 2) * 256:(fi2 % 2) * 256 + FG]
                if fi2 == 0:
                    sm.wait(g, "sSC", sSC0 + 1)
                sm.wait(g, "sQE", sQE0 + fi2 + 1)
                if fi2 >= 2:
                    sm.wait(g, "sSIL", sSIL0 + fi2 - 1)
                i = g.tensor_scalar(pq, exc, col(c_invs2), MAGIC,
                                    ALU.mult, ALU.add)
                sm.inc(i, "sROPE")

            def emit_fq_c(sc, fi2):
                pq = PQ[:, (fi2 % 2) * 256:(fi2 % 2) * 256 + FG]
                sm.wait(sc, "sROPE", sHQ0 + fi2 + 1)
                i = sc.activation(fq_src(fi2), pq, AF.Identity, bias=MCOL[:, 0:1])
                sm.inc(i, "sSIL")

            @blk.gpsimd
            def _(g):
                fi = 0
                for th in range(1, 8):
                    sm.wait(g, "sWQ", sWQ0 + th)
                    sm.wait(g, "dWD")
                    i = g.dma_start(out=X32[:, 0:II],
                                    in_=wd_h[th * 128:(th + 1) * 128, :])
                    sm.inc(i, "dWD", 16)
                    while fi < th * NFG // 8:
                        emit_fq_pl(g, fi)
                        fi += 1
                while fi < NFG:
                    emit_fq_pl(g, fi)
                    fi += 1

            @blk.scalar
            def _(sc):
                i = sc.activation(scal(4), scal(1), AF.Sigmoid)
                sm.inc(i, "cACT")

            act_sig = sm.n["cACT"]
            dg_g2 = sm.n["dG"]

            def emit_wd_transp(t, th):
                sm.wait(t, "sWQ", sWQ0 + th + 1)
                for bi, (c_lo, bn) in enumerate(batches):
                    nb = th * NB + bi
                    if nb >= 2:
                        sm.wait(t, "sWT", wt_ready_o + nb - 1)
                    pbank = 3072 + (nb % 2) * 512
                    for kk in range(bn):
                        ci = c_lo + kk
                        i = t.matmul(PS[:, pbank + kk * 128:pbank + (kk + 1) * 128],
                                     EX[:, 1024 + ci * 128:1024 + (ci + 1) * 128],
                                     IDE, start=True, stop=True)
                    sm.inc(i, "sT1")

            def emit_bc2(t):
                sm.wait(t, "sSP", sSP0 + 1)
                sm.wait(t, "sWQ", sWQ0 + 1)
                i = t.matmul(PS[:, 2048:2049], ONES[0:1, 0:128], SC16[0:1, 0:1],
                             start=True, stop=True)
                i = t.matmul(PS[:, 2049:2050], ONES[0:1, 0:128], SC16[0:1, 1:2],
                             start=True, stop=True)
                sm.inc(i, "cPE")
                cell["pe_p5"] = sm.n["cPE"]

            @blk.tensor
            def _(t):
                sm.wait(t, "sWT", wt_ready_o)
                nth = 0
                bc_done = False
                for tt in range(NT):
                    gb = (tt % 2) * 1024
                    if tt >= 2:
                        sm.wait(t, "sHP", sHP0 + tt - 1)
                    for cd in range(DC):
                        for c0, w in cchunks(D):
                            i = t.matmul(
                                PS[:, gb + c0:gb + c0 + w],
                                A1[:, cd * S + tt * 128:cd * S + (tt + 1) * 128],
                                WT[:, cd * 1024 + c0:cd * 1024 + c0 + w],
                                start=(cd == 0), stop=(cd == DC - 1))
                    sm.inc(i, "sG")
                    if tt == 1 and nth == 0:
                        emit_wd_transp(t, 0)
                        nth = 1
                    elif tt == 2 and not bc_done:
                        emit_bc2(t)
                        bc_done = True
                    elif tt >= 5 and tt % 2 == 1 and nth < 8 and nth == (tt - 3) // 2:
                        emit_wd_transp(t, nth)
                        nth += 1
                if not bc_done:
                    emit_bc2(t)
                while nth < 8:
                    emit_wd_transp(t, nth)
                    nth += 1

            def emit_wd_chain(v, th):
                sm.wait(v, "dWD", dWD0 + 16 * (th + 1))
                if th >= 1:
                    sm.wait(v, "sT1", sT0 + th * NB)
                ops = [
                    lambda v: v.tensor_reduce(col(c_rmax8 + th), X32[:, 0:II],
                                              AX.X, ALU.max,
                                              apply_absolute_value=True),
                    lambda v: v.reciprocal(col(c_tmp + 3), col(c_rmax8 + th)),
                    lambda v: v.tensor_scalar(col(c_tmp + 4), col(c_tmp + 3),
                                              127.0, None, ALU.mult),
                ]
                for q in range(3):
                    w = min(1024, II - q * 1024)
                    ops.append(lambda v, q=q, w=w: v.tensor_scalar(
                        PS[:, 2048:2048 + w], X32[:, q * 1024:q * 1024 + w],
                        col(c_tmp + 4), MAGIC, ALU.mult, ALU.add))
                    ops.append(lambda v, q=q, w=w: v.tensor_scalar(
                        EX[:, 1024 + q * 1024:1024 + q * 1024 + w],
                        PS[:, 2048:2048 + w], MAGIC, None, ALU.subtract))
                dve_chain(v, ops, last_sem="sWQ")

            def emit_scale_prep(v):
                sm.wait(v, "cACT", act_sig)
                sm.wait(v, "dG", dg_g2)
                dve_chain(v, [
                    lambda v: v.tensor_tensor(scal(5), scal(4), scal(3), ALU.mult),
                    lambda v: v.reciprocal(scal(6), scal(5)),
                    lambda v: v.tensor_scalar(SC16[0:1, 0:1], scal(6), 127.0, None,
                                              ALU.mult),
                    lambda v: v.tensor_scalar(SC16[0:1, 1:2], scal(5), 1.0 / 127.0,
                                              None, ALU.mult),
                ], last_sem="sSP")
                sm.wait(v, "cPE", cell["pe_p5"])
                dve_chain(v, [
                    lambda v: v.tensor_copy(col(c_invs2), PS[:, 2048:2049]),
                    lambda v: v.tensor_copy(col(c_s2), PS[:, 2049:2050]),
                ])
                dve_chain(v, [
                    lambda v: v.tensor_scalar(col(c_tmp + 6), col(c_s2), 127.0,
                                              None, ALU.mult),
                    lambda v: v.tensor_scalar(col(c_tmp + 7), col(c_s2), -128.0,
                                              None, ALU.mult),
                ])
                i = v.tensor_copy(col(c_tmp + 5), col(c_invs2))
                sm.inc(i, "sSC")

            ao_ds0 = [sm.n["dSa"], sm.n["dSb"]]
            sAOE0 = sm.n["sAOE"]

            @blk.sync
            def _(sy):
                for tt in range(NT):
                    for half in range(2):
                        dsem = "dSa" if half == 0 else "dSb"
                        if half == 0:
                            sm.wait(sy, "sAOE", sAOE0 + tt + 1)
                        else:
                            sm.wait(sy, "sHP", sHP0 + tt + 1)
                        sm.wait(sy, dsem)
                        i = sy.dma_start(
                            out=ao_d.ap()[tt * 128:(tt + 1) * 128,
                                          half * 512:(half + 1) * 512],
                            in_=EX[:, 3840 + half * 512:3840 + (half + 1) * 512])
                        sm.inc(i, dsem, 16)

            @blk.vector
            def _(v):
                vth = 1
                cfi = 0
                sp_done = False
                emit_wd_chain(v, 0)
                for tt in range(NT):
                    gb = (tt % 2) * 1024
                    sm.wait(v, "sG", sG0 + tt + 1)
                    for half in range(2):
                        dsem = "dSa" if half == 0 else "dSb"
                        if tt >= 1:
                            sm.wait(v, dsem, ao_ds0[half] + tt * 16)
                        i = v.tensor_copy(
                            EX[:, 3840 + half * 512:3840 + (half + 1) * 512],
                            PS[:, gb + half * 512:gb + half * 512 + 512])
                        sm.inc(i, "sAOE" if half == 0 else "sHP")
                    if tt == 2 and not sp_done:
                        emit_scale_prep(v)
                        sp_done = True
                    elif tt >= 4 and tt % 2 == 0 and vth < 8 and vth == (tt - 2) // 2:
                        emit_wd_chain(v, vth)
                        while cfi < vth * NFG // 8:
                            emit_fq_0(v, cfi)
                            cfi += 1
                        vth += 1
                if not sp_done:
                    emit_scale_prep(v)
                while vth < 8:
                    emit_wd_chain(v, vth)
                    while cfi < vth * NFG // 8:
                        emit_fq_0(v, cfi)
                        cfi += 1
                    vth += 1
                while cfi < NFG:
                    emit_fq_0(v, cfi)
                    cfi += 1

            @blk.scalar
            def _(sc):
                sm.wait(sc, "sWT", wt_ready_o)
                afi = 0
                for th in range(8):
                    for bi, (c_lo, bn) in enumerate(batches):
                        nb = th * NB + bi
                        sm.wait(sc, "sT1", sT0 + nb + 1)
                        pbank = 3072 + (nb % 2) * 512
                        for kk in range(bn):
                            ci = c_lo + kk
                            i = sc.copy(wdq_dst(ci, th),
                                        PS[:, pbank + kk * 128:pbank + (kk + 1) * 128])
                        sm.inc(i, "sWT")
                    while afi < th * NFG // 8:
                        emit_fq_c(sc, afi)
                        afi += 1
                while afi < NFG:
                    emit_fq_c(sc, afi)
                    afi += 1

        wd_done_wt = sm.n["sWT"]
        wd_done_wq = sm.n["sWQ"]
        fq_total = sm.n["sSIL"]

        if "ao" in tap_h:
            with nc.Block() as blk:
                @blk.gpsimd
                def _(g):
                    sm.wait(g, "dSa")
                    sm.wait(g, "dSb")
                    for tt in range(NT):
                        sm.wait(g, "dG")
                        i = g.dma_start(out=tap_h["ao"][:, tt * D:(tt + 1) * D],
                                        in_=ao_d.ap()[tt * 128:(tt + 1) * 128, :])
                        sm.inc(i, "dG", 16)

        with nc.Block() as blk:
            @blk.vector
            def _(v):
                dve_chain(v, [
                    lambda v: v.tensor_scalar(COLS[:, c_rsd:c_rsd + 8],
                                              COLS[:, c_rmax8:c_rmax8 + 8],
                                              col(c_s2), 1.0 / 127.0,
                                              ALU.mult, ALU.mult),
                ])

            rsd_dve = sm.n["cDVE"]

            @blk.gpsimd
            def _(g):
                sm.wait(g, "cDVE", rsd_dve)
                for th in range(8):
                    sm.wait(g, "dG")
                    i = g.dma_start(out=scr.ap()[th:th + 1, :].rearrange("a n -> n a"),
                                    in_=col(c_rsd + th))
                    sm.inc(i, "dG", 16)
                sm.wait(g, "dG")
                i = g.dma_start(out=BVR[0:1, :],
                                in_=scr.ap().rearrange("a n -> (a n)").rearrange(
                                    "(x y) -> x y", x=1))
                sm.inc(i, "dG", 16)

            dg_rsf = sm.n["dG"]

            @blk.tensor
            def _(t):
                sm.wait(t, "dG", dg_rsf)
                for c0, w in cchunks(D):
                    i = t.matmul(PS[:, 2048 + c0:2048 + c0 + w], ONES[0:1, 0:128],
                                 BVR[0:1, c0:c0 + w], start=True, stop=True)
                sm.inc(i, "cPE")

            pe_rsf = sm.n["cPE"]

            @blk.vector
            def _(v):
                sm.wait(v, "cPE", pe_rsf)
                i = v.tensor_copy(EX[:, 0:1024], PS[:, 2048:3072])
                sm.inc(i, "cDVE")

            rsf_dve = sm.n["cDVE"]

            sG0 = sm.n["sG"]
            sNM0 = sm.n["sNM"]

            def wdq_rhs(ci):
                if ci < 16:
                    return A4[:, ci * 1024:(ci + 1) * 1024]
                if ci < 20:
                    return A5[:, 6 * S + (ci - 16) * 1024:6 * S + (ci - 15) * 1024]
                return WS[:, (ci - 20) * 1024:(ci - 19) * 1024]

            @blk.tensor
            def _(t):
                sm.wait(t, "cDVE", rsf_dve)
                sm.wait(t, "sWT", wd_done_wt)
                sm.wait(t, "sSIL", fq_total)
                for tt in range(NT):
                    gb = (tt % 2) * 1024
                    if tt >= 2:
                        sm.wait(t, "sNM", sNM0 + tt - 1)
                    for ci in range(IT):
                        rhs = wdq_rhs(ci)
                        for c0, w in cchunks(D):
                            i = t.matmul(
                                PS[:, gb + c0:gb + c0 + w],
                                hp_ap(ci, tt * 128, 128),
                                rhs[:, c0:c0 + w],
                                start=(ci == 0), stop=(ci == IT - 1))
                    sm.inc(i, "sG")

            dsa0 = sm.n["dSa"]
            dsb0 = sm.n["dSb"]
            dga0 = sm.n["dGa"]
            dgb0 = sm.n["dGb"]

            @blk.gpsimd
            def _(g):
                sm.wait(g, "dSa")
                sm.wait(g, "dSb")
                for tt in range(NT):
                    gsem = "dGa" if tt % 2 == 0 else "dGb"
                    sm.wait(g, gsem)
                    i = g.dma_start(out=A1[:, tt * 1024:(tt + 1) * 1024],
                                    in_=ao_d.ap()[tt * 128:(tt + 1) * 128, :])
                    sm.inc(i, gsem, 16)

            @blk.vector
            def _(v):
                sm.wait(v, "cDVE", rsf_dve)
                for tt in range(NT):
                    dsem = "dSa" if tt % 2 == 0 else "dSb"
                    dbase = (dsa0 if tt % 2 == 0 else dsb0) + (tt // 2 + 1) * 16
                    gb = (tt % 2) * 1024
                    sm.wait(v, "sG", sG0 + tt + 1)
                    if tt >= 1:
                        sm.wait(v, "sNM", sNM0 + tt)
                    sm.wait(v, dsem, dbase)
                    gsem = "dGa" if tt % 2 == 0 else "dGb"
                    gbase = (dga0 if tt % 2 == 0 else dgb0) + (tt // 2 + 1) * 16
                    sm.wait(v, gsem, gbase)
                    xs = X32[:, (tt % 2) * 1024:(tt % 2) * 1024 + 1024]
                    dve_chain(v, [
                        lambda v: v.tensor_tensor(PS[:, 2048:3072],
                                                  PS[:, gb:gb + 1024],
                                                  EX[:, 0:1024], ALU.mult),
                        lambda v: v.tensor_tensor(PS[:, 2048:3072],
                                                  PS[:, 2048:3072],
                                                  A1[:, tt * 1024:(tt + 1) * 1024],
                                                  ALU.add),
                        lambda v: v.tensor_tensor(xs, PS[:, 2048:3072], xs, ALU.add),
                    ], last_sem="sNM")

            @blk.sync
            def _(sy):
                for tt in range(min(NT, 2)):
                    dsem = "dSa" if tt % 2 == 0 else "dSb"
                    sm.wait(sy, dsem)
                    i = sy.dma_start(out=X32[:, (tt % 2) * 1024:(tt % 2) * 1024 + 1024],
                                     in_=x_h[tt * 128:(tt + 1) * 128, :])
                    sm.inc(i, dsem, 16)
                for tt in range(NT):
                    osem = "dOa" if tt % 2 == 0 else "dOb"
                    sm.wait(sy, "sNM", sNM0 + tt + 1)
                    sm.wait(sy, osem)
                    i = sy.dma_start(out=out_h[tt * 128:(tt + 1) * 128, :],
                                     in_=X32[:, (tt % 2) * 1024:(tt % 2) * 1024 + 1024])
                    sm.inc(i, osem, 16)
                    if tt + 2 < NT:
                        dsem = "dSa" if tt % 2 == 0 else "dSb"
                        sm.wait(sy, osem)
                        sm.wait(sy, dsem)
                        i = sy.dma_start(out=X32[:, (tt % 2) * 1024:(tt % 2) * 1024 + 1024],
                                         in_=x_h[(tt + 2) * 128:(tt + 3) * 128, :])
                        sm.inc(i, dsem, 16)

            if "hp" in tap_h:
                @blk.gpsimd
                def _(g):
                    sm.wait(g, "sSIL", fq_total)
                    for it2 in range(IT):
                        sm.wait(g, "dG")
                        i = g.dma_start(out=tap_h["hp"][:, it2 * S:(it2 + 1) * S],
                                        in_=hp_ap(it2, 0, S))
                        sm.inc(i, "dG", 16)

            @blk.gpsimd
            def _(g):
                sm.wait(g, "dOa")
                sm.wait(g, "dOb")
                sm.wait(g, "dG")

    return nc, tap_h


def kernel(L0_attn_x, L0_attn_Wq, L0_attn_bq, L0_attn_Wk, L0_attn_bk,
           L0_attn_Wv, L0_attn_bv, L0_attn_Wo, L0_ffn_clip_x, L0_ffn_clip_down,
           L0_ffn_w_gate, L0_ffn_w_up, L0_ffn_w_down):
    from concourse.bass_utils import run_bass_kernel_spmd
    x = np.ascontiguousarray(np.asarray(L0_attn_x, dtype=np.float32))
    B, S_, D_ = x.shape
    assert D_ == D and B == 8
    shared = dict(
        Wq=np.ascontiguousarray(L0_attn_Wq, dtype=np.float32),
        bq=np.ascontiguousarray(L0_attn_bq, dtype=np.float32),
        Wk=np.ascontiguousarray(L0_attn_Wk, dtype=np.float32),
        bk=np.ascontiguousarray(L0_attn_bk, dtype=np.float32),
        Wv=np.ascontiguousarray(L0_attn_Wv, dtype=np.float32),
        bv=np.ascontiguousarray(L0_attn_bv, dtype=np.float32),
        Wo=np.ascontiguousarray(L0_attn_Wo, dtype=np.float32),
        clip_x=np.ascontiguousarray(L0_ffn_clip_x, dtype=np.float32),
        clip_down=np.ascontiguousarray(L0_ffn_clip_down, dtype=np.float32),
        w_gate=np.ascontiguousarray(L0_ffn_w_gate, dtype=np.float32),
        w_up=np.ascontiguousarray(L0_ffn_w_up, dtype=np.float32),
        w_down=np.ascontiguousarray(L0_ffn_w_down, dtype=np.float32),
    )
    nc, _ = build(S=S_, n_cores=B)
    in_maps = [dict(shared, x=np.ascontiguousarray(x[b])) for b in range(B)]
    res = run_bass_kernel_spmd(nc, in_maps, core_ids=list(range(B)))
    out = np.stack([np.asarray(res.results[b]["out"], dtype=np.float32)
                    for b in range(B)], axis=0)
    return out


# version 19 (final)
# speedup vs baseline: 1.4968x; 1.0316x over previous
import numpy as np
import contextlib
import concourse.bass as bass
import concourse.mybir as mybir

F32 = mybir.dt.float32
F16 = mybir.dt.float16
AF = mybir.ActivationFunctionType
ALU = mybir.AluOpType
AX = mybir.AxisListType

MAGIC = 1.5 * 2.0**23
THETA = 10000.0
TWO_PI = 2.0 * np.pi
C1 = 6.28125
C2 = float(np.float32(np.frombuffer(np.uint32(
    np.frombuffer(np.float32(TWO_PI - C1).tobytes(), dtype=np.uint32)[0]
    & 0xFFFFF000).tobytes(), dtype=np.float32)[0]))
C3 = float(TWO_PI - C1 - C2)
INV2PI = float(np.float32(1.0 / TWO_PI))
LNT32 = float(np.log(THETA) / 32.0)
PI = float(np.pi)

D = 1024
H = 16
HD = 64
DC = 8
II = 2816
IT = 22
EPS = 1e-5

c_jcol, c_hi, c_jm, c_pair, c_sgn, c_f = 0, 1, 2, 3, 4, 5
c_bq, c_bk = 6, 14
c_negmu, c_rstd = 22, 24
c_xmax = 26
c_invsx, c_sx = 28, 29
c_hmax = 30
c_invs2, c_s2 = 32, 33
c_cg, c_cu = 34, 38
c_rsd = 42
c_rmax8 = 56
c_bias = 48
c_sc = 64
c_eps = 79
c_st6 = 80
c_mv = 92
c_tmp = 96
c_mq2 = 104

SEMS = ["dS", "dSa", "dSb", "dG", "dGa", "dGb", "dOa", "dOb",
        "cPE", "cDVE", "cACT", "cPL", "cc1", "cc2",
        "sST", "sEXP", "sAV", "sBC", "sDR", "sDN", "sNM",
        "sG", "sU", "sSIL", "sHP", "sWQ", "sWT", "sQE", "sT1", "sPERM",
        "sROPE", "sSQ", "sRD", "dWD", "sSP", "sAOE"]


class Sems:
    def __init__(self, nc, st):
        self.h, self.n = {}, {}
        for name in SEMS:
            self.h[name] = st.enter_context(nc.semaphore(name))
            self.n[name] = 0

    def inc(self, instr, s, k=1):
        instr.then_inc(self.h[s], k)
        self.n[s] += k
        return self.n[s]

    def wait(self, eng, s, n=None):
        v = self.n[s] if n is None else n
        if v > 0:
            eng.wait_ge(self.h[s], v)


def build(S=2048, n_cores=8, taps=()):
    NT = S // 128
    SLAB = min(512, S)
    NSLAB = S // SLAB
    QG = min(1024, S)
    NQG = S // QG
    FQ = min(512, S)
    NFQ = S // FQ
    KTOT = DC * NSLAB

    nc = bass.Bass("TRN2", target_bir_lowering=False, debug=False)

    x_h = nc.dram_tensor("x", [S, D], F32, kind="ExternalInput").ap()
    wq_h = nc.dram_tensor("Wq", [D, D], F32, kind="ExternalInput").ap()
    bq_h = nc.dram_tensor("bq", [D], F32, kind="ExternalInput").ap()
    wk_h = nc.dram_tensor("Wk", [D, D], F32, kind="ExternalInput").ap()
    bk_h = nc.dram_tensor("bk", [D], F32, kind="ExternalInput").ap()
    wv_h = nc.dram_tensor("Wv", [D, D], F32, kind="ExternalInput").ap()
    bv_h = nc.dram_tensor("bv", [D], F32, kind="ExternalInput").ap()
    wo_h = nc.dram_tensor("Wo", [D, D], F32, kind="ExternalInput").ap()
    clipx_h = nc.dram_tensor("clip_x", [1], F32, kind="ExternalInput").ap()
    clipd_h = nc.dram_tensor("clip_down", [1], F32, kind="ExternalInput").ap()
    wg_h = nc.dram_tensor("w_gate", [II, D], F32, kind="ExternalInput").ap()
    wu_h = nc.dram_tensor("w_up", [II, D], F32, kind="ExternalInput").ap()
    wd_h = nc.dram_tensor("w_down", [D, II], F32, kind="ExternalInput").ap()
    out_h = nc.dram_tensor("out", [S, D], F32, kind="ExternalOutput").ap()

    tap_h = {}
    def mktap(name, shape, dt):
        if name in taps:
            tap_h[name] = nc.dram_tensor("tap_" + name, list(shape), dt,
                                         kind="ExternalOutput").ap()

    mktap("xn", [128, DC * S], F16)
    mktap("q", [128, DC * S], F16)
    mktap("k", [128, DC * S], F16)
    mktap("v", [128, NT * 1040], F16)
    mktap("xq", [128, DC * S], F16)
    mktap("attn", [128, DC * S], F16)
    mktap("hp", [128, IT * S], F16)
    mktap("ao", [128, DC * S], F16)
    mktap("gmax1", [1, 1], F32)
    mktap("gmax2", [1, 1], F32)

    scr = nc.dram_tensor("scr", [8, 128], F32)
    cc1_in = nc.dram_tensor("cc1_in", [1, 1], F32)
    cc1_out = nc.dram_tensor("cc1_out", [1, 1], F32)
    cc2_in = nc.dram_tensor("cc2_in", [1, 1], F32)
    cc2_out = nc.dram_tensor("cc2_out", [1, 1], F32)
    ao_d = nc.dram_tensor("ao_d", [S, D], F16)

    A_COLS = max(DC * S, 16 * 1024)

    st = contextlib.ExitStack()
    with st:
        A1 = st.enter_context(nc.sbuf_tensor("A1", [128, A_COLS], F16))
        A4 = st.enter_context(nc.sbuf_tensor("A4", [128, A_COLS], F16))
        A5 = st.enter_context(nc.sbuf_tensor("A5", [128, max(NT * 1040, 6 * S + 4096)], F16))
        WT = st.enter_context(nc.sbuf_tensor("WTa", [128, 8192], F16))
        WS = st.enter_context(nc.sbuf_tensor("WSa", [128, 2048], F16))
        X32 = st.enter_context(nc.sbuf_tensor("X32", [128, 2816], F32))
        EX = st.enter_context(nc.sbuf_tensor("EXa", [128, max(5120, 2 * S)], F16))
        COLS = st.enter_context(nc.sbuf_tensor("COLS", [128, 128], F32))
        CN = st.enter_context(nc.sbuf_tensor("CN", [128, 384], F16))
        ROW = st.enter_context(nc.sbuf_tensor("ROWa", [1, 128], F32))
        SC16 = st.enter_context(nc.sbuf_tensor("SC16", [1, 32], F16))
        BVR = st.enter_context(nc.sbuf_tensor("BVR", [1, D], F16))
        PS = st.enter_context(nc.psum_tensor("PS", [128, 4096], F32))
        sm = Sems(nc, st)

        IDE = CN[:, 0:128]
        PERM = CN[:, 128:256]
        ONES = CN[:, 256:384]

        def col(c):
            return COLS[:, c:c + 1]

        def scal(c):
            return COLS[0:1, c_sc + c:c_sc + c + 1]

        def cchunks(W):
            return [(c0, min(512, W - c0)) for c0 in range(0, W, 512)]

        def dve_chain(v, ops, last_sem=None):
            for j, fn in enumerate(ops):
                i = fn(v)
                if last_sem is not None and j == len(ops) - 1:
                    sm.inc(i, last_sem)
                else:
                    sm.inc(i, "cDVE")
                    sm.wait(v, "cDVE")

        def act_chain(sc, ops, last_sem=None):
            for j, fn in enumerate(ops):
                i = fn(sc)
                if last_sem is not None and j == len(ops) - 1:
                    sm.inc(i, last_sem)
                else:
                    sm.inc(i, "cACT")
                    sm.wait(sc, "cACT")

        def pl_chain(g, ops):
            for fn in ops:
                i = fn(g)
                sm.inc(i, "cPL")
                sm.wait(g, "cPL")

        def hp_ap(it, t0, w):
            if it < 8:
                return A2[:, it * S + t0:it * S + t0 + w]
            if it < 16:
                return A3[:, (it - 8) * S + t0:(it - 8) * S + t0 + w]
            return A5[:, (it - 16) * S + t0:(it - 16) * S + t0 + w]

        def emit_taps(blk, items):
            live = [(n, ap) for n, ap in items if n in tap_h]
            if not live:
                return
            @blk.gpsimd
            def _(g):
                sm.wait(g, "cDVE")
                sm.wait(g, "cACT")
                for n, ap in live:
                    sm.wait(g, "dG")
                    i = g.dma_start(out=tap_h[n][:, :] if tap_h[n].shape[0] > 1
                                    else tap_h[n][0:1, :], in_=ap)
                    sm.inc(i, "dG", 16)
                sm.wait(g, "dG")

        p0 = contextlib.ExitStack()
        A2 = A3 = None
        with p0:
            F32T = p0.enter_context(nc.sbuf_tensor("F32T", [128, 2 * S], F32))
            SA = X32[:, 0:S]
            SB = F32T[:, 0:S]
            SCc = F32T[:, S:2 * S]

            with nc.Block() as blk:
                @blk.sync
                def _(sy):
                    i = sy.dma_start(out=COLS[:, c_bq:c_bq + 8],
                                     in_=bq_h.rearrange("(a p) -> p a", p=128),
                                     allow_slow_non_contiguous=True)
                    sm.inc(i, "dS", 16)
                    i = sy.dma_start(out=COLS[:, c_bk:c_bk + 8],
                                     in_=bk_h.rearrange("(a p) -> p a", p=128),
                                     allow_slow_non_contiguous=True)
                    sm.inc(i, "dS", 16)
                    i = sy.dma_start(out=COLS[0:1, c_sc + 0:c_sc + 1],
                                     in_=clipx_h.rearrange("(a b) -> a b", a=1))
                    sm.inc(i, "dS", 16)
                    i = sy.dma_start(out=COLS[0:1, c_sc + 1:c_sc + 2],
                                     in_=clipd_h.rearrange("(a b) -> a b", a=1))
                    sm.inc(i, "dS", 16)

                @blk.gpsimd
                def _(g):
                    g.memset(col(c_eps), EPS)
                    g.iota(col(c_jcol), [[0, 1]], channel_multiplier=1,
                           allow_small_or_imprecise_dtypes=True)
                    i = g.iota(SA, [[1, S]], channel_multiplier=0,
                               allow_small_or_imprecise_dtypes=True)
                    sm.inc(i, "cPL")
                    g.memset(CN[:, 0:256], 0.0)
                    i = g.memset(ONES, 1.0)
                    sm.inc(i, "cPL")
                    sm.wait(g, "cPL")
                    pl_chain(g, [
                        lambda g: g.affine_select(out=IDE, in_=IDE,
                                                  compare_op=ALU.not_equal,
                                                  fill=1.0, base=0,
                                                  pattern=[[-1, 128]],
                                                  channel_multiplier=1),
                        lambda g: g.memset(WS[:, 0:256], 0.0),
                        lambda g: g.tensor_copy(WS[:, 1:128], IDE[:, 0:127]),
                        lambda g: g.tensor_copy(WS[:, 128:255], IDE[:, 1:128]),
                        lambda g: g.memset(
                            A5[:, 0:NT * 1040].rearrange("p (t h c) -> p t h c",
                                                         t=NT, h=H)[:, :, :, 64:65],
                            1.0),
                    ])

                @blk.vector
                def _(v):
                    sm.wait(v, "cPL", 1)
                    dve_chain(v, [
                        lambda v: v.tensor_scalar(col(c_hi), col(c_jcol), 1.0 / 64.0,
                                                  -0.5 + 0.25 / 64.0, ALU.mult, ALU.add),
                        lambda v: v.tensor_scalar(col(c_hi), col(c_hi), MAGIC, MAGIC,
                                                  ALU.add, ALU.subtract),
                        lambda v: v.scalar_tensor_tensor(col(c_jm), col(c_hi), -64.0,
                                                         col(c_jcol), ALU.mult, ALU.add),
                        lambda v: v.tensor_scalar(col(c_pair), col(c_jm), 0.5, -0.25,
                                                  ALU.mult, ALU.add),
                        lambda v: v.tensor_scalar(col(c_pair), col(c_pair), MAGIC, MAGIC,
                                                  ALU.add, ALU.subtract),
                        lambda v: v.scalar_tensor_tensor(col(c_sgn), col(c_pair), 2.0,
                                                         col(c_jm), ALU.mult, ALU.subtract),
                        lambda v: v.tensor_scalar(col(c_sgn), col(c_sgn), 2.0, 1.0,
                                                  ALU.mult, ALU.add),
                    ])

                @blk.scalar
                def _(sc):
                    sm.wait(sc, "cDVE")
                    i = sc.activation(col(c_f), col(c_pair), AF.Exp, scale=-LNT32)
                    sm.inc(i, "cACT")

                @blk.vector
                def _(v):
                    sm.wait(v, "cACT")
                    dve_chain(v, [
                        lambda v: v.tensor_scalar(SB, SA, col(c_f), None, ALU.mult),
                        lambda v: v.tensor_scalar(SA, SB, INV2PI, None, ALU.mult),
                        lambda v: v.tensor_scalar(SCc, SA, MAGIC, MAGIC, ALU.add, ALU.subtract),
                        lambda v: v.scalar_tensor_tensor(SA, SCc, -C1, SB, ALU.mult, ALU.add),
                        lambda v: v.scalar_tensor_tensor(SB, SCc, -C2, SA, ALU.mult, ALU.add),
                        lambda v: v.scalar_tensor_tensor(SCc, SCc, -C3, SB, ALU.mult, ALU.add)
                        if False else
                        v.scalar_tensor_tensor(SB, SCc, -C3, SB, ALU.mult, ALU.add)
                        if False else
                        v.scalar_tensor_tensor(SA, SCc, -C3, SB, ALU.mult, ALU.add),
                        lambda v: v.tensor_scalar(SA, SA, PI, -PI, ALU.min, ALU.max)
                        if False else
                        v.tensor_scalar(SB, SA, PI, -PI, ALU.min, ALU.max),
                        lambda v: v.tensor_scalar(SA, SB, 1.0, None, ALU.mult),
                        lambda v: v.tensor_scalar(SB, SA, col(c_sgn), None, ALU.mult),
                    ])

                @blk.scalar
                def _(sc):
                    sm.wait(sc, "cDVE")
                    i = sc.activation(EX[:, 0:S], SB, AF.Sin)
                    sm.inc(i, "cACT")

                @blk.vector
                def _(v):
                    sm.wait(v, "cACT")
                    dve_chain(v, [
                        lambda v: v.tensor_scalar(SB, SA, PI / 2.0, None, ALU.add),
                        lambda v: v.tensor_scalar(SCc, SB, PI, None, ALU.is_gt),
                        lambda v: v.scalar_tensor_tensor(SA, SCc, -TWO_PI, SB,
                                                         ALU.mult, ALU.add),
                        lambda v: v.tensor_scalar(SCc, SA, PI, -PI, ALU.min, ALU.max),
                        lambda v: v.tensor_scalar(SA, SCc, 1.0, None, ALU.mult),
                    ])

                @blk.scalar
                def _(sc):
                    sm.wait(sc, "cDVE")
                    i = sc.activation(EX[:, S:2 * S], SA, AF.Sin)
                    sm.inc(i, "cACT")

                @blk.vector
                def _(v):
                    sm.wait(v, "cACT")
                    sm.wait(v, "cPL", 6)
                    dve_chain(v, [
                        lambda v: v.tensor_scalar(col(c_tmp), col(c_sgn), 0.5, 0.5,
                                                  ALU.mult, ALU.add),
                        lambda v: v.tensor_tensor(WS[:, 256:384], WS[:, 0:128],
                                                  WS[:, 128:256], ALU.subtract),
                        lambda v: v.tensor_scalar(WS[:, 256:384], WS[:, 256:384],
                                                  col(c_tmp), None, ALU.mult),
                        lambda v: v.tensor_tensor(PERM, WS[:, 256:384], WS[:, 128:256],
                                                  ALU.add),
                        lambda v: v.memset(COLS[:, c_xmax:c_xmax + 2], 0.0),
                    ])

                act_pre = sm.n["cACT"]
                dve_pre = sm.n["cDVE"]
                VPT, APT = 7, 2

                @blk.sync
                def _(sy):
                    for tt in range(NT):
                        dsem = "dSa" if tt % 2 == 0 else "dSb"
                        if tt < 2:
                            sm.wait(sy, "cDVE", dve_pre)
                            sm.wait(sy, "cACT", act_pre)
                        else:
                            sm.wait(sy, "cDVE", dve_pre + (tt - 2) * VPT + 5)
                        sm.wait(sy, dsem)
                        i = sy.dma_start(out=X32[:, (tt % 2) * 1024:(tt % 2) * 1024 + 1024],
                                         in_=x_h[tt * 128:(tt + 1) * 128, :])
                        sm.inc(i, dsem, 16)

                @blk.vector
                def _(v):
                    for tt in range(NT):
                        dsem = "dSa" if tt % 2 == 0 else "dSb"
                        xs = X32[:, (tt % 2) * 1024:(tt % 2) * 1024 + 1024]
                        ws = WS[:, (tt % 2) * 1024:(tt % 2) * 1024 + 1024]
                        sm.wait(v, dsem, (tt // 2 + 1) * 16)
                        dve_chain(v, [
                            lambda v: v.bn_stats(COLS[:, c_st6:c_st6 + 6], xs[:, 0:512]),
                            lambda v: v.bn_stats(COLS[:, c_st6 + 6:c_st6 + 12],
                                                 xs[:, 512:1024]),
                            lambda v: v.bn_aggr(COLS[:, c_mv:c_mv + 2],
                                                COLS[:, c_st6:c_st6 + 12]),
                            lambda v: v.tensor_scalar(col(c_negmu + tt % 2),
                                                      col(c_mv), -1.0, None, ALU.mult),
                        ])
                        sm.wait(v, "cACT", act_pre + tt * APT + 2)
                        if tt >= 2:
                            sm.wait(v, "sG", tt - 1)
                        dve_chain(v, [
                            lambda v: v.tensor_scalar(ws, xs, col(c_negmu + tt % 2),
                                                      col(c_rstd + tt % 2),
                                                      ALU.add, ALU.mult),
                            lambda v: v.tensor_reduce(col(c_tmp), ws, AX.X, ALU.max,
                                                      apply_absolute_value=True),
                            lambda v: v.tensor_tensor(col(c_xmax + (tt + 1) % 2),
                                                      col(c_xmax + tt % 2),
                                                      col(c_tmp), ALU.max),
                        ])

                @blk.scalar
                def _(sc):
                    for tt in range(NT):
                        sm.wait(sc, "cDVE", dve_pre + tt * VPT + 4)
                        act_chain(sc, [
                            lambda sc: sc.activation(col(c_tmp + 1), col(c_mv + 1),
                                                     AF.Ln, bias=col(c_eps)),
                            lambda sc: sc.activation(col(c_rstd + tt % 2),
                                                     col(c_tmp + 1), AF.Exp, scale=-0.5),
                        ])
                        sm.wait(sc, "sG", tt + 1)
                        bank = (tt % 2) * 2048
                        i = sc.copy(
                            A1[:, 0:DC * S].rearrange("p (c s) -> p c s", c=DC)[:, :, tt * 128:(tt + 1) * 128],
                            PS[:, bank:bank + 1024].rearrange("p (c b) -> p c b", c=DC))
                        sm.inc(i, "sWT")

                @blk.tensor
                def _(t):
                    sm.wait(t, "cPL", 3)
                    for tt in range(NT):
                        ws = WS[:, (tt % 2) * 1024:(tt % 2) * 1024 + 1024]
                        sm.wait(t, "cDVE", dve_pre + tt * VPT + 5)
                        if tt >= 2:
                            sm.wait(t, "sWT", tt - 1)
                        bank = (tt % 2) * 2048
                        for cd in range(DC):
                            i = t.matmul(PS[:, bank + cd * 128:bank + (cd + 1) * 128],
                                         ws[:, cd * 128:(cd + 1) * 128], IDE,
                                         start=True, stop=True)
                        sm.inc(i, "sG")

                @blk.gpsimd
                def _(g):
                    sm.wait(g, "cDVE", dve_pre + NT * VPT)
                    i = g.dma_start(out=scr.ap()[0:1, :].rearrange("a n -> n a"),
                                    in_=col(c_xmax + NT % 2))
                    sm.inc(i, "dG", 16)
                    sm.wait(g, "dG")
                    i = g.dma_start(out=ROW[0:1, :], in_=scr.ap()[0:1, :])
                    sm.inc(i, "dG", 16)

                @blk.vector
                def _(v):
                    sm.wait(v, "dG")
                    i = v.tensor_reduce(scal(2), ROW[0:1, :], AX.X, ALU.max)
                    sm.inc(i, "cDVE")

                @blk.gpsimd
                def _(g):
                    sm.wait(g, "cDVE")
                    i = g.dma_start(out=cc1_in.ap()[0:1, :], in_=scal(2))
                    sm.inc(i, "dG", 16)
                    sm.wait(g, "dG")
                    i = g.collective_compute(
                        "AllReduce", ALU.max,
                        replica_groups=[list(range(n_cores))],
                        ins=[cc1_in.ap().opt()], outs=[cc1_out.ap().opt()])
                    sm.inc(i, "cc1")

        A2 = st.enter_context(nc.sbuf_tensor("A2", [128, max(DC * S, 8 * S)], F16))
        A3 = st.enter_context(nc.sbuf_tensor("A3", [128, max(DC * S, 8 * S)], F16))

        def load_weight_T(blk, w_dram, wt_free_sem=None, wt_free_val=0):
            pe0 = sm.n["sU"]
            ev0 = sm.n["sWT"]

            @blk.gpsimd
            def _(g):
                for cj in range(8):
                    wsem = "dGa" if cj % 2 == 0 else "dGb"
                    sm.wait(g, "sU", pe0 + max(0, (cj - 1) * 2))
                    sm.wait(g, wsem)
                    i = g.dma_start(out=WS[:, (cj % 2) * 1024:(cj % 2) * 1024 + 1024],
                                    in_=w_dram[cj * 128:(cj + 1) * 128, :])
                    sm.inc(i, wsem, 16)

            dg_a = sm.n["dGa"] - 4 * 16
            dg_b = sm.n["dGb"] - 4 * 16

            @blk.tensor
            def _(t):
                for cj in range(8):
                    wsem = "dGa" if cj % 2 == 0 else "dGb"
                    base = (dg_a if cj % 2 == 0 else dg_b) + (cj // 2 + 1) * 16
                    sm.wait(t, wsem, base)
                    ws = WS[:, (cj % 2) * 1024:(cj % 2) * 1024 + 1024]
                    for b in range(2):
                        nb = 2 * cj + b
                        if nb >= 2:
                            sm.wait(t, "sWT", ev0 + nb - 1)
                        pbank = 3072 + (nb % 2) * 512
                        for kk in range(4):
                            cd = b * 4 + kk
                            i = t.matmul(PS[:, pbank + kk * 128:pbank + (kk + 1) * 128],
                                         ws[:, cd * 128:(cd + 1) * 128], IDE,
                                         start=True, stop=True)
                        sm.inc(i, "sU")

            @blk.scalar
            def _(sc):
                if wt_free_sem is not None:
                    sm.wait(sc, wt_free_sem, wt_free_val)
                for cj in range(8):
                    for b in range(2):
                        nb = 2 * cj + b
                        sm.wait(sc, "sU", pe0 + nb + 1)
                        pbank = 3072 + (nb % 2) * 512
                        i = sc.copy(
                            WT[:, 0:8192].rearrange("p (c b) -> p c b", c=DC)
                            [:, b * 4:(b + 1) * 4, cj * 128:(cj + 1) * 128],
                            PS[:, pbank:pbank + 512].rearrange("p (c b) -> p c b", c=4))
                        sm.inc(i, "sWT")

        with nc.Block() as blk:
            def qk_proj(dst, bias_c):
                NG = 8 * NQG
                sST0 = sm.n["sST"]
                sQE0 = sm.n["sQE"]
                sT10 = sm.n["sT1"]
                sPERM0 = sm.n["sPERM"]
                sROPE0 = sm.n["sROPE"]
                wt_ready = sm.n["sWT"]

                @blk.tensor
                def _(t):
                    sm.wait(t, "sWT", wt_ready)
                    for g in range(NG + 1):
                        if g < NG:
                            cj, qg = divmod(g, NQG)
                            gb = (g % 2) * 1024
                            if g >= 2:
                                sm.wait(t, "sQE", sQE0 + g - 1)
                            for cd in range(DC):
                                for c0, w in cchunks(QG):
                                    i = t.matmul(
                                        PS[:, gb + c0:gb + c0 + w],
                                        WT[:, cd * 1024 + cj * 128:cd * 1024 + (cj + 1) * 128],
                                        A1[:, cd * S + qg * QG + c0:cd * S + qg * QG + c0 + w],
                                        start=(cd == 0), stop=(cd == DC - 1))
                            sm.inc(i, "sST")
                        if g >= 1:
                            gp = g - 1
                            sm.wait(t, "sT1", sT10 + gp + 1)
                            if gp >= 1:
                                sm.wait(t, "sROPE", sROPE0 + gp)
                            rs = A4[:, (gp % 2) * 1024:(gp % 2) * 1024 + QG]
                            for c0, w in cchunks(QG):
                                i = t.matmul(PS[:, 2048 + c0:2048 + c0 + w],
                                             PERM, rs[:, c0:c0 + w],
                                             start=True, stop=True)
                            sm.inc(i, "sPERM")

                sSQ0 = sm.n["sSQ"]

                @blk.scalar
                def _(sc):
                    sm.wait(sc, "dS", 64)
                    for g in range(NG):
                        cj = g // NQG
                        gb = (g % 2) * 1024
                        sm.wait(sc, "sST", sST0 + g + 1)
                        sm.wait(sc, "sT1", sT10 + max(0, g - 1))
                        sm.wait(sc, "sSQ", sSQ0 + max(0, g - 1))
                        i = sc.activation(X32[:, gb:gb + QG], PS[:, gb:gb + QG],
                                          AF.Identity, bias=col(bias_c + cj))
                        sm.inc(i, "sQE")

                @blk.vector
                def _(v):
                    for g in range(NG):
                        cj, qg = divmod(g, NQG)
                        t0 = qg * QG
                        gb = (g % 2) * 1024
                        xr = X32[:, gb:gb + QG]
                        rs = A4[:, gb:gb + QG]
                        sm.wait(v, "sQE", sQE0 + g + 1)
                        if g >= 2:
                            sm.wait(v, "sPERM", sPERM0 + g - 1)
                        i = v.tensor_tensor(rs, xr, EX[:, t0:t0 + QG], ALU.mult)
                        sm.inc(i, "sT1")
                        sm.wait(v, "sPERM", sPERM0 + g + 1)
                        i = v.tensor_tensor(rs, xr, EX[:, S + t0:S + t0 + QG], ALU.mult)
                        sm.inc(i, "sSQ")
                        sm.wait(v, "sSQ")
                        i = v.tensor_tensor(dst[:, cj * S + t0:cj * S + t0 + QG],
                                            rs, PS[:, 2048:2048 + QG], ALU.add)
                        sm.inc(i, "sROPE")

            load_weight_T(blk, wq_h)
            qk_proj(A2, c_bq)
            load_weight_T(blk, wk_h, wt_free_sem="sST", wt_free_val=sm.n["sST"])
            qk_proj(A3, c_bk)
            load_weight_T(blk, wv_h, wt_free_sem="sST", wt_free_val=sm.n["sST"])

            @blk.gpsimd
            def _(g):
                sm.wait(g, "dGa")
                i = g.dma_start(out=BVR[0:1, :], in_=bv_h.rearrange("(a b) -> a b", a=1))
                sm.inc(i, "dGa", 16)

            dg_bv = sm.n["dGa"]
            rope_done = sm.n["sROPE"]
            wtv_ready = sm.n["sWT"]
            sG0 = sm.n["sG"]
            sU0v = sm.n["sHP"]

            @blk.tensor
            def _(t):
                sm.wait(t, "dGa", dg_bv)
                sm.wait(t, "sROPE", rope_done)
                sm.wait(t, "sWT", wtv_ready)
                for c0, w in cchunks(D):
                    i = t.matmul(PS[:, 2048 + c0:2048 + c0 + w], ONES[0:1, 0:128],
                                 BVR[0:1, c0:c0 + w], start=True, stop=True)
                sm.inc(i, "sG")
                for tt in range(NT):
                    gb = (tt % 2) * 1024
                    if tt >= 2:
                        sm.wait(t, "sHP", sU0v + tt - 1)
                    for cd in range(DC):
                        for c0, w in cchunks(D):
                            i = t.matmul(
                                PS[:, gb + c0:gb + c0 + w],
                                A1[:, cd * S + tt * 128:cd * S + (tt + 1) * 128],
                                WT[:, cd * 1024 + c0:cd * 1024 + c0 + w],
                                start=(cd == 0), stop=(cd == DC - 1))
                    sm.inc(i, "sG")

            @blk.vector
            def _(v):
                sm.wait(v, "sG", sG0 + 1)
                i = v.tensor_copy(WS[:, 0:1024], PS[:, 2048:3072])
                sm.inc(i, "cDVE")
                sm.wait(v, "cDVE")
                for tt in range(NT):
                    gb = (tt % 2) * 1024
                    sm.wait(v, "sG", sG0 + tt + 2)
                    i = v.tensor_tensor(
                        A5[:, tt * 1040:(tt + 1) * 1040].rearrange(
                            "p (h c) -> p h c", h=H)[:, :, 0:64],
                        PS[:, gb:gb + 1024].rearrange("p (h c) -> p h c", h=H),
                        WS[:, 0:1024].rearrange("p (h c) -> p h c", h=H),
                        ALU.add)
                    sm.inc(i, "sHP")

            @blk.gpsimd
            def _(g):
                sm.wait(g, "cc1")
                i = g.dma_start(out=scal(3), in_=cc1_out.ap()[0:1, :])
                sm.inc(i, "dG", 16)

            @blk.scalar
            def _(sc):
                sm.wait(sc, "dS", 64)
                i = sc.activation(scal(4), scal(0), AF.Sigmoid)
                sm.inc(i, "cACT")

            @blk.vector
            def _(v):
                sm.wait(v, "cACT")
                sm.wait(v, "dG")
                dve_chain(v, [
                    lambda v: v.tensor_tensor(scal(5), scal(4), scal(3), ALU.mult),
                    lambda v: v.reciprocal(scal(6), scal(5)),
                    lambda v: v.tensor_scalar(SC16[0:1, 0:1], scal(6), 127.0, None,
                                              ALU.mult),
                    lambda v: v.tensor_scalar(SC16[0:1, 1:2], scal(5), 1.0 / 127.0, None,
                                              ALU.mult),
                ])

            v_done = sm.n["sHP"]
            dve_sc = sm.n["cDVE"]

            @blk.tensor
            def _(t):
                sm.wait(t, "cDVE", dve_sc)
                sm.wait(t, "sHP", v_done)
                i = t.matmul(PS[:, 2048:2049], ONES[0:1, 0:128], SC16[0:1, 0:1],
                             start=True, stop=True)
                sm.inc(i, "cPE")
                i = t.matmul(PS[:, 2049:2050], ONES[0:1, 0:128], SC16[0:1, 1:2],
                             start=True, stop=True)
                sm.inc(i, "cPE")

            @blk.vector
            def _(v):
                sm.wait(v, "cPE")
                dve_chain(v, [
                    lambda v: v.tensor_copy(col(c_invsx), PS[:, 2048:2049]),
                    lambda v: v.tensor_copy(col(c_sx), PS[:, 2049:2050]),
                ])
                for cd in range(DC):
                    dve_chain(v, [
                        lambda v: v.tensor_scalar(X32[:, 0:S], A1[:, cd * S:(cd + 1) * S],
                                                  col(c_invsx), MAGIC, ALU.mult, ALU.add),
                        lambda v: v.tensor_scalar(WS[:, 0:min(S, 2048)], X32[:, 0:S],
                                                  MAGIC, 127.0, ALU.subtract, ALU.min),
                        lambda v: v.tensor_scalar(A4[:, cd * S:(cd + 1) * S],
                                                  WS[:, 0:min(S, 2048)],
                                                  -128.0, None, ALU.max),
                    ])

        with nc.Block() as blk:
            emit_taps(blk, [("xn", A1[:, 0:DC * S]), ("q", A2[:, 0:DC * S]),
                            ("k", A3[:, 0:DC * S]), ("v", A5[:, 0:NT * 1040]),
                            ("xq", A4[:, 0:DC * S]), ("gmax1", scal(3))])

        with nc.Block() as blk:
            sSQ0 = sm.n["sSQ"]
            sU0 = sm.n["sU"]
            sRD0 = sm.n["sRD"]

            @blk.scalar
            def _(sc):
                for n in range(2 * DC):
                    p, which = divmod(n, 2)
                    if n >= 2:
                        sm.wait(sc, "sU", sU0 + 2 * (n - 1))
                    src = A2 if which == 0 else A3
                    i = sc.activation(EX[:, which * S:which * S + S],
                                      src[:, p * S:(p + 1) * S], AF.Square)
                    sm.inc(i, "sSQ")

            def c0_region(idx):
                rb = 64 * (idx % 2)
                cb = 2048 * ((idx // 2) % 2)
                return rb, cb

            @blk.tensor
            def _(t):
                for n in range(2 * DC):
                    p, which = divmod(n, 2)
                    sm.wait(t, "sSQ", sSQ0 + n + 1)
                    for half in range(2):
                        idx = 2 * n + half
                        if idx >= 4:
                            sm.wait(t, "sRD", sRD0 + idx - 3)
                        rb, cb = c0_region(idx)
                        for c0, w in cchunks(S):
                            i = t.matmul(
                                PS[rb:rb + 64, cb + c0:cb + c0 + w],
                                ONES[64 * half:64 * half + 64, 0:64],
                                EX[64 * half:64 * half + 64,
                                   which * S + c0:which * S + c0 + w],
                                start=True, stop=True)
                        sm.inc(i, "sU")

            @blk.vector
            def _(v):
                for n in range(2 * DC):
                    p, which = divmod(n, 2)
                    for half in range(2):
                        idx = 2 * n + half
                        sm.wait(v, "sU", sU0 + idx + 1)
                        h = 2 * p + half
                        dstc = (80 if which == 0 else 96) + h
                        rb, cb = c0_region(idx)
                        i = v.tensor_reduce(COLS[0:1, dstc:dstc + 1],
                                            PS[rb:rb + 1, cb:cb + S], AX.X, ALU.max)
                        sm.inc(i, "sRD")

            @blk.vector
            def _(v):
                sm.wait(v, "sRD", sRD0 + 4 * DC)
                i = v.tensor_tensor(COLS[0:1, 112:128], COLS[0:1, 80:96],
                                    COLS[0:1, 96:112], ALU.mult)
                sm.inc(i, "cDVE")

            dve_c0 = sm.n["cDVE"]

            @blk.scalar
            def _(sc):
                sm.wait(sc, "cDVE", dve_c0)
                act_chain(sc, [
                    lambda sc: sc.activation(COLS[0:1, 80:96], COLS[0:1, 112:128],
                                             AF.Ln),
                    lambda sc: sc.activation(COLS[0:1, 96:112], COLS[0:1, 80:96],
                                             AF.Exp, scale=0.5),
                ])

            @blk.vector
            def _(v):
                sm.wait(v, "cACT")
                i = v.tensor_scalar(SC16[0:1, 2:18], COLS[0:1, 96:112],
                                    -1.0 / 32.0, None, ALU.mult)
                sm.inc(i, "cDVE")

            dve_c0b = sm.n["cDVE"]

            @blk.tensor
            def _(t):
                sm.wait(t, "cDVE", dve_c0b)
                i = t.matmul(PS[:, 2048:2064], ONES[0:1, 0:128], SC16[0:1, 2:18],
                             start=True, stop=True)
                sm.inc(i, "cPE")

            pe_c0 = sm.n["cPE"]

            @blk.vector
            def _(v):
                sm.wait(v, "cPE", pe_c0)
                dve_chain(v, [
                    lambda v: v.tensor_copy(COLS[:, c_bias:c_bias + 16], PS[:, 2048:2064]),
                    lambda v: v.tensor_tensor(
                        COLS[:, 104:112],
                        COLS[:, c_bias:c_bias + 16].rearrange("p (a b) -> p a b", b=2)[:, :, 0],
                        COLS[:, c_bias:c_bias + 16].rearrange("p (a b) -> p a b", b=2)[:, :, 1],
                        ALU.min),
                ])

            bias_dve = sm.n["cDVE"]

            sST0 = sm.n["sST"]
            sEXP0 = sm.n["sEXP"]
            sAV0 = sm.n["sAV"]
            sBC0 = sm.n["sBC"]
            sDR0 = sm.n["sDR"]
            sDN0 = sm.n["sDN"]
            sNM0 = sm.n["sNM"]
            dG0 = sm.n["dG"]

            def st_bank(half, c):
                return (2 * half + c % 2) * 512

            def av_bank(half, k):
                return 2048 + (2 * (k % 2) + half) * 512

            def bc_bank(half):
                return half * 1024

            @blk.tensor
            def _(t):
                sm.wait(t, "cDVE", bias_dve)
                def emit_st(t, k, p, s_, c):
                    sm.wait(t, "sEXP", sEXP0 + 2 * max(0, NT * k + c - 1))
                    for half in range(2):
                        i = t.matmul(
                            PS[:, st_bank(half, c):st_bank(half, c) + SLAB],
                            A3[64 * half:64 * half + 64,
                               p * S + c * 128:p * S + (c + 1) * 128],
                            A2[64 * half:64 * half + 64,
                               p * S + s_ * SLAB:p * S + (s_ + 1) * SLAB],
                            start=True, stop=True)
                    sm.inc(i, "sST")

                def emit_av(t, k, p, cp):
                    sm.wait(t, "sEXP", sEXP0 + 2 * (NT * k + cp + 1))
                    if cp == 0:
                        sm.wait(t, "sNM", sNM0 + max(0, 2 * k - 2))
                    for half in range(2):
                        hh = 2 * p + half
                        exb = ((cp % 2) * 2 + half) * SLAB
                        i = t.matmul(
                            PS[0:65, av_bank(half, k):av_bank(half, k) + SLAB],
                            A5[:, cp * 1040 + hh * 65:cp * 1040 + (hh + 1) * 65],
                            EX[:, exb:exb + SLAB],
                            start=(cp == 0), stop=(cp == NT - 1))
                    sm.inc(i, "sAV")

                def emit_bc(t, j, h):
                    sm.wait(t, "sDR", sDR0 + 2 * j + h + 1)
                    if j >= 2:
                        sm.wait(t, "sDN", sDN0 + 4 * (j - 2) + (1 if h == 0 else 3))
                    i = t.matmul(
                        PS[64:128, av_bank(h, j):av_bank(h, j) + SLAB],
                        ONES[64:65, 0:64],
                        WS[64:65, 1024 + h * 512:1024 + h * 512 + SLAB],
                        start=True, stop=True)
                    sm.inc(i, "sBC")

                deep = NT >= 8
                for k in range(KTOT):
                    p, s_ = divmod(k, NSLAB)
                    for c in range(NT):
                        emit_st(t, k, p, s_, c)
                        if c >= 2:
                            emit_av(t, k, p, c - 2)
                        if deep and k >= 1:
                            if c == 2:
                                emit_bc(t, k - 1, 0)
                            elif c == 3:
                                emit_bc(t, k - 1, 1)
                    for cp in (NT - 2, NT - 1):
                        if cp < 0:
                            continue
                        emit_av(t, k, p, cp)
                    if not deep and k >= 1:
                        emit_bc(t, k - 1, 0)
                        emit_bc(t, k - 1, 1)
                emit_bc(t, KTOT - 1, 0)
                emit_bc(t, KTOT - 1, 1)

            st_total = sm.n["sST"]

            @blk.scalar
            def _(sc):
                def emit_ln(sc, j, h):
                    sm.wait(sc, "sBC", sBC0 + 2 * j + h + 1)
                    xslab = X32[0:64, h * 1024:h * 1024 + SLAB]
                    i = sc.activation(xslab,
                                      PS[64:128, av_bank(h, j):av_bank(h, j) + SLAB],
                                      AF.Ln)
                    sm.inc(i, "sDN")

                def emit_rep(sc, j, h):
                    sm.wait(sc, "sDN")
                    if h == 1:
                        sm.wait(sc, "sNM", sNM0 + 2 * j + 1)
                    xslab = X32[0:64, h * 1024:h * 1024 + SLAB]
                    i = sc.activation(WS[0:64, 0:SLAB], xslab, AF.Exp, scale=-1.0)
                    sm.inc(i, "sDN")

                deep = NT >= 8
                for k in range(KTOT):
                    for c in range(NT):
                        sm.wait(sc, "sST", sST0 + NT * k + c + 1)
                        if c >= 2:
                            sm.wait(sc, "sAV", sAV0 + NT * k + c - 1)
                        pp = k // NSLAB
                        i = sc.activation(
                            EX[:, (c % 2) * 2 * SLAB:(c % 2) * 2 * SLAB + 2 * SLAB]
                            .rearrange("p (a r) -> p a r", a=2),
                            PS[:, 0:2048].rearrange("p (a q) -> p a q", a=2)
                            [:, :, (c % 2) * 512:(c % 2) * 512 + SLAB],
                            AF.Exp, bias=col(104 + pp),
                            scale=1.0 / 32.0)
                        sm.inc(i, "sEXP", 2)
                        if deep and k >= 1:
                            if c == 3:
                                emit_ln(sc, k - 1, 0)
                            elif c == 4:
                                emit_rep(sc, k - 1, 0)
                            elif c == 6:
                                emit_ln(sc, k - 1, 1)
                            elif c == 7:
                                emit_rep(sc, k - 1, 1)
                    if not deep and k >= 1:
                        for h in range(2):
                            emit_ln(sc, k - 1, h)
                            emit_rep(sc, k - 1, h)
                for h in range(2):
                    emit_ln(sc, KTOT - 1, h)
                    emit_rep(sc, KTOT - 1, h)

            @blk.vector
            def _(v):
                def emit_tt(v, j):
                    pj, sj = divmod(j, NSLAB)
                    sm.wait(v, "sDN", sDN0 + 4 * j + 2)
                    i = v.tensor_tensor(
                        A1[0:64, pj * S + sj * SLAB:pj * S + (sj + 1) * SLAB],
                        PS[0:64, av_bank(0, j):av_bank(0, j) + SLAB],
                        WS[0:64, 0:SLAB], ALU.mult)
                    sm.inc(i, "sNM")
                    sm.wait(v, "sDN", sDN0 + 4 * j + 4)
                    if j >= 1:
                        sm.wait(v, "dG", dG0 + 16 * j)
                    i = v.tensor_tensor(
                        WS[0:64, 2048 - SLAB:2048],
                        PS[0:64, av_bank(1, j):av_bank(1, j) + SLAB],
                        WS[0:64, 0:SLAB], ALU.mult)
                    sm.inc(i, "sNM")

                for k in range(KTOT):
                    if k >= 1:
                        emit_tt(v, k - 1)
                    for half in range(2):
                        sm.wait(v, "sAV", sAV0 + NT * (k + 1))
                        if k >= 1:
                            sm.wait(v, "sBC", sBC0 + 2 * (k - 1) + half + 1)
                        i = v.tensor_copy(
                            WS[64:65, 1024 + half * 512:1024 + half * 512 + SLAB],
                            PS[64:65, av_bank(half, k):av_bank(half, k) + SLAB])
                        sm.inc(i, "sDR")
                emit_tt(v, KTOT - 1)

            @blk.gpsimd
            def _(g):
                def emit_shift(g, j):
                    pj, sj = divmod(j, NSLAB)
                    sm.wait(g, "sNM", sNM0 + 2 * j + 2)
                    sm.wait(g, "dG")
                    i = g.dma_start(
                        out=A1[64:128, pj * S + sj * SLAB:pj * S + (sj + 1) * SLAB],
                        in_=WS[0:64, 2048 - SLAB:2048])
                    sm.inc(i, "dG", 16)

                for k in range(1, KTOT):
                    emit_shift(g, k - 1)
                emit_shift(g, KTOT - 1)
                sm.wait(g, "dG")

            emit_taps(blk, [("attn", A1[:, 0:DC * S])])

        with nc.Block() as blk:
            sWQ0 = sm.n["sWQ"]
            sWT0 = sm.n["sWT"]
            sG0 = sm.n["sG"]
            sU0 = sm.n["sU"]
            sSIL0 = sm.n["sSIL"]
            sHP0 = sm.n["sHP"]
            sT0 = sm.n["sT1"]
            dGa0 = sm.n["dGa"]
            dGb0 = sm.n["dGb"]

            def zslot(it, wsel):
                return 1024 + (wsel * 2 + it % 2) * 1024

            def emit_chain(v, it, only_wsel=None):
                for wsel, ccol in ((0, c_cg), (1, c_cu)):
                    if only_wsel is not None and wsel != only_wsel:
                        continue
                    wsem = "dGa" if wsel == 0 else "dGb"
                    sm.wait(v, wsem, (dGa0 if wsel == 0 else dGb0) + (it + 1) * 16)
                    xs = X32[:, wsel * 1024:wsel * 1024 + 1024]
                    if it >= 2:
                        sm.wait(v, "sT1", sT0 + 2 * (2 * (it - 2) + wsel + 1))
                    if it >= 2:
                        sm.wait(v, "sSIL", sSIL0 + NFQ * (it - 1))
                        sm.wait(v, "sHP", sHP0 + NFQ * (it - 1))
                    dve_chain(v, [
                        lambda v: v.tensor_reduce(col(c_tmp + 2), xs, AX.X,
                                                  ALU.max, apply_absolute_value=True),
                        lambda v: v.reciprocal(col(c_tmp + 3), col(c_tmp + 2)),
                        lambda v: v.tensor_scalar(col(c_tmp + 4), col(c_tmp + 3),
                                                  127.0, None, ALU.mult),
                        lambda v: v.tensor_scalar(col(ccol + it % 2), col(c_tmp + 2),
                                                  col(c_sx), 1.0 / 127.0,
                                                  ALU.mult, ALU.mult),
                        lambda v: v.tensor_scalar(PS[:, 2048:3072], xs,
                                                  col(c_tmp + 4), MAGIC,
                                                  ALU.mult, ALU.add),
                        lambda v: v.tensor_scalar(
                            EX[:, zslot(it, wsel):zslot(it, wsel) + 1024],
                            PS[:, 2048:3072], MAGIC, None, ALU.subtract),
                    ], last_sem="sWQ")

            @blk.vector
            def _(v):
                i = v.memset(COLS[:, c_hmax:c_hmax + 2], 0.0)
                sm.inc(i, "cDVE")
                gi = 0
                emit_chain(v, 0)
                for it in range(IT):
                    if it + 1 < IT and NFQ == 1:
                        emit_chain(v, it + 1)
                    for fq in range(NFQ):
                        if it + 1 < IT and NFQ > 1 and fq in (0, 1):
                            emit_chain(v, it + 1, only_wsel=fq)
                        m = gi
                        gi += 1
                        t0 = fq * FQ
                        sm.wait(v, "sU", sU0 + m + 1)
                        sm.wait(v, "sSIL", sSIL0 + m + 1)
                        gbu = 1024 + (m % 2) * 512
                        exs = (m % 2) * 512
                        dve_chain(v, [
                            lambda v: v.scalar_tensor_tensor(
                                hp_ap(it, t0, FQ), PS[:, gbu:gbu + FQ],
                                col(c_cu + it % 2), EX[:, exs:exs + FQ],
                                ALU.mult, ALU.mult),
                            lambda v: v.tensor_reduce(col(c_tmp + 5), hp_ap(it, t0, FQ),
                                                      AX.X, ALU.max,
                                                      apply_absolute_value=True),
                            lambda v: v.tensor_tensor(col(c_hmax + (m + 1) % 2),
                                                      col(c_hmax + m % 2),
                                                      col(c_tmp + 5), ALU.max),
                        ], last_sem="sHP")

            @blk.gpsimd
            def _(g):
                for it in range(IT):
                    for wsel, w_dram in ((0, wg_h), (1, wu_h)):
                        wsem = "dGa" if wsel == 0 else "dGb"
                        sm.wait(g, "sWQ", sWQ0 + 2 * it + wsel)
                        sm.wait(g, wsem)
                        i = g.dma_start(out=X32[:, wsel * 1024:wsel * 1024 + 1024],
                                        in_=w_dram[it * 128:(it + 1) * 128, :])
                        sm.inc(i, wsem, 16)

            def emit_transp(t, it):
                for wsel in range(2):
                    sm.wait(t, "sWQ", sWQ0 + 2 * it + wsel + 1)
                    zb = zslot(it, wsel)
                    for b in range(2):
                        nb = 2 * (2 * it + wsel) + b
                        if nb >= 2:
                            sm.wait(t, "sWT", sWT0 + nb - 1)
                        pbank = 3072 + (nb % 2) * 512
                        for kk in range(4):
                            cd = b * 4 + kk
                            i = t.matmul(
                                PS[:, pbank + kk * 128:pbank + (kk + 1) * 128],
                                EX[:, zb + cd * 128:zb + (cd + 1) * 128], IDE,
                                start=True, stop=True)
                        sm.inc(i, "sT1")

            @blk.tensor
            def _(t):
                gi = 0
                emit_transp(t, 0)
                for it in range(IT):
                    for fq in range(NFQ):
                        m = gi
                        gi += 1
                        t0 = fq * FQ
                        sm.wait(t, "sWT", sWT0 + 4 * (it + 1))
                        wtg = (2 * (it % 2) + 0) * 1024
                        wtu = (2 * (it % 2) + 1) * 1024
                        gbg = (m % 2) * 512
                        gbu = 1024 + (m % 2) * 512
                        if m >= 2:
                            sm.wait(t, "sSIL", sSIL0 + m - 1)
                        for cd in range(DC):
                            i = t.matmul(PS[:, gbg:gbg + FQ],
                                         WT[:, wtg + cd * 128:wtg + (cd + 1) * 128],
                                         A4[:, cd * S + t0:cd * S + t0 + FQ],
                                         start=(cd == 0), stop=(cd == DC - 1))
                        sm.inc(i, "sG")
                        if m >= 2:
                            sm.wait(t, "sHP", sHP0 + m - 1)
                        for cd in range(DC):
                            i = t.matmul(PS[:, gbu:gbu + FQ],
                                         WT[:, wtu + cd * 128:wtu + (cd + 1) * 128],
                                         A4[:, cd * S + t0:cd * S + t0 + FQ],
                                         start=(cd == 0), stop=(cd == DC - 1))
                        sm.inc(i, "sU")
                    if it + 1 < IT:
                        emit_transp(t, it + 1)

            def emit_evacs(sc, it):
                for wsel in range(2):
                    if it >= 2:
                        sm.wait(sc, "sU", sU0 + NFQ * (it - 1))
                    for b in range(2):
                        nb = 2 * (2 * it + wsel) + b
                        sm.wait(sc, "sT1", sT0 + nb + 1)
                        pbank = 3072 + (nb % 2) * 512
                        wtb = (2 * (it % 2) + wsel) * 1024
                        i = sc.copy(WT[:, wtb + b * 512:wtb + (b + 1) * 512],
                                    PS[:, pbank:pbank + 512])
                        sm.inc(i, "sWT")

            @blk.scalar
            def _(sc):
                gi = 0
                emit_evacs(sc, 0)
                for it in range(IT):
                    for fq in range(NFQ):
                        m = gi
                        gi += 1
                        sm.wait(sc, "sG", sG0 + m + 1)
                        if m >= 2:
                            sm.wait(sc, "sHP", sHP0 + m - 1)
                        gbg = (m % 2) * 512
                        exs = (m % 2) * 512
                        i = sc.activation(EX[:, exs:exs + FQ], PS[:, gbg:gbg + FQ],
                                          AF.Silu, scale=col(c_cg + it % 2))
                        sm.inc(i, "sSIL")
                        if fq == NFQ - 1 and it + 1 < IT:
                            emit_evacs(sc, it + 1)

            NGRAN = IT * NFQ

            @blk.gpsimd
            def _(g):
                sm.wait(g, "sHP", sHP0 + NGRAN)
                i = g.dma_start(out=scr.ap()[0:1, :].rearrange("a n -> n a"),
                                in_=col(c_hmax + NGRAN % 2))
                sm.inc(i, "dG", 16)
                sm.wait(g, "dG")
                i = g.dma_start(out=ROW[0:1, :], in_=scr.ap()[0:1, :])
                sm.inc(i, "dG", 16)

            @blk.vector
            def _(v):
                sm.wait(v, "dG")
                i = v.tensor_reduce(scal(2), ROW[0:1, :], AX.X, ALU.max)
                sm.inc(i, "cDVE")

            @blk.gpsimd
            def _(g):
                sm.wait(g, "cDVE")
                i = g.dma_start(out=cc2_in.ap()[0:1, :], in_=scal(2))
                sm.inc(i, "dG", 16)
                sm.wait(g, "dG")
                i = g.collective_compute(
                    "AllReduce", ALU.max,
                    replica_groups=[list(range(n_cores))],
                    ins=[cc2_in.ap().opt()], outs=[cc2_out.ap().opt()])
                sm.inc(i, "cc2")

        p3_wq = sm.n["sWQ"]
        with nc.Block() as blk:
            load_weight_T(blk, wo_h)
            wt_ready_o = sm.n["sWT"]
            sG0 = sm.n["sG"]
            sHP0 = sm.n["sHP"]
            sHQ0 = sm.n["sROPE"]
            sSP0 = sm.n["sSP"]
            sWQ0 = sm.n["sWQ"]
            sT0 = sm.n["sT1"]
            dWD0 = sm.n["dWD"]
            cell = {}

            batches = []
            bstart = 0
            while bstart < IT:
                batches.append((bstart, min(4, IT - bstart)))
                bstart += 4
            NB = len(batches)

            def wdq_dst(ci, th):
                if ci < 16:
                    return A4[:, ci * 1024 + th * 128:ci * 1024 + (th + 1) * 128]
                if ci < 20:
                    return A5[:, 6 * S + (ci - 16) * 1024 + th * 128:
                              6 * S + (ci - 16) * 1024 + (th + 1) * 128]
                return WS[:, (ci - 20) * 1024 + th * 128:
                          (ci - 20) * 1024 + (th + 1) * 128]

            @blk.gpsimd
            def _(g):
                sm.wait(g, "sWQ", p3_wq)
                sm.wait(g, "dWD")
                i = g.dma_start(out=X32[:, 0:II], in_=wd_h[0:128, :])
                sm.inc(i, "dWD", 16)

            @blk.gpsimd
            def _(g):
                sm.wait(g, "cc2")
                sm.wait(g, "dG")
                i = g.dma_start(out=scal(3), in_=cc2_out.ap()[0:1, :])
                sm.inc(i, "dG", 16)

            if "gmax2" in tap_h:
                @blk.gpsimd
                def _(g):
                    sm.wait(g, "dG")
                    i = g.dma_start(out=tap_h["gmax2"][0:1, :], in_=scal(3))
                    sm.inc(i, "dG", 16)

            @blk.gpsimd
            def _(g):
                for th in range(1, 8):
                    sm.wait(g, "sWQ", sWQ0 + th)
                    sm.wait(g, "dWD")
                    i = g.dma_start(out=X32[:, 0:II],
                                    in_=wd_h[th * 128:(th + 1) * 128, :])
                    sm.inc(i, "dWD", 16)

            @blk.scalar
            def _(sc):
                i = sc.activation(scal(4), scal(1), AF.Sigmoid)
                sm.inc(i, "cACT")

            act_sig = sm.n["cACT"]
            dg_g2 = sm.n["dG"]

            def emit_wd_transp(t, th):
                sm.wait(t, "sWQ", sWQ0 + th + 1)
                for bi, (c_lo, bn) in enumerate(batches):
                    nb = th * NB + bi
                    if nb >= 2:
                        sm.wait(t, "sWT", wt_ready_o + nb - 1)
                    pbank = 3072 + (nb % 2) * 512
                    for kk in range(bn):
                        ci = c_lo + kk
                        i = t.matmul(PS[:, pbank + kk * 128:pbank + (kk + 1) * 128],
                                     EX[:, 1024 + ci * 128:1024 + (ci + 1) * 128],
                                     IDE, start=True, stop=True)
                    sm.inc(i, "sT1")

            def emit_bc2(t):
                sm.wait(t, "sSP", sSP0 + 1)
                sm.wait(t, "sWQ", sWQ0 + 1)
                i = t.matmul(PS[:, 2048:2049], ONES[0:1, 0:128], SC16[0:1, 0:1],
                             start=True, stop=True)
                i = t.matmul(PS[:, 2049:2050], ONES[0:1, 0:128], SC16[0:1, 1:2],
                             start=True, stop=True)
                sm.inc(i, "cPE")
                cell["pe_p5"] = sm.n["cPE"]

            @blk.tensor
            def _(t):
                sm.wait(t, "sWT", wt_ready_o)
                nth = 0
                bc_done = False
                for tt in range(NT):
                    gb = (tt % 2) * 1024
                    if tt >= 2:
                        sm.wait(t, "sHP", sHP0 + tt - 1)
                    for cd in range(DC):
                        for c0, w in cchunks(D):
                            i = t.matmul(
                                PS[:, gb + c0:gb + c0 + w],
                                A1[:, cd * S + tt * 128:cd * S + (tt + 1) * 128],
                                WT[:, cd * 1024 + c0:cd * 1024 + c0 + w],
                                start=(cd == 0), stop=(cd == DC - 1))
                    sm.inc(i, "sG")
                    if tt == 1 and nth == 0:
                        emit_wd_transp(t, 0)
                        nth = 1
                    elif tt == 2 and not bc_done:
                        emit_bc2(t)
                        bc_done = True
                    elif tt >= 5 and tt % 2 == 1 and nth < 8 and nth == (tt - 3) // 2:
                        emit_wd_transp(t, nth)
                        nth += 1
                if not bc_done:
                    emit_bc2(t)
                while nth < 8:
                    emit_wd_transp(t, nth)
                    nth += 1

            FG = min(1024, S)
            NFG = IT * (S // FG)

            def emit_wd_chain(v, th):
                sm.wait(v, "dWD", dWD0 + 16 * (th + 1))
                if th >= 1:
                    sm.wait(v, "sT1", sT0 + th * NB)
                ops = [
                    lambda v: v.tensor_reduce(col(c_rmax8 + th), X32[:, 0:II],
                                              AX.X, ALU.max,
                                              apply_absolute_value=True),
                    lambda v: v.reciprocal(col(c_tmp + 3), col(c_rmax8 + th)),
                    lambda v: v.tensor_scalar(col(c_tmp + 4), col(c_tmp + 3),
                                              127.0, None, ALU.mult),
                ]
                for q in range(3):
                    w = min(1024, II - q * 1024)
                    ops.append(lambda v, q=q, w=w: v.tensor_scalar(
                        PS[:, 2048:2048 + w], X32[:, q * 1024:q * 1024 + w],
                        col(c_tmp + 4), MAGIC, ALU.mult, ALU.add))
                    ops.append(lambda v, q=q, w=w: v.tensor_scalar(
                        EX[:, 1024 + q * 1024:1024 + q * 1024 + w],
                        PS[:, 2048:2048 + w], MAGIC, None, ALU.subtract))
                dve_chain(v, ops, last_sem="sWQ")

            def emit_fq(v, fi2):
                it2, g2 = divmod(fi2, S // FG)
                t0 = g2 * FG
                sm.wait(v, "sWQ")
                dve_chain(v, [
                    lambda v: v.tensor_scalar(PS[:, 2048:2048 + FG],
                                              hp_ap(it2, t0, FG),
                                              col(c_invs2), MAGIC,
                                              ALU.mult, ALU.add),
                    lambda v: v.tensor_scalar(EX[:, 0:FG], PS[:, 2048:2048 + FG],
                                              MAGIC, 127.0,
                                              ALU.subtract, ALU.min),
                    lambda v: v.tensor_scalar(hp_ap(it2, t0, FG), EX[:, 0:FG],
                                              -128.0, None, ALU.max),
                ], last_sem="sROPE")

            def emit_scale_prep(v):
                sm.wait(v, "cACT", act_sig)
                sm.wait(v, "dG", dg_g2)
                dve_chain(v, [
                    lambda v: v.tensor_tensor(scal(5), scal(4), scal(3), ALU.mult),
                    lambda v: v.reciprocal(scal(6), scal(5)),
                    lambda v: v.tensor_scalar(SC16[0:1, 0:1], scal(6), 127.0, None,
                                              ALU.mult),
                    lambda v: v.tensor_scalar(SC16[0:1, 1:2], scal(5), 1.0 / 127.0,
                                              None, ALU.mult),
                ], last_sem="sSP")
                sm.wait(v, "cPE", cell["pe_p5"])
                dve_chain(v, [
                    lambda v: v.tensor_copy(col(c_invs2), PS[:, 2048:2049]),
                    lambda v: v.tensor_copy(col(c_s2), PS[:, 2049:2050]),
                ])

            ao_ds0 = [sm.n["dSa"], sm.n["dSb"]]
            sAOE0 = sm.n["sAOE"]

            @blk.sync
            def _(sy):
                for tt in range(NT):
                    for half in range(2):
                        dsem = "dSa" if half == 0 else "dSb"
                        if half == 0:
                            sm.wait(sy, "sAOE", sAOE0 + tt + 1)
                        else:
                            sm.wait(sy, "sHP", sHP0 + tt + 1)
                        sm.wait(sy, dsem)
                        i = sy.dma_start(
                            out=ao_d.ap()[tt * 128:(tt + 1) * 128,
                                          half * 512:(half + 1) * 512],
                            in_=EX[:, 3840 + half * 512:3840 + (half + 1) * 512])
                        sm.inc(i, dsem, 16)

            @blk.vector
            def _(v):
                fi = 0
                emit_wd_chain(v, 0)
                emit_scale_prep(v)
                for vth in range(1, 8):
                    emit_wd_chain(v, vth)
                    while fi < vth * NFG // 8:
                        emit_fq(v, fi)
                        fi += 1
                while fi < NFG:
                    emit_fq(v, fi)
                    fi += 1

            @blk.scalar
            def _(sc):
                sm.wait(sc, "sWT", wt_ready_o)
                aot = 0

                def emit_ao(sc, tt):
                    gb = (tt % 2) * 1024
                    sm.wait(sc, "sG", sG0 + tt + 1)
                    for half in range(2):
                        dsem = "dSa" if half == 0 else "dSb"
                        if tt >= 1:
                            sm.wait(sc, dsem, ao_ds0[half] + tt * 16)
                        i = sc.copy(
                            EX[:, 3840 + half * 512:3840 + (half + 1) * 512],
                            PS[:, gb + half * 512:gb + half * 512 + 512])
                        sm.inc(i, "sAOE" if half == 0 else "sHP")

                for th in range(8):
                    while aot < min(NT, 2 * th + 2):
                        emit_ao(sc, aot)
                        aot += 1
                    for bi, (c_lo, bn) in enumerate(batches):
                        nb = th * NB + bi
                        sm.wait(sc, "sT1", sT0 + nb + 1)
                        pbank = 3072 + (nb % 2) * 512
                        for kk in range(bn):
                            ci = c_lo + kk
                            i = sc.copy(wdq_dst(ci, th),
                                        PS[:, pbank + kk * 128:pbank + (kk + 1) * 128])
                        sm.inc(i, "sWT")
                while aot < NT:
                    emit_ao(sc, aot)
                    aot += 1

        wd_done_wt = sm.n["sWT"]
        wd_done_wq = sm.n["sWQ"]

        if "ao" in tap_h:
            with nc.Block() as blk:
                @blk.gpsimd
                def _(g):
                    sm.wait(g, "dSa")
                    sm.wait(g, "dSb")
                    for tt in range(NT):
                        sm.wait(g, "dG")
                        i = g.dma_start(out=tap_h["ao"][:, tt * D:(tt + 1) * D],
                                        in_=ao_d.ap()[tt * 128:(tt + 1) * 128, :])
                        sm.inc(i, "dG", 16)

        with nc.Block() as blk:
            @blk.vector
            def _(v):
                dve_chain(v, [
                    lambda v: v.tensor_scalar(COLS[:, c_rsd:c_rsd + 8],
                                              COLS[:, c_rmax8:c_rmax8 + 8],
                                              col(c_s2), 1.0 / 127.0,
                                              ALU.mult, ALU.mult),
                ])

            rsd_dve = sm.n["cDVE"]

            @blk.gpsimd
            def _(g):
                sm.wait(g, "cDVE", rsd_dve)
                for th in range(8):
                    sm.wait(g, "dG")
                    i = g.dma_start(out=scr.ap()[th:th + 1, :].rearrange("a n -> n a"),
                                    in_=col(c_rsd + th))
                    sm.inc(i, "dG", 16)
                sm.wait(g, "dG")
                i = g.dma_start(out=BVR[0:1, :],
                                in_=scr.ap().rearrange("a n -> (a n)").rearrange(
                                    "(x y) -> x y", x=1))
                sm.inc(i, "dG", 16)

            dg_rsf = sm.n["dG"]

            @blk.tensor
            def _(t):
                sm.wait(t, "dG", dg_rsf)
                for c0, w in cchunks(D):
                    i = t.matmul(PS[:, 2048 + c0:2048 + c0 + w], ONES[0:1, 0:128],
                                 BVR[0:1, c0:c0 + w], start=True, stop=True)
                sm.inc(i, "cPE")

            pe_rsf = sm.n["cPE"]

            @blk.vector
            def _(v):
                sm.wait(v, "cPE", pe_rsf)
                i = v.tensor_copy(EX[:, 0:1024], PS[:, 2048:3072])
                sm.inc(i, "cDVE")

            rsf_dve = sm.n["cDVE"]

            sG0 = sm.n["sG"]
            sNM0 = sm.n["sNM"]

            def wdq_rhs(ci):
                if ci < 16:
                    return A4[:, ci * 1024:(ci + 1) * 1024]
                if ci < 20:
                    return A5[:, 6 * S + (ci - 16) * 1024:6 * S + (ci - 15) * 1024]
                return WS[:, (ci - 20) * 1024:(ci - 19) * 1024]

            @blk.tensor
            def _(t):
                sm.wait(t, "cDVE", rsf_dve)
                sm.wait(t, "sWT", wd_done_wt)
                sm.wait(t, "sROPE", sHQ0 + IT * max(1, S // 1024))
                for tt in range(NT):
                    gb = (tt % 2) * 1024
                    if tt >= 2:
                        sm.wait(t, "sNM", sNM0 + tt - 1)
                    for ci in range(IT):
                        rhs = wdq_rhs(ci)
                        for c0, w in cchunks(D):
                            i = t.matmul(
                                PS[:, gb + c0:gb + c0 + w],
                                hp_ap(ci, tt * 128, 128),
                                rhs[:, c0:c0 + w],
                                start=(ci == 0), stop=(ci == IT - 1))
                    sm.inc(i, "sG")

            dsa0 = sm.n["dSa"]
            dsb0 = sm.n["dSb"]
            dga0 = sm.n["dGa"]
            dgb0 = sm.n["dGb"]

            @blk.gpsimd
            def _(g):
                sm.wait(g, "dSa")
                sm.wait(g, "dSb")
                for tt in range(NT):
                    gsem = "dGa" if tt % 2 == 0 else "dGb"
                    sm.wait(g, gsem)
                    i = g.dma_start(out=A1[:, tt * 1024:(tt + 1) * 1024],
                                    in_=ao_d.ap()[tt * 128:(tt + 1) * 128, :])
                    sm.inc(i, gsem, 16)

            @blk.vector
            def _(v):
                sm.wait(v, "cDVE", rsf_dve)
                for tt in range(NT):
                    dsem = "dSa" if tt % 2 == 0 else "dSb"
                    dbase = (dsa0 if tt % 2 == 0 else dsb0) + (tt // 2 + 1) * 16
                    gb = (tt % 2) * 1024
                    sm.wait(v, "sG", sG0 + tt + 1)
                    if tt >= 1:
                        sm.wait(v, "sNM", sNM0 + tt)
                    sm.wait(v, dsem, dbase)
                    gsem = "dGa" if tt % 2 == 0 else "dGb"
                    gbase = (dga0 if tt % 2 == 0 else dgb0) + (tt // 2 + 1) * 16
                    sm.wait(v, gsem, gbase)
                    xs = X32[:, (tt % 2) * 1024:(tt % 2) * 1024 + 1024]
                    dve_chain(v, [
                        lambda v: v.tensor_tensor(PS[:, 2048:3072],
                                                  PS[:, gb:gb + 1024],
                                                  EX[:, 0:1024], ALU.mult),
                        lambda v: v.tensor_tensor(PS[:, 2048:3072],
                                                  PS[:, 2048:3072],
                                                  A1[:, tt * 1024:(tt + 1) * 1024],
                                                  ALU.add),
                        lambda v: v.tensor_tensor(xs, PS[:, 2048:3072], xs, ALU.add),
                    ], last_sem="sNM")

            @blk.sync
            def _(sy):
                for tt in range(min(NT, 2)):
                    dsem = "dSa" if tt % 2 == 0 else "dSb"
                    sm.wait(sy, dsem)
                    i = sy.dma_start(out=X32[:, (tt % 2) * 1024:(tt % 2) * 1024 + 1024],
                                     in_=x_h[tt * 128:(tt + 1) * 128, :])
                    sm.inc(i, dsem, 16)
                for tt in range(NT):
                    osem = "dOa" if tt % 2 == 0 else "dOb"
                    sm.wait(sy, "sNM", sNM0 + tt + 1)
                    sm.wait(sy, osem)
                    i = sy.dma_start(out=out_h[tt * 128:(tt + 1) * 128, :],
                                     in_=X32[:, (tt % 2) * 1024:(tt % 2) * 1024 + 1024])
                    sm.inc(i, osem, 16)
                    if tt + 2 < NT:
                        dsem = "dSa" if tt % 2 == 0 else "dSb"
                        sm.wait(sy, osem)
                        sm.wait(sy, dsem)
                        i = sy.dma_start(out=X32[:, (tt % 2) * 1024:(tt % 2) * 1024 + 1024],
                                         in_=x_h[(tt + 2) * 128:(tt + 3) * 128, :])
                        sm.inc(i, dsem, 16)

            if "hp" in tap_h:
                @blk.gpsimd
                def _(g):
                    sm.wait(g, "sROPE", sHQ0 + IT * max(1, S // 2048))
                    for it2 in range(IT):
                        sm.wait(g, "dG")
                        i = g.dma_start(out=tap_h["hp"][:, it2 * S:(it2 + 1) * S],
                                        in_=hp_ap(it2, 0, S))
                        sm.inc(i, "dG", 16)

            @blk.gpsimd
            def _(g):
                sm.wait(g, "dOa")
                sm.wait(g, "dOb")
                sm.wait(g, "dG")

    return nc, tap_h


def kernel(L0_attn_x, L0_attn_Wq, L0_attn_bq, L0_attn_Wk, L0_attn_bk,
           L0_attn_Wv, L0_attn_bv, L0_attn_Wo, L0_ffn_clip_x, L0_ffn_clip_down,
           L0_ffn_w_gate, L0_ffn_w_up, L0_ffn_w_down):
    from concourse.bass_utils import run_bass_kernel_spmd
    x = np.ascontiguousarray(np.asarray(L0_attn_x, dtype=np.float32))
    B, S_, D_ = x.shape
    assert D_ == D and B == 8
    shared = dict(
        Wq=np.ascontiguousarray(L0_attn_Wq, dtype=np.float32),
        bq=np.ascontiguousarray(L0_attn_bq, dtype=np.float32),
        Wk=np.ascontiguousarray(L0_attn_Wk, dtype=np.float32),
        bk=np.ascontiguousarray(L0_attn_bk, dtype=np.float32),
        Wv=np.ascontiguousarray(L0_attn_Wv, dtype=np.float32),
        bv=np.ascontiguousarray(L0_attn_bv, dtype=np.float32),
        Wo=np.ascontiguousarray(L0_attn_Wo, dtype=np.float32),
        clip_x=np.ascontiguousarray(L0_ffn_clip_x, dtype=np.float32),
        clip_down=np.ascontiguousarray(L0_ffn_clip_down, dtype=np.float32),
        w_gate=np.ascontiguousarray(L0_ffn_w_gate, dtype=np.float32),
        w_up=np.ascontiguousarray(L0_ffn_w_up, dtype=np.float32),
        w_down=np.ascontiguousarray(L0_ffn_w_down, dtype=np.float32),
    )
    nc, _ = build(S=S_, n_cores=B)
    in_maps = [dict(shared, x=np.ascontiguousarray(x[b])) for b in range(B)]
    res = run_bass_kernel_spmd(nc, in_maps, core_ids=list(range(B)))
    out = np.stack([np.asarray(res.results[b]["out"], dtype=np.float32)
                    for b in range(B)], axis=0)
    return out
